# version 13
# baseline (speedup 1.0000x reference)
import numpy as np
import ml_dtypes
from contextlib import ExitStack

import concourse.bass as bass
import concourse.mybir as mybir
from concourse.bass_utils import run_bass_kernel_spmd

F32 = mybir.dt.float32
BF16 = mybir.dt.bfloat16
AF = mybir.ActivationFunctionType
ALU = mybir.AluOpType
BF = ml_dtypes.bfloat16

D = 1024
T = 2048
H = 16
HD = 64
NJ = 8
TBS = 512
NTB = T // TBS
CH = 64
NCH = TBS // CH
DECAY = -0.6065306597126334
LN_EPS = 1e-5
GN_EPS = HD * 1e-05
DH = 4096
TB2 = 1024

COLV = {"mix_r": 0, "mix_w": 1, "mix_k": 2, "mix_v": 3, "mix_a": 4, "mix_g": 5,
        "k_k": 6, "k_a": 7, "gn_w": 8, "gn_b": 9, "w_bias": 10, "a_bias": 11}
NCOLV = 12



def host_prepare(inp):
    f = np.float32
    shared = {}
    shared["Wr"] = np.ascontiguousarray(inp["L0_attn_Wr"], f)
    shared["Wk"] = np.ascontiguousarray(inp["L0_attn_Wk"], f)
    shared["Wv"] = np.ascontiguousarray(inp["L0_attn_Wv"], f)
    shared["Wo"] = np.ascontiguousarray(inp["L0_attn_Wo"], f)
    shared["wA"] = np.ascontiguousarray(inp["L0_attn_wA"], f)
    shared["wB"] = np.ascontiguousarray(inp["L0_attn_wB"], f)
    shared["aA"] = np.ascontiguousarray(inp["L0_attn_aA"], f)
    shared["aB"] = np.ascontiguousarray(inp["L0_attn_aB"], f)
    gA = np.asarray(inp["L0_attn_gA"], f)
    shared["gA"] = np.ascontiguousarray(np.hstack([gA, np.zeros((D, 32), f)]))
    gB = np.asarray(inp["L0_attn_gB"], f)
    shared["gB"] = np.ascontiguousarray(np.vstack([gB, np.zeros((32, D), f)]))
    cp = np.zeros((128, 8 * NCOLV), f)

    def putcol(name, vec):
        v = np.asarray(vec, f).reshape(NJ, 128)
        cp[:, 8 * COLV[name]: 8 * COLV[name] + 8] = v.T
    putcol("mix_r", inp["L0_attn_mix_r"]); putcol("mix_w", inp["L0_attn_mix_w"])
    putcol("mix_k", inp["L0_attn_mix_k"]); putcol("mix_v", inp["L0_attn_mix_v"])
    putcol("mix_a", inp["L0_attn_mix_a"]); putcol("mix_g", inp["L0_attn_mix_g"])
    putcol("k_k", inp["L0_attn_k_k"]); putcol("k_a", inp["L0_attn_k_a"])
    putcol("gn_w", inp["L0_attn_gn_weight"]); putcol("gn_b", inp["L0_attn_gn_bias"])
    putcol("w_bias", inp["L0_attn_w_bias"]); putcol("a_bias", inp["L0_attn_a_bias"])
    shared["colpack"] = cp
    for nm, key in [("kk_bc", "L0_attn_k_k"), ("ka_bc", "L0_attn_k_a"),
                    ("wb_bc", "L0_attn_w_bias"), ("ab_bc", "L0_attn_a_bias")]:
        shared[nm] = np.ascontiguousarray(
            np.broadcast_to(np.asarray(inp[key], f).reshape(1, D), (128, D)))
    rk = np.asarray(inp["L0_attn_r_k"], f)
    rkstat = np.zeros((NJ, 128, H), f)
    for j in range(NJ):
        rkstat[j, 0:64, 2 * j] = rk[2 * j]
        rkstat[j, 64:128, 2 * j + 1] = rk[2 * j + 1]
    shared["rkstat"] = rkstat
    b = BF
    shared["ones_col"] = np.ones((128, 1), b)
    shared["onesb_row"] = np.ones((1, 128), b)
    i64 = np.eye(64, dtype=np.float32)
    z64 = np.zeros((64, 64), np.float32)
    bd = lambda a, c: np.block([[a, z64], [z64, c]])
    tril_s = np.tril(np.ones((64, 64), np.float32), -1)
    triu_s = np.triu(np.ones((64, 64), np.float32), 1)
    triu_i = np.triu(np.ones((64, 64), np.float32), 0)
    shared["msl"] = bd(tril_s, tril_s).astype(b)
    shared["msu"] = bd(triu_s, triu_s).astype(b)
    shared["mtu"] = bd(triu_i, triu_i).astype(b)
    shared["I_bd"] = bd(i64, i64).astype(b)
    shared["I64x2"] = np.block([[i64, i64], [i64, i64]]).astype(b)
    shared["cumUT"] = bd(triu_i, triu_i).astype(b)
    shared["mcomb"] = np.hstack([bd(tril_s, tril_s), bd(triu_s, triu_s),
                                 bd(triu_i, triu_i), bd(triu_i, triu_i)]).astype(b)
    hiT = np.zeros((NJ, 128, H), np.float32)
    bci = np.zeros((NJ, H, 128), np.float32)
    for j in range(NJ):
        hiT[j, 0:64, 2 * j] = 1.0
        hiT[j, 64:128, 2 * j + 1] = 1.0
        bci[j, 2 * j, 0:64] = 1.0
        bci[j, 2 * j + 1, 64:128] = 1.0
    shared["hiT"] = hiT.astype(b)
    shared["bcind"] = bci.astype(b)
    indd = np.zeros((H, D), np.float32)
    for h in range(H):
        indd[h, 64 * h: 64 * h + 64] = 1.0
    shared["ind_d"] = indd.astype(b)
    cm = np.ones((128, TBS), np.float32)
    cm[:, 0::CH] = 0.0
    shared["carrym"] = cm.astype(b)

    def tileT(w, nm_, nk_):
        wT = np.asarray(w, f).T
        t4 = wT.reshape(nk_, 128, nm_, 128).transpose(2, 0, 1, 3)
        return np.ascontiguousarray(t4)
    shared["w1T"] = tileT(inp["L0_ffn_w1_w"], 32, 8)
    shared["w2T"] = tileT(inp["L0_ffn_w2_w"], 32, 8)
    shared["h1T"] = tileT(inp["L0_ffn_h1_w"], 32, 32)
    shared["h2T"] = tileT(inp["L0_ffn_h2_w"], 32, 32)
    shared["w3T"] = tileT(inp["L0_ffn_w3_w"], 8, 32)
    fc = np.zeros((128, 136), f)
    fc[:, 0:32] = np.asarray(inp["L0_ffn_w1_s"], f).reshape(32, 128).T
    fc[:, 32:64] = np.asarray(inp["L0_ffn_w2_s"], f).reshape(32, 128).T
    fc[:, 64:96] = np.asarray(inp["L0_ffn_h1_s"], f).reshape(32, 128).T
    fc[:, 96:128] = np.asarray(inp["L0_ffn_h2_s"], f).reshape(32, 128).T
    fc[:, 128:136] = np.asarray(inp["L0_ffn_w3_s"], f).reshape(8, 128).T
    shared["ffncol"] = fc
    x = np.asarray(inp["L0_attn_hidden_states"], f)
    cores = [{"x_fm": np.ascontiguousarray(x[c].T)} for c in range(8)]
    return shared, cores



ENGS = ["tensor", "vector", "scalar", "gpsimd", "sync"]
ENG_SEM = {"tensor": "t", "vector": "v", "scalar": "s", "gpsimd": "g"}


class G:
    def __init__(self, nc, block, sems, pool):
        self.nc = nc
        self.block = block
        self.sems = sems
        self.cnt = {k: 0 for k in sems}
        self._pool = pool
        self._pool_i = 0

    def dsem(self, name):
        if name not in self.sems:
            assert self._pool_i < len(self._pool), "semaphore pool exhausted"
            self.sems[name] = self._pool[self._pool_i]
            self._pool_i += 1
            self.cnt[name] = 0
        return name

    def emit(self, eng, fn, waits=(), inc=None, n=1):
        wl = [(self.sems[s], v) for (s, v) in waits if v > 0]
        g = self

        def body(e):
            seen = set()
            for sh, v in wl:
                key = (id(sh), v)
                if key not in seen:
                    e.wait_ge(sh, v)
                    seen.add(key)
            ins = fn(e)
            if inc is not None:
                assert ins is not None
                ins.then_inc(g.sems[inc], n)
        getattr(self.block, eng)(body)
        if inc is not None:
            self.cnt[inc] += n

    def tok(self, s):
        return (s, self.cnt[s])

    def barrier(self):
        targets = [(s, self.cnt[s]) for s in self.cnt]
        for eng in ENGS:
            self.emit(eng, lambda e: None, waits=targets)


def mkprog(spills_external=(), phases=("P", "S", "O", "F"), upto=99):
    nc = bass.Bass("TRN2", target_bir_lowering=False, debug=False)
    es = ExitStack()
    dr = {}

    def dram(name, shape, dtype, kind):
        dr[name] = nc.dram_tensor(name, list(shape), dtype, kind=kind).ap()

    dram("x_fm", (D, T), F32, "ExternalInput")
    for nm in ["Wr", "Wk", "Wv", "Wo"]:
        dram(nm, (D, D), F32, "ExternalInput")
    dram("wA", (D, 64), F32, "ExternalInput")
    dram("wB", (64, D), F32, "ExternalInput")
    dram("aA", (D, 64), F32, "ExternalInput")
    dram("aB", (64, D), F32, "ExternalInput")
    dram("gA", (D, 192), F32, "ExternalInput")
    dram("gB", (192, D), F32, "ExternalInput")
    dram("colpack", (128, 8 * NCOLV), F32, "ExternalInput")
    for nm in ["kk_bc", "ka_bc", "wb_bc", "ab_bc"]:
        dram(nm, (128, D), F32, "ExternalInput")
    dram("rkstat", (NJ, 128, H), F32, "ExternalInput")
    dram("ones_col", (128, 1), BF16, "ExternalInput")
    dram("onesb_row", (1, 128), BF16, "ExternalInput")
    for nm in ["msl", "msu", "mtu", "I_bd", "I64x2", "cumUT"]:
        dram(nm, (128, 128), BF16, "ExternalInput")
    dram("mcomb", (128, 512), BF16, "ExternalInput")
    dram("hiT", (NJ, 128, H), BF16, "ExternalInput")
    dram("bcind", (NJ, H, 128), BF16, "ExternalInput")
    dram("ind_d", (H, D), BF16, "ExternalInput")
    dram("carrym", (128, TBS), BF16, "ExternalInput")
    dram("w1T", (32, 8, 128, 128), F32, "ExternalInput")
    dram("w2T", (32, 8, 128, 128), F32, "ExternalInput")
    dram("h1T", (32, 32, 128, 128), F32, "ExternalInput")
    dram("h2T", (32, 32, 128, 128), F32, "ExternalInput")
    dram("w3T", (8, 32, 128, 128), F32, "ExternalInput")
    dram("ffncol", (128, 136), F32, "ExternalInput")

    def spk(nm):
        return "ExternalOutput" if nm in spills_external else "Internal"
    for nm in ["r_sp", "k_sp", "v_sp", "w_sp", "a_sp", "g_sp", "o_sp"]:
        dram(nm, (D, T), BF16, spk(nm))
    for nm in ["kT_sp", "vT_sp", "aT_sp", "wT_sp"]:
        dram(nm, (T, D), BF16, spk(nm))
    dram("scorr_sp", (H, T), F32, spk("scorr_sp"))
    dram("w1b", (32, 8, 128, 128), BF16, "Internal")
    dram("w2b", (32, 8, 128, 128), BF16, "Internal")
    dram("h1b", (32, 32, 128, 128), BF16, "Internal")
    dram("h2b", (32, 32, 128, 128), BF16, "Internal")
    dram("w3b", (8, 32, 128, 128), BF16, "Internal")
    dram("x2_sp", (D, T), F32, spk("x2_sp"))
    dram("out_fm", (D, T), F32, "ExternalOutput")

    sems = {}
    for s in ["t", "v", "s", "g"]:
        sems[s] = es.enter_context(nc.semaphore("sem_" + s))
    pool = [es.enter_context(nc.semaphore(f"dpool{i}")) for i in range(48)]
    block = es.enter_context(nc.Block())
    g = G(nc, block, sems, pool)
    g.dr = dr
    g.es = es
    g.pb = [es.enter_context(nc.psum_tensor(f"pb{i}", [128, 512], F32)) for i in range(8)]

    if "P" in phases:
        with ExitStack() as ph_es:
            emit_phaseP(g, ph_es, upto=upto)
        g.barrier()
    if "S" in phases:
        with ExitStack() as ph_es:
            emit_phaseS(g, ph_es, upto=upto)
        g.barrier()
    if "O" in phases:
        with ExitStack() as ph_es:
            emit_phaseO(g, ph_es, upto=upto)
        g.barrier()
    if "F" in phases:
        with ExitStack() as ph_es:
            emit_phaseF(g, ph_es, upto=upto)
        g.barrier()

    final = [(s, g.cnt[s]) for s in g.cnt]
    g.emit("sync", lambda e: None, waits=final)
    es.close()
    return nc



def emit_phaseP(g, es, upto=99):
    nc, dr, pb = g.nc, g.dr, g.pb
    sb = lambda name, shape, dt=BF16: es.enter_context(nc.sbuf_tensor(name, list(shape), dt))

    Wsb = {nm: sb("p_" + nm, (128, NJ * D)) for nm in ["Wr", "Wk", "Wv"]}
    wAsb = sb("p_wA", (128, NJ * 64))
    aAsb = sb("p_aA", (128, NJ * 64))
    gAsb = sb("p_gA", (128, NJ * 192))
    wBsb = sb("p_wB", (64, D))
    aBsb = sb("p_aB", (64, D))
    gBsb = sb("p_gB", (128, 2 * D))
    colpk = sb("p_col", (128, 8 * NCOLV), F32)
    onesc = sb("p_ones", (128, 1))
    onesb = sb("p_onesb", (1, 128))
    wstg = sb("p_wstg", (128, 2 * D), F32)
    xf32 = sb("p_xf32", (128, NJ * TBS), F32)
    xln = sb("p_xln", (128, NJ * T))
    delta = sb("p_delta", (128, NJ * T))
    xMr = [sb(f"p_xM{i}", (128, NJ * TBS)) for i in range(2)]
    sqring = [sb(f"p_sq{i}", (128, TBS)) for i in range(2)]
    rowbf = sb("p_rowbf", (1, TBS))
    rowf = sb("p_rowf", (1, TBS), F32)
    stageT = [sb(f"p_stT{i}", (128, D)) for i in range(2)]
    srng = [sb(f"p_srng{i}", (128, TBS)) for i in range(4)]
    tz = sb("p_tz", (64, T))
    tza = sb("p_tza", (64, T))
    tzg = sb("p_tzg", (128, T))
    tzg2 = sb("p_tzg2", (64, T))
    wbbc = sb("p_wbbc", (128, D), F32)
    abbc = sb("p_abbc", (128, D), F32)
    cst = sb("p_cst", (128, 8), F32)

    colAP = lambda name, j: colpk[:, 8 * COLV[name] + j: 8 * COLV[name] + j + 1]

    g.emit("gpsimd", lambda e: e.memset(cst[:, 0:1], LN_EPS), inc="g")
    g.emit("gpsimd", lambda e: e.memset(cst[:, 1:2], GN_EPS), inc="g")
    cst_tok = g.tok("g")
    g.cst = cst
    g.cst_tok = cst_tok
    g.colpk = colpk
    g.colAP = colAP
    g.onesc = onesc
    g.onesb = onesb

    dc = g.dsem("dconst")
    for (dst, srcn) in [(colpk, "colpack"), (onesc, "ones_col"), (onesb, "onesb_row"),
                        (wbbc, "wb_bc"), (abbc, "ab_bc")]:
        g.emit("sync", lambda e, dst=dst, srcn=srcn: e.dma_start(out=dst[:], in_=dr[srcn][:, :]),
               inc=dc, n=16)
    tok_const = g.tok(dc)

    dwl = g.dsem("dwl")
    cast_done = [("t", 0)]

    def stage_load_cast(src3, dst_ap, nparts, cols, n3=None):
        nonlocal cast_done
        g.emit("sync", lambda e: e.dma_start(
            out=wstg[0:nparts, 0:cols].rearrange("p (c n) -> p c n", n=n3) if n3 else wstg[0:nparts, 0:cols],
            in_=src3),
            waits=cast_done, inc=dwl, n=16)
        qtok = g.tok(dwl)
        toks = []
        npieces = (cols + D - 1) // D
        for p in range(npieces):
            c0, c1 = p * D, min(cols, (p + 1) * D)
            eng = ["gpsimd", "vector", "scalar"][p % 3]
            if eng == "scalar":
                g.emit(eng, lambda e, c0=c0, c1=c1: e.copy(
                    out=dst_ap[:, c0:c1], in_=wstg[0:nparts, c0:c1]),
                    waits=[qtok], inc="s")
            else:
                g.emit(eng, lambda e, c0=c0, c1=c1: e.tensor_copy(
                    out=dst_ap[:, c0:c1], in_=wstg[0:nparts, c0:c1]),
                    waits=[qtok], inc=ENG_SEM[eng])
            toks.append(g.tok(ENG_SEM[eng]))
        cast_done = list(toks)
        return toks

    for nm in ["Wr", "Wk", "Wv"]:
        for kk2 in range(0, NJ, 2):
            rin = dr[nm][128 * kk2: 128 * (kk2 + 2), :].rearrange("(k p) n -> p k n", p=128)
            stage_load_cast(rin, Wsb[nm][:, D * kk2: D * (kk2 + 2)], 128, 2 * D, n3=D)
    for (nm, dst, ccols) in [("wA", wAsb, 64), ("aA", aAsb, 64), ("gA", gAsb, 192)]:
        rin = dr[nm][:, :].rearrange("(k p) n -> p k n", p=128)
        stage_load_cast(rin, dst, 128, NJ * ccols, n3=ccols)
    for (nm, dst, nparts) in [("wB", wBsb, 64), ("aB", aBsb, 64)]:
        stage_load_cast(dr[nm][:, :], dst, nparts, D)
    stage_load_cast(dr["gB"][0:128, :], gBsb[:, 0:D], 128, D)
    stage_load_cast(dr["gB"][128:192, :], gBsb[0:64, D:2 * D], 64, D)
    cast_join = [("g", g.cnt["g"]), ("s", g.cnt["s"]), ("v", g.cnt["v"])]
    if upto <= 1:
        return

    dx = g.dsem("dx")
    pb5_free = ("v", 0)
    pb6_free = ("v", 0)
    prev_sum_t = ("t", 0)
    prev_sq_s = ("s", 0)
    prev_mul_v = ("v", 0)
    for tb in range(NTB):
        g.emit("sync", lambda e, tb=tb: e.dma_start(
            out=xf32[:, :].rearrange("p (j t) -> p j t", t=TBS),
            in_=dr["x_fm"][:, TBS * tb: TBS * (tb + 1)].rearrange("(j p) t -> p j t", p=128)),
            waits=[prev_mul_v, prev_sq_s], inc=dx, n=16)
        xtok = g.tok(dx)
        for j in range(NJ):
            g.emit("scalar", lambda e, j=j: e.copy(
                out=xMr[1][:, TBS * j: TBS * (j + 1)],
                in_=xf32[:, TBS * j: TBS * (j + 1)]),
                waits=[xtok, tok_const, prev_sum_t], inc="s")
            stok = g.tok("s")
            g.emit("tensor", lambda e, j=j: e.matmul(
                pb[7][0:1, 0:TBS], onesc[:, :], xMr[1][:, TBS * j: TBS * (j + 1)],
                start=(j == 0), stop=(j == NJ - 1)),
                waits=[stok], inc="t")
        prev_sum_t = g.tok("t")
        ttok = g.tok("t")
        g.emit("scalar", lambda e: e.activation(
            out=rowbf[:, :], in_=pb[7][0:1, 0:TBS], func=AF.Copy, scale=1.0 / D),
            waits=[ttok], inc="s")
        stok = g.tok("s")
        g.emit("tensor", lambda e: e.matmul(
            pb[6][:, 0:TBS], onesb[:, :], rowbf[:, :], start=True, stop=True),
            waits=[stok, pb6_free], inc="t")
        mubc = g.tok("t")
        for j in range(NJ):
            g.emit("vector", lambda e, j=j: e.tensor_sub(
                xMr[0][:, TBS * j: TBS * (j + 1)], xf32[:, TBS * j: TBS * (j + 1)],
                pb[6][:, 0:TBS]),
                waits=[mubc, prev_sq_s], inc="v")
        pb6_free = g.tok("v")
        xc_done = g.tok("v")
        sq_t = [("t", 0), ("t", 0)]
        for j in range(NJ):
            g.emit("scalar", lambda e, j=j: e.activation(
                out=sqring[j % 2][:, :], in_=xMr[0][:, TBS * j: TBS * (j + 1)],
                func=AF.Square),
                waits=[xc_done, sq_t[j % 2]], inc="s")
            stok = g.tok("s")
            g.emit("tensor", lambda e, j=j: e.matmul(
                pb[7][0:1, 0:TBS], onesc[:, :], sqring[j % 2][:, :],
                start=(j == 0), stop=(j == NJ - 1)),
                waits=[stok], inc="t")
            sq_t[j % 2] = g.tok("t")
        prev_sq_s = g.tok("s")
        ttok = g.tok("t")
        g.emit("scalar", lambda e: e.activation(
            out=rowf[:, :], in_=pb[7][0:1, 0:TBS], func=AF.Sqrt,
            scale=1.0 / D, bias=cst[0:1, 0:1]),
            waits=[ttok, cst_tok], inc="s")
        stok = g.tok("s")
        g.emit("vector", lambda e: e.reciprocal(out=rowf[:, :], in_=rowf[:, :]),
               waits=[stok], inc="v")
        vtok = g.tok("v")
        g.emit("scalar", lambda e: e.copy(out=rowbf[:, :], in_=rowf[:, :]),
               waits=[vtok], inc="s")
        stok = g.tok("s")
        g.emit("tensor", lambda e: e.matmul(
            pb[5][:, 0:TBS], onesb[:, :], rowbf[:, :], start=True, stop=True),
            waits=[stok, pb5_free], inc="t")
        rsbc = g.tok("t")
        for j in range(NJ):
            g.emit("vector", lambda e, j=j, tb=tb: e.tensor_mul(
                xln[:, T * j + TBS * tb: T * j + TBS * (tb + 1)],
                xMr[0][:, TBS * j: TBS * (j + 1)], pb[5][:, 0:TBS]),
                waits=[rsbc], inc="v")
        pb5_free = g.tok("v")
        prev_mul_v = g.tok("v")
    ln_done = g.tok("v")
    if upto <= 2:
        return

    for j in range(NJ):
        g.emit("vector", lambda e, j=j: e.tensor_sub(
            delta[:, T * j + 1: T * (j + 1)],
            xln[:, T * j: T * (j + 1) - 1],
            xln[:, T * j + 1: T * (j + 1)]),
            waits=[ln_done], inc="v")
        g.emit("vector", lambda e, j=j: e.tensor_scalar_mul(
            delta[:, T * j: T * j + 1], xln[:, T * j: T * j + 1], -1.0),
            inc="v")
    delta_done = g.tok("v")

    bank_free = {i: ("v", 0) for i in range(8)}
    bank_free[5] = pb5_free
    bank_free[6] = pb6_free
    bank_free[7] = ("s", g.cnt["s"])
    xm_free = [g.tok("t"), g.tok("t")]
    stT_free = [(g.dsem("dsT0"), 0), (g.dsem("dsT1"), 0)]
    srng_free = [(g.dsem(f"dsr{i}"), 0) for i in range(4)]

    def build_xM(mixname, tt, slot, extra=()):
        for j in range(NJ):
            g.emit("vector", lambda e, j=j, tt=tt, slot=slot: e.scalar_tensor_tensor(
                out=xMr[slot][:, TBS * j: TBS * (j + 1)],
                in0=delta[:, T * j + TBS * tt: T * j + TBS * (tt + 1)],
                scalar=colAP(mixname, j),
                in1=xln[:, T * j + TBS * tt: T * j + TBS * (tt + 1)],
                op0=ALU.mult, op1=ALU.add),
                waits=[delta_done, xm_free[slot]] + list(extra), inc="v")
        return g.tok("v")

    def proj_pass(mixname, Wsb_t, spname, tm_spname=None, first_extra=()):
        for tt in range(NTB):
            slot = tt % 2
            xmtok = build_xM(mixname, tt, slot, extra=first_extra if tt == 0 else ())
            for m in range(NJ):
                bank = m % 2
                for k in range(NJ):
                    g.emit("tensor", lambda e, m=m, k=k, bank=bank, slot=slot: e.matmul(
                        pb[bank][:, 0:TBS],
                        Wsb_t[:, D * k + 128 * m: D * k + 128 * (m + 1)],
                        xMr[slot][:, TBS * k: TBS * (k + 1)],
                        start=(k == 0), stop=(k == NJ - 1)),
                        waits=[bank_free[bank], xmtok] if k == 0 else [],
                        inc="t" if k == NJ - 1 else None)
                mmtok = g.tok("t")
                rs = m % 4
                g.emit("vector", lambda e, bank=bank, rs=rs: e.tensor_copy(
                    out=srng[rs][:, :], in_=pb[bank][:, 0:TBS]),
                    waits=[mmtok, srng_free[rs]], inc="v")
                bank_free[bank] = g.tok("v")
                etok = g.tok("v")
                g.emit("gpsimd", lambda e, m=m, tt=tt, rs=rs: e.dma_start(
                    out=dr[spname][128 * m:128 * (m + 1), TBS * tt: TBS * (tt + 1)],
                    in_=srng[rs][:, :]),
                    waits=[etok], inc=srng_free[rs][0], n=16)
                srng_free[rs] = g.tok(srng_free[rs][0])
            if tm_spname is not None:
                for tl in range(4):
                    tblk = tt * 4 + tl
                    sl2 = tl % 2
                    for nn in range(2):
                        bank = 2 + nn
                        for k in range(NJ):
                            g.emit("tensor", lambda e, tl=tl, nn=nn, k=k, bank=bank, slot=slot: e.matmul(
                                pb[bank][:, 0:TBS],
                                xMr[slot][:, TBS * k + 128 * tl: TBS * k + 128 * (tl + 1)],
                                Wsb_t[:, D * k + TBS * nn: D * k + TBS * (nn + 1)],
                                start=(k == 0), stop=(k == NJ - 1)),
                                waits=[bank_free[bank]] if k == 0 else [],
                                inc="t" if k == NJ - 1 else None)
                        mmtok = g.tok("t")
                        g.emit("vector", lambda e, nn=nn, bank=bank, sl2=sl2: e.tensor_copy(
                            out=stageT[sl2][:, TBS * nn: TBS * (nn + 1)],
                            in_=pb[bank][:, 0:TBS]),
                            waits=[mmtok, stT_free[sl2]], inc="v")
                        bank_free[bank] = g.tok("v")
                    etok = g.tok("v")
                    g.emit("gpsimd", lambda e, tblk=tblk, sl2=sl2: e.dma_start(
                        out=dr[tm_spname][128 * tblk:128 * (tblk + 1), :],
                        in_=stageT[sl2][:, :]),
                        waits=[etok], inc=stT_free[sl2][0], n=16)
                    stT_free[sl2] = g.tok(stT_free[sl2][0])
            xm_free[slot] = g.tok("t")

    if upto <= 3:
        return
    proj_pass("mix_r", Wsb["Wr"], "r_sp", first_extra=cast_join)
    if upto <= 4:
        return
    proj_pass("mix_k", Wsb["Wk"], "k_sp", tm_spname="kT_sp")
    proj_pass("mix_v", Wsb["Wv"], "v_sp", tm_spname="vT_sp")
    if upto <= 5:
        return

    def z1_pass(mixname, Asb, zt, func, extra=()):
        for tt in range(NTB):
            slot = tt % 2
            xmtok = build_xM(mixname, tt, slot, extra=extra if tt == 0 else ())
            for k in range(NJ):
                g.emit("tensor", lambda e, k=k, slot=slot: e.matmul(
                    pb[4][0:64, 0:TBS], Asb[:, 64 * k: 64 * (k + 1)],
                    xMr[slot][:, TBS * k: TBS * (k + 1)],
                    start=(k == 0), stop=(k == NJ - 1)),
                    waits=[bank_free[4], xmtok] if k == 0 else [],
                    inc="t" if k == NJ - 1 else None)
            mmtok = g.tok("t")
            g.emit("scalar", lambda e, tt=tt, zt=zt, func=func: e.activation(
                out=zt[:, TBS * tt: TBS * (tt + 1)], in_=pb[4][0:64, 0:TBS], func=func),
                waits=[mmtok], inc="s")
            bank_free[4] = g.tok("s")
            xm_free[slot] = g.tok("t")
        return g.tok("s")

    def z2_fm(zt, Bsb, spname, bias_name, ztok):
        for m in range(NJ):
            for tt in range(NTB):
                bank = (m * NTB + tt) % 2
                rs = (m * NTB + tt) % 4
                g.emit("tensor", lambda e, m=m, tt=tt, bank=bank: e.matmul(
                    pb[bank][:, 0:TBS], Bsb[:, 128 * m: 128 * (m + 1)],
                    zt[:, TBS * tt: TBS * (tt + 1)], start=True, stop=True),
                    waits=[bank_free[bank], ztok], inc="t")
                mmtok = g.tok("t")
                g.emit("scalar", lambda e, rs=rs, bank=bank, m=m: e.activation(
                    out=srng[rs][:, :], in_=pb[bank][:, 0:TBS],
                    func=AF.Sigmoid, bias=colAP(bias_name, m)),
                    waits=[mmtok, srng_free[rs]], inc="s")
                bank_free[bank] = g.tok("s")
                etok = g.tok("s")
                g.emit("gpsimd", lambda e, m=m, tt=tt, rs=rs: e.dma_start(
                    out=dr[spname][128 * m:128 * (m + 1), TBS * tt: TBS * (tt + 1)],
                    in_=srng[rs][:, :]),
                    waits=[etok], inc=srng_free[rs][0], n=16)
                srng_free[rs] = g.tok(srng_free[rs][0])

    def z2_tm(zt, Bsb, spname, bc_tile, ztok):
        for tblk in range(16):
            slot = tblk % 2
            for nn in range(2):
                bank = 2 + nn
                g.emit("tensor", lambda e, tblk=tblk, nn=nn, bank=bank: e.matmul(
                    pb[bank][:, 0:TBS], zt[0:64, 128 * tblk: 128 * (tblk + 1)],
                    Bsb[:, TBS * nn: TBS * (nn + 1)], start=True, stop=True),
                    waits=[bank_free[bank], ztok], inc="t")
                mmtok = g.tok("t")
                g.emit("vector", lambda e, bank=bank, nn=nn, bc_tile=bc_tile: e.tensor_add(
                    pb[bank][:, 0:TBS], pb[bank][:, 0:TBS],
                    bc_tile[:, TBS * nn: TBS * (nn + 1)]),
                    waits=[mmtok, tok_const], inc="v")
                vtok = g.tok("v")
                g.emit("scalar", lambda e, slot=slot, nn=nn, bank=bank: e.activation(
                    out=stageT[slot][:, TBS * nn: TBS * (nn + 1)], in_=pb[bank][:, 0:TBS],
                    func=AF.Sigmoid),
                    waits=[vtok, stT_free[slot]], inc="s")
                bank_free[bank] = g.tok("s")
            etok = g.tok("s")
            g.emit("gpsimd", lambda e, tblk=tblk, slot=slot: e.dma_start(
                out=dr[spname][128 * tblk:128 * (tblk + 1), :], in_=stageT[slot][:, :]),
                waits=[etok], inc=stT_free[slot][0], n=16)
            stT_free[slot] = g.tok(stT_free[slot][0])

    ztok = z1_pass("mix_w", wAsb, tz, AF.Tanh)
    z2_fm(tz, wBsb, "w_sp", "w_bias", ztok)
    z2_tm(tz, wBsb, "wT_sp", wbbc, ztok)

    ztok = z1_pass("mix_a", aAsb, tza, AF.Copy)
    z2_fm(tza, aBsb, "a_sp", "a_bias", ztok)
    z2_tm(tza, aBsb, "aT_sp", abbc, ztok)

    if upto <= 6:
        return
    for tt in range(NTB):
        slot = tt % 2
        xmtok = build_xM("mix_g", tt, slot)
        for k in range(NJ):
            g.emit("tensor", lambda e, k=k, slot=slot: e.matmul(
                pb[4][:, 0:TBS], gAsb[:, 192 * k: 192 * k + 128],
                xMr[slot][:, TBS * k: TBS * (k + 1)],
                start=(k == 0), stop=(k == NJ - 1)),
                waits=[bank_free[4], xmtok] if k == 0 else [],
                inc="t" if k == NJ - 1 else None)
        for k in range(NJ):
            g.emit("tensor", lambda e, k=k, slot=slot: e.matmul(
                pb[5][0:64, 0:TBS], gAsb[:, 192 * k + 128: 192 * (k + 1)],
                xMr[slot][:, TBS * k: TBS * (k + 1)],
                start=(k == 0), stop=(k == NJ - 1)),
                waits=[bank_free[5]] if k == 0 else [],
                inc="t" if k == NJ - 1 else None)
        mmtok = g.tok("t")
        g.emit("scalar", lambda e, tt=tt: e.activation(
            out=tzg[:, TBS * tt: TBS * (tt + 1)], in_=pb[4][:, 0:TBS], func=AF.Sigmoid),
            waits=[mmtok], inc="s")
        bank_free[4] = g.tok("s")
        g.emit("scalar", lambda e, tt=tt: e.activation(
            out=tzg2[:, TBS * tt: TBS * (tt + 1)], in_=pb[5][0:64, 0:TBS], func=AF.Sigmoid),
            waits=[mmtok], inc="s")
        bank_free[5] = g.tok("s")
        xm_free[slot] = g.tok("t")
    ztok = g.tok("s")
    for m in range(NJ):
        sslot = m % 2
        for tt in range(NTB):
            bank = (m * NTB + tt) % 2
            g.emit("tensor", lambda e, m=m, tt=tt, bank=bank: e.matmul(
                pb[bank][:, 0:TBS], gBsb[:, 128 * m: 128 * (m + 1)],
                tzg[:, TBS * tt: TBS * (tt + 1)], start=True, stop=False),
                waits=[bank_free[bank], ztok], inc="t")
            g.emit("tensor", lambda e, m=m, tt=tt, bank=bank: e.matmul(
                pb[bank][:, 0:TBS], gBsb[0:64, D + 128 * m: D + 128 * (m + 1)],
                tzg2[0:64, TBS * tt: TBS * (tt + 1)], start=False, stop=True),
                inc="t")
            mmtok = g.tok("t")
            rs = (m * NTB + tt) % 4
            g.emit("vector", lambda e, rs=rs, bank=bank: e.tensor_copy(
                out=srng[rs][:, :], in_=pb[bank][:, 0:TBS]),
                waits=[mmtok, srng_free[rs]], inc="v")
            bank_free[bank] = g.tok("v")
            etok = g.tok("v")
            g.emit("gpsimd", lambda e, m=m, tt=tt, rs=rs: e.dma_start(
                out=dr["g_sp"][128 * m:128 * (m + 1), TBS * tt: TBS * (tt + 1)],
                in_=srng[rs][:, :]),
                waits=[etok], inc=srng_free[rs][0], n=16)
            srng_free[rs] = g.tok(srng_free[rs][0])


def emit_phaseS(g, es, upto=99):
    nc, dr, pb = g.nc, g.dr, g.pb
    sb = lambda name, shape, dt=BF16: es.enter_context(nc.sbuf_tensor(name, list(shape), dt))
    HBS = 256
    NCH2 = HBS // CH
    NCP = 2

    rfm = [sb(f"s_rfm{p}", (128, NJ * HBS)) for p in range(2)]
    kfm = [sb(f"s_kfm{p}", (128, NJ * HBS)) for p in range(2)]
    wfm = [sb(f"s_wfm{p}", (128, NJ * HBS)) for p in range(2)]
    afm = [sb(f"s_afm{p}", (128, NJ * HBS)) for p in range(2)]
    kTt = sb("s_kT", (128, D))
    vTt = [sb(f"s_vT{p}", (128, 2 * D)) for p in range(2)]
    aTt = sb("s_aT", (128, D))
    wTt = sb("s_wT", (128, D))
    fma = [sb(f"s_fma{p}", (128, NJ * 4 * 256)) for p in range(2)]
    aTar = [sb(f"s_aTar{p}", (128, 2 * D)) for p in range(2)]
    bTar = [sb(f"s_bTar{p}", (128, 2 * D)) for p in range(2)]
    kTar = [sb(f"s_kTar{p}", (128, 2 * D)) for p in range(2)]
    oar = sb("s_oar", (128, NJ * T))
    g.oar = oar
    scs = sb("s_scorr", (16, T), F32)
    g.scs = scs
    cCf = [sb(f"s_cC{p}", (128, 32), F32) for p in range(2)]
    stj = sb("s_st", (128, NJ * 64))
    colpk = sb("s_col", (128, 8 * NCOLV), F32)
    omka = sb("s_omka", (128, 8), F32)
    masks = {nm: sb("s_" + nm, (128, 128)) for nm in ["msu", "I_bd", "I64x2", "cumUT"]}
    mcomb = sb("s_mcomb", (128, 512))
    hiTs = sb("s_hiT", (128, NJ * H))
    bcis = sb("s_bci", (16, NJ * 128))
    rkbf = sb("s_rkb", (128, NJ * H))
    rkf32 = sb("s_rkf", (128, NJ * H), F32)
    indd = sb("s_indd", (16, D))
    carrym = sb("s_cm", (128, HBS))
    kkbc = sb("s_kkbc", (128, D))
    kabc = sb("s_kabc", (128, D))
    omkabc = sb("s_omkabc", (128, D))
    kkr = sb("s_kkr", (128, HBS))
    sqr = [sb(f"s_sqr{i}", (128, HBS)) for i in range(2)]
    nrow = sb("s_nrow", (16, HBS), F32)
    nrbf = sb("s_nrbf", (16, HBS))
    gring = sb("s_gring", (128, HBS), F32)
    gmr = sb("s_gmr", (128, HBS), F32)
    cr = sb("s_cr", (128, HBS))
    cir = sb("s_cir", (128, HBS))
    cpr = sb("s_cpr", (128, HBS))
    t1r = sb("s_t1r", (128, HBS))
    t2r = sb("s_t2r", (128, HBS))
    kmr = sb("s_kmr", (128, HBS))
    gmT = sb("s_gmT", (128, D), F32)
    ctT = sb("s_ctT", (128, D))
    ciT = sb("s_ciT", (128, D))
    cpT = sb("s_cpT", (128, D))
    kkTr = sb("s_kkTr", (128, D))
    knTr = sb("s_knTr", (128, D))
    t2T = sb("s_t2T", (128, D))
    kmT = sb("s_kmT", (128, D))
    t3T = kmT
    scu2 = [sb(f"s_scu{hh}", (128, 1344)) for hh in range(2)]
    mcl = sb("s_mcl", (128, 128))
    reff = sb("s_reff", (128, 128))
    srw = nrow

    colAP = lambda name, j: colpk[:, 8 * COLV[name] + j: 8 * COLV[name] + j + 1]

    dc2 = g.dsem("dc2")
    loads = [(colpk[:, :], dr["colpack"][:, :]), (indd[:, :], dr["ind_d"][:, :]),
             (carrym[:, :], dr["carrym"][:, 0:HBS]),
             (hiTs[:, :].rearrange("p (j h) -> p j h", h=H), dr["hiT"][:, :, :].rearrange("j p h -> p j h")),
             (bcis[:, :].rearrange("p (j c) -> p j c", c=128), dr["bcind"][:, :, :].rearrange("j p c -> p j c")),
             (rkf32[:, :].rearrange("p (j h) -> p j h", h=H), dr["rkstat"][:, :, :].rearrange("j p h -> p j h")),
             (mcomb[:, :], dr["mcomb"][:, :])]
    for nm in masks:
        loads.append((masks[nm][:, :], dr[nm][:, :]))
    for (dstap, srcap) in loads:
        g.emit("sync", lambda e, dstap=dstap, srcap=srcap: e.dma_start(out=dstap, in_=srcap),
               inc=dc2, n=16)
    tok_c2 = g.tok(dc2)
    dbc = g.dsem("dbc")
    cast_t = [("v", 0)]
    for (dstb, srcn) in [(kkbc, "kk_bc"), (kabc, "ka_bc")]:
        g.emit("sync", lambda e, srcn=srcn: e.dma_start(out=gmT[:, :], in_=dr[srcn][:, :]),
               waits=cast_t, inc=dbc, n=16)
        qt_ = g.tok(dbc)
        g.emit("vector", lambda e, dstb=dstb: e.tensor_copy(out=dstb[:, :], in_=gmT[:, :]),
               waits=[qt_], inc="v")
        cast_t = [g.tok("v")]
    bc_ready = cast_t[0]
    g.emit("vector", lambda e: e.tensor_scalar(
        out=omka[:, :], in0=colpk[:, 8 * COLV["k_a"]: 8 * COLV["k_a"] + 8],
        scalar1=-1.0, scalar2=1.0, op0=ALU.mult, op1=ALU.add),
        waits=[tok_c2], inc="v")
    g.emit("vector", lambda e: e.tensor_scalar(
        out=omkabc[:, :], in0=kabc[:, :], scalar1=-1.0, scalar2=1.0,
        op0=ALU.mult, op1=ALU.add),
        waits=[bc_ready], inc="v")
    g.emit("scalar", lambda e: e.copy(out=rkbf[:, :], in_=rkf32[:, :]),
           waits=[tok_c2], inc="s")
    g.emit("gpsimd", lambda e: e.memset(stj[:, :], 0.0), waits=[tok_c2], inc="g")
    tok_init = [("v", g.cnt["v"]), ("s", g.cnt["s"]), ("g", g.cnt["g"])]

    dls = {nm: g.dsem("dl_" + nm) for nm in ["kT", "aT", "wT"]}
    for nm in ["r", "k", "w", "a", "vT"]:
        for p in range(2):
            dls[(nm, p)] = g.dsem(f"dl_{nm}{p}")
    consumed = [{nm: [("t", 0), ("v", 0), ("s", 0)] for nm in ["r", "k", "w", "a", "kT", "vT", "aT", "wT"]} for p in range(2)]
    bank_free = {i: [("v", 0)] for i in range(8)}
    dso = g.dsem("dso")
    doo = g.dsem("doo")
    scr_busy = [[("t", 0)], [("t", 0)]]
    mclreff_busy = [("t", 0)]
    stj_tok = [[("v", 0)] for _ in range(NJ)]
    half_state = [None, None]
    sq_guard = [("t", 0), ("t", 0)]
    kkr_guard = [("s", 0)]
    tmu = [("v", 0)]
    PING = [(0, 128, 256), (768, 896, 1024)]

    def prep_ops(hv):
        P = hv % 2
        t0 = HBS * hv
        ops = []
        st = {"ltok": {}, "karena": None, "prep_done": None}
        half_state[P] = st

        def ld(nm, spn, dst3):
            def run():
                sem = dls[(nm, P)]
                g.emit("sync", lambda e: e.dma_start(
                    out=dst3[0], in_=dst3[1]),
                    waits=consumed[P][nm], inc=sem, n=16)
                st["ltok"][nm] = g.tok(sem)
            return run
        for nm, spn, dst in [("r", "r_sp", rfm[P]), ("k", "k_sp", kfm[P]),
                             ("w", "w_sp", wfm[P]), ("a", "a_sp", afm[P])]:
            ops.append(ld(nm, spn, (dst[:, :].rearrange("p (j t) -> p j t", t=HBS),
                                    dr[spn][:, t0: t0 + HBS].rearrange("(j p) t -> p j t", p=128))))
        ops.append(ld("vT", "vT_sp", (vTt[P][:, :].rearrange("p (b n) -> p b n", n=D),
                                      dr["vT_sp"][t0: t0 + HBS, :].rearrange("(b p) n -> p b n", p=128))))

        def fmj(j):
            def run():
                lt = st["ltok"]
                prev_v = g.tok("v")
                g.emit("vector", lambda e: e.tensor_scalar_mul(
                    kkr[:, :], kfm[P][:, HBS * j: HBS * (j + 1)], colAP("k_k", j)),
                    waits=[lt["k"], tok_c2, prev_v, kkr_guard[0]] + (bank_free[4] if j == 0 else []), inc="v")
                vt = g.tok("v")
                g.emit("scalar", lambda e: e.activation(
                    out=sqr[j % 2][:, :], in_=kkr[:, :], func=AF.Square),
                    waits=[vt, sq_guard[j % 2]], inc="s")
                kkr_guard[0] = ("s", g.cnt["s"])
                st0 = g.tok("s")
                g.emit("tensor", lambda e: e.matmul(
                    pb[4][0:16, 0:HBS], hiTs[:, H * j: H * (j + 1)], sqr[j % 2][:, :],
                    start=(j == 0), stop=(j == NJ - 1)),
                    waits=[st0], inc="t")
                sq_guard[j % 2] = g.tok("t")
            return run
        for j in range(NJ):
            ops.append(fmj(j))

        def nrows():
            ttok = g.tok("t")
            g.emit("scalar", lambda e: e.activation(
                out=nrow[:, :], in_=pb[4][0:16, 0:HBS], func=AF.Sqrt),
                waits=[ttok, g.tok(dso)], inc="s")
            st0 = g.tok("s")
            bank_free[4] = [st0]
            g.emit("vector", lambda e: e.tensor_scalar_max(
                nrow[:, :], nrow[:, :], 1e-12), waits=[st0], inc="v")
            g.emit("vector", lambda e: e.reciprocal(nrow[:, :], nrow[:, :]),
                   waits=[g.tok("v")], inc="v")
            vt = g.tok("v")
            g.emit("scalar", lambda e: e.copy(out=nrbf[:, :], in_=nrow[:, :]),
                   waits=[vt], inc="s")
            st["ninv"] = g.tok("s")
        ops.append(nrows)

        def cfj(j):
            def run():
                lt = st["ltok"]
                bcb = 5
                g.emit("tensor", lambda e, bcb=bcb: e.matmul(
                    pb[bcb][:, 0:HBS], bcis[:, 128 * j: 128 * (j + 1)], nrbf[:, :],
                    start=True, stop=True),
                    waits=[st["ninv"]] + bank_free[bcb], inc="t")
                bct = g.tok("t")
                g.emit("vector", lambda e: e.tensor_scalar_mul(
                    kkr[:, :], kfm[P][:, HBS * j: HBS * (j + 1)], colAP("k_k", j)),
                    waits=[bct, g.tok("v"), kkr_guard[0]], inc="v")
                g.emit("vector", lambda e, bcb=bcb: e.tensor_mul(
                    kkr[:, :], kkr[:, :], pb[bcb][:, 0:HBS]),
                    waits=[g.tok("v")], inc="v")
                bank_free[bcb] = [g.tok("v")]
                g.emit("vector", lambda e: e.tensor_tensor_scan(
                    out=gring[:, :], data0=carrym[:, :], data1=wfm[P][:, HBS * j: HBS * (j + 1)],
                    initial=0.0, op0=ALU.mult, op1=ALU.add),
                    waits=[lt["w"], ("s", g.cnt["s"])], inc="v")
                vt = g.tok("v")
                g.emit("scalar", lambda e: e.activation(
                    out=cCf[P][:, 4 * j: 4 * (j + 1)], in_=gring[:, CH - 1:HBS:CH],
                    func=AF.Exp, scale=DECAY),
                    waits=[vt], inc="s")
                g.emit("scalar", lambda e: e.activation(
                    out=cr[:, :], in_=gring[:, :], func=AF.Exp, scale=DECAY), inc="s")
                g.emit("scalar", lambda e: e.activation(
                    out=cir[:, :], in_=gring[:, :], func=AF.Exp, scale=-DECAY), inc="s")
                g.emit("vector", lambda e: e.tensor_sub(
                    gmr[:, :], gring[:, :], wfm[P][:, HBS * j: HBS * (j + 1)]),
                    waits=[vt], inc="v")
                vt2 = g.tok("v")
                g.emit("scalar", lambda e: e.activation(
                    out=cpr[:, :], in_=gmr[:, :], func=AF.Exp, scale=DECAY),
                    waits=[vt2], inc="s")
                cfam = g.tok("s")

                def arr(off):
                    return fma[P][:, 1024 * j: 1024 * (j + 1)].rearrange(
                        "p (c x) -> p c x", x=256)[:, :, off:off + 64]
                def rng(t_):
                    return t_[:, :].rearrange("p (c x) -> p c x", x=64)
                g.emit("vector", lambda e: e.scalar_tensor_tensor(
                    out=arr(0), in0=kkr[:, :].rearrange("p (c x) -> p c x", x=64),
                    scalar=-1.0, in1=rng(cpr), op0=ALU.mult, op1=ALU.mult),
                    waits=[cfam], inc="v")
                g.emit("vector", lambda e: e.tensor_mul(
                    arr(64), rfm[P][:, HBS * j: HBS * (j + 1)].rearrange("p (c x) -> p c x", x=64),
                    rng(cr)),
                    waits=[lt["r"], cfam], inc="v")
                g.emit("vector", lambda e: e.tensor_scalar(
                    out=t1r[:, :], in0=afm[P][:, HBS * j: HBS * (j + 1)],
                    scalar1=colAP("k_a", j), scalar2=omka[:, j:j + 1],
                    op0=ALU.mult, op1=ALU.add),
                    waits=[lt["a"]] + tok_init, inc="v")
                g.emit("vector", lambda e: e.tensor_mul(
                    kmr[:, :], kfm[P][:, HBS * j: HBS * (j + 1)], t1r[:, :]),
                    waits=[g.tok("v")], inc="v")
                g.emit("vector", lambda e: e.tensor_mul(
                    arr(192), rng(kmr), rng(cir)), waits=[g.tok("v")], inc="v")
                g.emit("vector", lambda e: e.tensor_mul(
                    t2r[:, :], kkr[:, :], afm[P][:, HBS * j: HBS * (j + 1)]),
                    waits=[g.tok("v")], inc="v")
                g.emit("vector", lambda e: e.tensor_mul(
                    arr(128), rng(t2r), rng(cir)), waits=[g.tok("v")], inc="v")
                g.emit("vector", lambda e: e.tensor_mul(
                    t2r[:, :], rfm[P][:, HBS * j: HBS * (j + 1)], kmr[:, :]),
                    waits=[g.tok("v")], inc="v")
                vt3 = g.tok("v")
                g.emit("tensor", lambda e: e.matmul(
                    pb[4][0:16, 0:HBS], rkbf[:, H * j: H * (j + 1)], t2r[:, :],
                    start=(j == 0), stop=(j == NJ - 1)),
                    waits=[vt3] + (bank_free[4] if j == 0 else []), inc="t")
                st["karena"] = g.tok("v")
            return run
        for j in range(NJ):
            ops.append(cfj(j))

        def scorr_fin():
            ttok = g.tok("t")
            g.emit("vector", lambda e: e.tensor_copy(
                out=srw[:, :], in_=pb[4][0:16, 0:HBS]), waits=[ttok, g.tok(dso), ("s", g.cnt["s"])], inc="v")
            vt = g.tok("v")
            bank_free[4] = [vt]
            g.emit("vector", lambda e: e.tensor_copy(
                out=scs[:, t0: t0 + HBS], in_=srw[:, :]), waits=[g.tok("v")], inc="v")
            vt2 = g.tok("v")
            g.emit("gpsimd", lambda e: e.dma_start(
                out=dr["scorr_sp"][:, t0: t0 + HBS], in_=srw[:, :]),
                waits=[vt2], inc=dso, n=16)
        ops.append(scorr_fin)

        def tmb(b4):
            def run():
                lt = st["ltok"]
                for (nm, dst) in [("kT", kTt), ("aT", aTt), ("wT", wTt)]:
                    g.emit("sync", lambda e, nm=nm, dst=dst: e.dma_start(
                        out=dst[:, :],
                        in_=dr[nm + "_sp"][t0 + 128 * b4: t0 + 128 * (b4 + 1), :]),
                        waits=tmu, inc=dls[nm], n=16)
                    lt[nm] = g.tok(dls[nm])
                for nn in range(2):
                    g.emit("tensor", lambda e, nn=nn: e.matmul(
                        pb[4 + nn][:, 0:TBS], masks["cumUT"][:, :],
                        wTt[:, TBS * nn: TBS * (nn + 1)],
                        start=True, stop=True),
                        waits=[lt["wT"], tok_c2] + bank_free[4 + nn], inc="t")
                gt = g.tok("t")
                for nn in range(2):
                    g.emit("scalar", lambda e, nn=nn: e.activation(
                        out=ctT[:, TBS * nn: TBS * (nn + 1)], in_=pb[4 + nn][:, 0:TBS],
                        func=AF.Exp, scale=DECAY),
                        waits=[gt], inc="s")
                    g.emit("scalar", lambda e, nn=nn: e.activation(
                        out=ciT[:, TBS * nn: TBS * (nn + 1)], in_=pb[4 + nn][:, 0:TBS],
                        func=AF.Exp, scale=-DECAY),
                        inc="s")
                    g.emit("vector", lambda e, nn=nn: e.tensor_sub(
                        gmT[:, TBS * nn: TBS * (nn + 1)], pb[4 + nn][:, 0:TBS],
                        wTt[:, TBS * nn: TBS * (nn + 1)]),
                        waits=[gt], inc="v")
                    vt = g.tok("v")
                    g.emit("scalar", lambda e, nn=nn: e.activation(
                        out=cpT[:, TBS * nn: TBS * (nn + 1)], in_=gmT[:, TBS * nn: TBS * (nn + 1)],
                        func=AF.Exp, scale=DECAY),
                        waits=[vt], inc="s")
                bank_free[4] = [g.tok("s"), g.tok("v")]
                bank_free[5] = bank_free[4]
                cfamT = g.tok("s")
                g.emit("vector", lambda e: e.tensor_mul(
                    kkTr[:, :], kTt[:, :], kkbc[:, :]),
                    waits=[lt["kT"], bc_ready, g.tok("v")], inc="v")
                kkt = g.tok("v")
                for nn in range(2):
                    g.emit("tensor", lambda e, nn=nn: e.matmul(
                        pb[4 + nn][:, 0:TBS], nrbf[:, 128 * b4: 128 * (b4 + 1)],
                        indd[:, TBS * nn: TBS * (nn + 1)], start=True, stop=True),
                        waits=[st["ninv"]] + bank_free[4 + nn], inc="t")
                    nbt = g.tok("t")
                    g.emit("vector", lambda e, nn=nn: e.tensor_mul(
                        knTr[:, TBS * nn: TBS * (nn + 1)], kkTr[:, TBS * nn: TBS * (nn + 1)],
                        pb[4 + nn][:, 0:TBS]),
                        waits=[nbt, kkt], inc="v")
                    bank_free[4 + nn] = [g.tok("v")]
                knt = g.tok("v")
                g.emit("vector", lambda e: e.scalar_tensor_tensor(
                    out=aTar[P][:, D * b4: D * (b4 + 1)], in0=knTr[:, :], scalar=-1.0,
                    in1=cpT[:, :], op0=ALU.mult, op1=ALU.mult),
                    waits=[knt, cfamT], inc="v")
                g.emit("vector", lambda e: e.tensor_mul(
                    t2T[:, :], aTt[:, :], kabc[:, :]),
                    waits=[lt["aT"], bc_ready], inc="v")
                g.emit("vector", lambda e: e.tensor_add(t2T[:, :], t2T[:, :], omkabc[:, :]),
                       waits=[g.tok("v")], inc="v")
                g.emit("vector", lambda e: e.tensor_mul(
                    kmT[:, :], kTt[:, :], t2T[:, :]), waits=[g.tok("v")], inc="v")
                g.emit("vector", lambda e: e.tensor_mul(
                    kTar[P][:, D * b4: D * (b4 + 1)], kmT[:, :], ciT[:, :]),
                    waits=[g.tok("v"), cfamT], inc="v")
                g.emit("vector", lambda e: e.tensor_mul(
                    t3T[:, :], knTr[:, :], aTt[:, :]), waits=[g.tok("v")], inc="v")
                g.emit("vector", lambda e: e.tensor_mul(
                    bTar[P][:, D * b4: D * (b4 + 1)], t3T[:, :], ciT[:, :]),
                    waits=[g.tok("v"), cfamT], inc="v")
                st["prep_done"] = g.tok("v")
                tmu[:] = [g.tok("v"), g.tok("s")]
            return run
        ops.append(tmb(0))
        ops.append(tmb(1))
        return ops

    def group_ops(hv):
        P = hv % 2
        tb = hv // 2
        hb = hv % 2
        st = half_state[P]
        allops = []
        for cpl in range(NCP):
            cp_abs = 2 * hb + cpl
            for j in range(NJ):
                allops.extend(make_group(hv, cpl, j, P, st))
        return allops

    def make_group(hv, cpl, j, P, st):
        tb = hv // 2
        c0g = 2 * cpl
        c0_tb = (hv % 2) * 4 + 2 * cpl
        B = 0
        U = []
        for hh in range(2):
            U.append({"h": 2 * j + hh, "hp": hh * 64, "tmc": D * cpl + 64 * (2 * j + hh),
                      "b0": 2 * hh, "b1": 2 * hh + 1,
                      "scu": scu2[hh], "cur": 0, "ev": None, "tt": None, "hh": hh})

        def A_(hp, c):
            return fma[P][hp:hp + 64, 1024 * j + 256 * c: 1024 * j + 256 * c + 64]
        def R_(hp, c):
            return fma[P][hp:hp + 64, 1024 * j + 256 * c + 64: 1024 * j + 256 * c + 128]
        def B_(hp, c):
            return fma[P][hp:hp + 64, 1024 * j + 256 * c + 128: 1024 * j + 256 * c + 192]
        def K_(hp, c):
            return fma[P][hp:hp + 64, 1024 * j + 256 * c + 192: 1024 * j + 256 * c + 256]

        def stA(u):
            hp, b0, b1 = u["hp"], u["b0"], u["b1"]
            pre_w = [st["karena"], st["prep_done"]] + bank_free[b0] + bank_free[b1] + scr_busy[u["hh"]]
            for ci in range(2):
                c = c0g + ci
                po = 64 * ci
                for (rgn, lhs_f, rhs_f, bnk, ofs) in [
                        (0, A_, B_, b0, 0), (1, K_, A_, b0, 128),
                        (2, B_, R_, b0, 256), (3, K_, R_, b0, 384),
                        (4, B_, A_, b1, 384)]:
                    g.emit("tensor", lambda e, c=c, po=po, hp=hp, bnk=bnk, ofs=ofs, lhs_f=lhs_f, rhs_f=rhs_f: e.matmul(
                        pb[bnk][po:po + 64, ofs + po:ofs + po + 64], lhs_f(hp, c), rhs_f(hp, c),
                        start=True, stop=True, tile_position=(hp, po)),
                        waits=pre_w if (ci == 0 and rgn == 0) else [], inc="t")
            u["tt"] = g.tok("t")

        def stB(u):
            s, b0, b1 = u["scu"], u["b0"], u["b1"]
            g.emit("vector", lambda e, s=s, b0=b0: e.tensor_mul(
                s[:, 256:768], pb[b0][:, 0:512], mcomb[:, :]),
                waits=[u["tt"]], inc="v")
            g.emit("vector", lambda e, s=s, b1=b1: e.tensor_mul(
                s[:, 128:256], pb[b1][:, 384:512], masks["msu"][:, :]),
                inc="v")
            g.emit("vector", lambda e, s=s: e.tensor_add(
                s[:, 768:896], s[:, 128:256], masks["I_bd"][:, :]),
                waits=[g.tok("v")], inc="v")
            u["ev"] = g.tok("v")
            bank_free[b0] = [u["ev"]]

        def mkC(it):
            def stC(u):
                s, b1 = u["scu"], u["b1"]
                cb = PING[u["cur"]]
                if it == 1:
                    g.emit("tensor", lambda e, s=s, b1=b1, cb=cb: e.matmul(
                        pb[b1][:, 128:256], s[:, cb[2]:cb[2] + 128], s[:, cb[1]:cb[1] + 128],
                        start=True, stop=True),
                        waits=[u["ev"]], inc="t")
                else:
                    g.emit("tensor", lambda e, s=s, b1=b1, cb=cb: e.matmul(
                        pb[b1][:, 0:256], s[:, cb[2]:cb[2] + 128], s[:, cb[0]:cb[0] + 256],
                        start=True, stop=True),
                        waits=[u["ev"]], inc="t")
                g.emit("tensor", lambda e, s=s, b1=b1, cb=cb: e.matmul(
                    pb[b1][:, 256:384], s[:, cb[1]:cb[1] + 128], s[:, cb[2]:cb[2] + 128],
                    start=True, stop=True), inc="t")
                u["tt"] = g.tok("t")
            return stC

        def mkD(it):
            def stD(u):
                s, b1 = u["scu"], u["b1"]
                cb = PING[u["cur"]]
                ob = PING[1 - u["cur"]]
                g.emit("vector", lambda e, s=s, b1=b1, ob=ob: e.tensor_copy(
                    out=s[:, ob[1]:ob[1] + 256], in_=pb[b1][:, 128:384]),
                    waits=[u["tt"]], inc="v")
                if it > 1:
                    g.emit("vector", lambda e, s=s, b1=b1, cb=cb, ob=ob: e.tensor_add(
                        s[:, ob[0]:ob[0] + 128], pb[b1][:, 0:128], s[:, cb[0]:cb[0] + 128]),
                        inc="v")
                u["ev"] = g.tok("v")
                u["cur"] = 1 - u["cur"]
            return stD

        def stCf(u):
            s, b1 = u["scu"], u["b1"]
            cb = PING[u["cur"]]
            g.emit("tensor", lambda e, s=s, b1=b1, cb=cb: e.matmul(
                pb[b1][:, 0:128], s[:, cb[2]:cb[2] + 128], s[:, cb[0]:cb[0] + 128],
                start=True, stop=True),
                waits=[u["ev"]], inc="t")
            u["tt"] = g.tok("t")

        def stDf(u):
            s, b1 = u["scu"], u["b1"]
            cb = PING[u["cur"]]
            ob = PING[1 - u["cur"]]
            g.emit("vector", lambda e, s=s, b1=b1, cb=cb, ob=ob: e.tensor_add(
                s[:, ob[0]:ob[0] + 128], pb[b1][:, 0:128], s[:, cb[0]:cb[0] + 128]),
                waits=[u["tt"]], inc="v")
            u["ev"] = g.tok("v")
            u["cur"] = 1 - u["cur"]

        def stE1(u):
            s, b1, tmc = u["scu"], u["b1"], u["tmc"]
            g.emit("tensor", lambda e, s=s, b1=b1, tmc=tmc: e.matmul(
                pb[b1][:, 448:512], s[:, 384:512], vTt[P][:, tmc: tmc + 64],
                start=True, stop=True),
                waits=[u["ev"], st["ltok"]["vT"]], inc="t")
            u["tt"] = g.tok("t")

        def stF1(u):
            s, b1 = u["scu"], u["b1"]
            g.emit("vector", lambda e, s=s, b1=b1: e.tensor_copy(
                out=s[:, 1152:1216], in_=pb[b1][:, 448:512]),
                waits=[u["tt"]], inc="v")
            u["ev"] = g.tok("v")

        def stE2(u):
            s, b1, tmc = u["scu"], u["b1"], u["tmc"]
            tcol = PING[u["cur"]][0]
            g.emit("tensor", lambda e, s=s, b1=b1, tcol=tcol: e.matmul(
                pb[b1][:, 128:192], s[:, tcol:tcol + 128], s[:, 1152:1216],
                start=True, stop=True),
                waits=[u["ev"]], inc="t")
            g.emit("tensor", lambda e, s=s, b1=b1, tcol=tcol, tmc=tmc: e.matmul(
                pb[b1][:, 192:256], s[:, tcol:tcol + 128], aTar[P][:, tmc: tmc + 64],
                start=True, stop=True), inc="t")
            u["tt"] = g.tok("t")

        def stF2(u):
            s, b1 = u["scu"], u["b1"]
            g.emit("vector", lambda e, s=s, b1=b1: e.tensor_copy(
                out=s[:, 1216:1344], in_=pb[b1][:, 128:256]),
                waits=[u["tt"]], inc="v")
            u["ev"] = g.tok("v")

        def stE3(u):
            s, b1, hp, tmc = u["scu"], u["b1"], u["hp"], u["tmc"]
            xw = [u["ev"]] + mclreff_busy
            for ci in range(2):
                po = 64 * ci
                g.emit("tensor", lambda e, s=s, b1=b1, hp=hp, po=po, tmc=tmc: e.matmul(
                    pb[b1][hp:hp + 64, 256 + po:256 + po + 64],
                    s[po:po + 64, 1280:1344], bTar[P][po:po + 64, tmc: tmc + 64],
                    start=True, stop=True, tile_position=(po, hp)),
                    waits=xw if ci == 0 else [], inc="t")
                g.emit("tensor", lambda e, s=s, b1=b1, hp=hp, po=po: e.matmul(
                    pb[b1][hp:hp + 64, 384 + po:384 + po + 64],
                    s[po:po + 64, 1280:1344], s[po:po + 64, 512 + po:512 + po + 64],
                    start=True, stop=True, tile_position=(po, hp)), inc="t")
            u["tt"] = g.tok("t")

        grp = {"mr": None}

        def stF3(u):
            s, b1, hp = u["scu"], u["b1"], u["hp"]
            g.emit("vector", lambda e, s=s, b1=b1, hp=hp: e.tensor_add(
                mcl[hp:hp + 64, 0:128], pb[b1][hp:hp + 64, 256:384],
                masks["I64x2"][hp:hp + 64, 0:128]),
                waits=[u["tt"]], inc="v")
            rhat2 = fma[P][hp:hp + 64, 1024 * j: 1024 * (j + 1)].rearrange(
                "p (c x) -> p c x", x=256)[:, c0g:c0g + 2, 64:128]
            g.emit("vector", lambda e, s=s, b1=b1, hp=hp, rhat2=rhat2: e.tensor_add(
                reff[hp:hp + 64, 0:128].rearrange("p (c x) -> p c x", x=64),
                pb[b1][hp:hp + 64, 384:512].rearrange("p (c x) -> p c x", x=64),
                rhat2),
                inc="v")
            u["ev"] = g.tok("v")
            bank_free[b1] = [u["ev"]]
            grp["mr"] = u["ev"]

        ops = []
        stages = [stA, stB, mkC(1), mkD(1), mkC(2), mkD(2), mkC(3), mkD(3),
                  mkC(4), mkD(4), mkC(5), mkD(5), stCf, stDf,
                  stE1, stF1, stE2, stF2, stE3, stF3]
        for stg in stages:
            ops.append(lambda stg=stg: stg(U[0]))
            ops.append(lambda stg=stg: stg(U[1]))

        def oseq(ci):
            def run():
                mr_t = grp["mr"]
                c = c0g + ci
                c_tb = (hv % 2) * 4 + c
                po = 64 * ci
                ow = [mr_t] + bank_free[6] + bank_free[7] + stj_tok[j]
                for hh in range(2):
                    hp = hh * 64
                    ob = 6 + hh
                    s = scu2[hh]
                    tmc = D * cpl + 64 * (2 * j + hh)
                    g.emit("tensor", lambda e, hp=hp, ci=ci, ob=ob: e.matmul(
                        pb[ob][hp:hp + 64, 0:64], stj[hp:hp + 64, 64 * j:64 * (j + 1)],
                        reff[hp:hp + 64, 64 * ci:64 * (ci + 1)],
                        start=True, stop=False, tile_position=(hp, hp)),
                        waits=ow if hh == 0 else [], inc="t")
                    g.emit("tensor", lambda e, s=s, po=po, hp=hp, ob=ob: e.matmul(
                        pb[ob][hp:hp + 64, 0:64], s[po:po + 64, 1216:1280],
                        s[po:po + 64, 512 + po:512 + po + 64],
                        start=False, stop=False, tile_position=(po, hp)), inc="t")
                    g.emit("tensor", lambda e, s=s, po=po, hp=hp, tmc=tmc, ob=ob: e.matmul(
                        pb[ob][hp:hp + 64, 0:64], vTt[P][po:po + 64, tmc: tmc + 64],
                        s[po:po + 64, 640 + po:640 + po + 64],
                        start=False, stop=True, tile_position=(po, hp)), inc="t")
                ot = g.tok("t")
                for hh in range(2):
                    hp = hh * 64
                    ob = 6 + hh
                    g.emit("vector", lambda e, c_tb=c_tb, hp=hp, ob=ob: e.tensor_copy(
                        out=oar[hp:hp + 64, T * j + TBS * tb + CH * c_tb: T * j + TBS * tb + CH * (c_tb + 1)],
                        in_=pb[ob][hp:hp + 64, 0:64]),
                        waits=[ot], inc="v")
                vt = g.tok("v")
                oev = [vt]
                sw = [ot, vt]
                for hh in range(2):
                    hp = hh * 64
                    sbk = 6 + hh
                    s = scu2[hh]
                    tmc = D * cpl + 64 * (2 * j + hh)
                    g.emit("tensor", lambda e, hp=hp, ci=ci, sbk=sbk: e.matmul(
                        pb[sbk][hp:hp + 64, 64:128], mcl[hp:hp + 64, 64 * ci:64 * (ci + 1)],
                        stj[hp:hp + 64, 64 * j:64 * (j + 1)],
                        start=True, stop=False, tile_position=(hp, hp)),
                        waits=sw if hh == 0 else [], inc="t")
                    g.emit("tensor", lambda e, s=s, po=po, hp=hp, tmc=tmc, sbk=sbk: e.matmul(
                        pb[sbk][hp:hp + 64, 64:128], bTar[P][po:po + 64, tmc: tmc + 64],
                        s[po:po + 64, 1216:1280],
                        start=False, stop=False, tile_position=(po, hp)), inc="t")
                    g.emit("tensor", lambda e, po=po, hp=hp, tmc=tmc, sbk=sbk: e.matmul(
                        pb[sbk][hp:hp + 64, 64:128], kTar[P][po:po + 64, tmc: tmc + 64],
                        vTt[P][po:po + 64, tmc: tmc + 64],
                        start=False, stop=True, tile_position=(po, hp)), inc="t")
                st_t = g.tok("t")
                for hh in range(2):
                    hp = hh * 64
                    sbk = 6 + hh
                    g.emit("vector", lambda e, c=c, hp=hp, sbk=sbk: e.tensor_scalar_mul(
                        stj[hp:hp + 64, 64 * j:64 * (j + 1)], pb[sbk][hp:hp + 64, 64:128],
                        cCf[P][hp:hp + 64, 4 * j + c: 4 * j + c + 1]),
                        waits=[st_t], inc="v")
                vt = g.tok("v")
                bank_free[6] = [vt]
                bank_free[7] = [vt]
                stj_tok[j] = [vt]
                if ci == 1:
                    scr_busy[0] = scr_busy[1] = [g.tok("t")]
                    mclreff_busy[:] = [g.tok("t")]
            return run

        ops.append(oseq(0))
        ops.append(oseq(1))
        return ops

    pops = prep_ops(0)
    for op in pops:
        op()
    for hv in range(8):
        gops = group_ops(hv)
        nxt = prep_ops(hv + 1) if hv < 7 else []
        ratio = max(1, len(gops) // max(1, len(nxt)))
        gi = 0
        for op in gops:
            op()
            gi += 1
            if nxt and gi % ratio == 0:
                nxt.pop(0)()
        for op in nxt:
            op()
        P = hv % 2
        consumed[P] = {nm: [("t", g.cnt["t"]), ("v", g.cnt["v"]), ("s", g.cnt["s"])]
                       for nm in ["r", "k", "w", "a", "kT", "vT", "aT", "wT"]}

    otok = [("v", g.cnt["v"]), ("s", g.cnt["s"])]
    for j in range(NJ):
        g.emit("gpsimd", lambda e, j=j: e.dma_start(
            out=dr["o_sp"][128 * j:128 * (j + 1), :], in_=oar[:, T * j: T * (j + 1)]),
            waits=otok, inc=doo, n=16)


def emit_phaseO(g, es, upto=99):
    nc, dr, pb = g.nc, g.dr, g.pb
    sb = lambda name, shape, dt=BF16: es.enter_context(nc.sbuf_tensor(name, list(shape), dt))

    Wosb = sb("o_Wo", (128, NJ * D))
    wstg = sb("o_wstg", (128, 2 * D), F32)
    colpk = sb("o_col", (128, 8 * NCOLV), F32)
    hiTs = sb("o_hiT", (128, NJ * H))
    bcis = sb("o_bci", (16, NJ * 128))
    onesc = sb("o_ones", (128, 1))
    onesb = sb("o_onesb", (1, 128))
    cst = sb("o_cst", (128, 8), F32)
    oL = sb("o_oL", (128, NJ * TBS))
    odar = sb("o_od", (128, NJ * TBS))
    vfm = sb("o_v", (128, NJ * TBS))
    gfm = sb("o_g", (128, NJ * TBS))
    xf32 = sb("o_x", (128, NJ * TBS), F32)
    aoutT = sb("o_aout", (128, NJ * TBS))
    atnT = sb("o_atn", (128, NJ * TBS))
    xc2 = sb("o_xc2", (128, NJ * TBS))
    srow = sb("o_srow", (16, TBS), F32)
    srbf = sb("o_srbf", (16, TBS))
    murow = sb("o_mur", (1, TBS))
    mrow16 = sb("o_m16", (16, TBS))
    rowf = sb("o_rowf", (16, TBS), F32)
    sqr = [sb(f"o_sqr{i}", (128, TBS)) for i in range(2)]
    ring1 = sb("o_r1", (128, TBS))
    ring2 = sb("o_r2", (128, TBS), F32)

    colAP = lambda name, j: colpk[:, 8 * COLV[name] + j: 8 * COLV[name] + j + 1]

    dc3 = g.dsem("dc3")
    loads = [(colpk[:, :], dr["colpack"][:, :]), (onesc[:, :], dr["ones_col"][:, :]),
             (onesb[:, :], dr["onesb_row"][:, :]),
             (hiTs[:, :].rearrange("p (j h) -> p j h", h=H), dr["hiT"][:, :, :].rearrange("j p h -> p j h")),
             (bcis[:, :].rearrange("p (j c) -> p j c", c=128), dr["bcind"][:, :, :].rearrange("j p c -> p j c"))]
    for (dstap, srcap) in loads:
        g.emit("sync", lambda e, dstap=dstap, srcap=srcap: e.dma_start(out=dstap, in_=srcap),
               inc=dc3, n=16)
    tok_c3 = g.tok(dc3)
    g.emit("gpsimd", lambda e: e.memset(cst[:, 0:1], LN_EPS), inc="g")
    g.emit("gpsimd", lambda e: e.memset(cst[:, 1:2], GN_EPS), inc="g")
    cst_tok = g.tok("g")

    dwo = g.dsem("dwo")
    cast_done = [("v", 0)]
    for kk2 in range(0, NJ, 2):
        g.emit("sync", lambda e, kk2=kk2: e.dma_start(
            out=wstg[:, :].rearrange("p (c n) -> p c n", n=D),
            in_=dr["Wo"][128 * kk2:128 * (kk2 + 2), :].rearrange("(k p) n -> p k n", p=128)),
            waits=cast_done, inc=dwo, n=16)
        qt = g.tok(dwo)
        for c2 in range(2):
            eng = ["gpsimd", "vector"][c2 % 2]
            g.emit(eng, lambda e, kk2=kk2, c2=c2: e.tensor_copy(
                out=Wosb[:, D * (kk2 + c2): D * (kk2 + c2 + 1)], in_=wstg[:, D * c2: D * (c2 + 1)]),
                waits=[qt], inc=ENG_SEM[eng])
        cast_done = [("g", g.cnt["g"]), ("v", g.cnt["v"])]
    wo_ready = cast_done

    dlo = {nm: g.dsem("do_" + nm) for nm in ["o", "v", "g", "x", "s"]}
    consumed = {nm: [("t", 0), ("v", 0), ("s", 0)] for nm in dlo}
    bank_free = {i: [("v", 0)] for i in range(8)}
    dx2 = g.dsem("dx2")
    x2_busy = [("v", 0)]

    for tb in range(NTB):
        ltok = {}
        srcs = {"o": ("o_sp", oL), "v": ("v_sp", vfm), "g": ("g_sp", gfm)}
        for nm, (spn, dst) in srcs.items():
            g.emit("sync", lambda e, spn=spn, dst=dst, tb=tb: e.dma_start(
                out=dst[:, :].rearrange("p (j t) -> p j t", t=TBS),
                in_=dr[spn][:, TBS * tb: TBS * (tb + 1)].rearrange("(j p) t -> p j t", p=128)),
                waits=consumed[nm], inc=dlo[nm], n=16)
            ltok[nm] = g.tok(dlo[nm])
        g.emit("sync", lambda e, tb=tb: e.dma_start(
            out=xf32[:, :].rearrange("p (j t) -> p j t", t=TBS),
            in_=dr["x_fm"][:, TBS * tb: TBS * (tb + 1)].rearrange("(j p) t -> p j t", p=128)),
            waits=consumed["x"], inc=dlo["x"], n=16)
        ltok["x"] = g.tok(dlo["x"])
        g.emit("sync", lambda e, tb=tb: e.dma_start(
            out=srow[:, :], in_=dr["scorr_sp"][:, TBS * tb: TBS * (tb + 1)]),
            waits=consumed["s"], inc=dlo["s"], n=16)
        ltok["s"] = g.tok(dlo["s"])

        for j in range(NJ):
            g.emit("tensor", lambda e, j=j: e.matmul(
                pb[4][0:16, 0:TBS], hiTs[:, H * j: H * (j + 1)], oL[:, TBS * j: TBS * (j + 1)],
                start=(j == 0), stop=(j == NJ - 1)),
                waits=[ltok["o"], tok_c3] + (bank_free[4] if j == 0 else []), inc="t")
        ttok = g.tok("t")
        g.emit("scalar", lambda e: e.activation(
            out=mrow16[:, :], in_=pb[4][0:16, 0:TBS], func=AF.Copy, scale=1.0 / HD),
            waits=[ttok], inc="s")
        mutok = g.tok("s")
        bank_free[4] = [mutok]
        prev_sq = ("s", 0)
        sq_slot = [("t", 0), ("t", 0)]
        for j in range(NJ):
            bcb = 5 + (j % 2)
            g.emit("tensor", lambda e, j=j, bcb=bcb: e.matmul(
                pb[bcb][:, 0:TBS], bcis[:, 128 * j: 128 * (j + 1)], mrow16[:, :],
                start=True, stop=True),
                waits=[mutok] + bank_free[bcb], inc="t")
            bct = g.tok("t")
            g.emit("vector", lambda e, j=j, bcb=bcb: e.tensor_sub(
                odar[:, TBS * j: TBS * (j + 1)], oL[:, TBS * j: TBS * (j + 1)],
                pb[bcb][:, 0:TBS]),
                waits=[bct], inc="v")
            vt = g.tok("v")
            bank_free[bcb] = [vt]
            g.emit("scalar", lambda e, j=j: e.activation(
                out=sqr[j % 2][:, :], in_=odar[:, TBS * j: TBS * (j + 1)], func=AF.Square),
                waits=[vt, sq_slot[j % 2]], inc="s")
            st = g.tok("s")
            g.emit("tensor", lambda e, j=j: e.matmul(
                pb[4][0:16, 0:TBS], hiTs[:, H * j: H * (j + 1)], sqr[j % 2][:, :],
                start=(j == 0), stop=(j == NJ - 1)),
                waits=[st] + (bank_free[4] if j == 0 else []), inc="t")
            sq_slot[j % 2] = g.tok("t")
        ttok = g.tok("t")
        g.emit("scalar", lambda e: e.activation(
            out=rowf[:, :], in_=pb[4][0:16, 0:TBS], func=AF.Sqrt,
            scale=1.0 / HD, bias=cst[0:16, 1:2]),
            waits=[ttok, cst_tok], inc="s")
        st = g.tok("s")
        bank_free[4] = [st]
        g.emit("vector", lambda e: e.reciprocal(rowf[:, :], rowf[:, :]),
               waits=[st], inc="v")
        vt = g.tok("v")
        g.emit("scalar", lambda e: e.copy(out=mrow16[:, :], in_=rowf[:, :]),
               waits=[vt], inc="s")
        rstok = g.tok("s")
        g.emit("scalar", lambda e: e.copy(out=srbf[:, :], in_=srow[:, :]),
               waits=[ltok["s"]], inc="s")
        srtok = g.tok("s")
        for j in range(NJ):
            bcb = 5 + (j % 2)
            g.emit("tensor", lambda e, j=j, bcb=bcb: e.matmul(
                pb[bcb][:, 0:TBS], bcis[:, 128 * j: 128 * (j + 1)], mrow16[:, :],
                start=True, stop=True),
                waits=[rstok] + bank_free[bcb], inc="t")
            bct = g.tok("t")
            g.emit("vector", lambda e, j=j, bcb=bcb: e.tensor_mul(
                ring1[:, :], odar[:, TBS * j: TBS * (j + 1)], pb[bcb][:, 0:TBS]),
                waits=[bct, g.tok("v")], inc="v")
            g.emit("vector", lambda e, j=j: e.tensor_scalar(
                out=ring1[:, :], in0=ring1[:, :], scalar1=colAP("gn_w", j),
                scalar2=colAP("gn_b", j), op0=ALU.mult, op1=ALU.add),
                waits=[g.tok("v")], inc="v")
            bank_free[bcb] = [g.tok("v")]
            g.emit("tensor", lambda e, j=j, bcb=bcb: e.matmul(
                pb[bcb][:, 0:TBS], bcis[:, 128 * j: 128 * (j + 1)], srbf[:, :],
                start=True, stop=True),
                waits=[srtok] + bank_free[bcb], inc="t")
            bct = g.tok("t")
            g.emit("vector", lambda e, j=j, bcb=bcb: e.tensor_mul(
                sqr[j % 2][:, :], vfm[:, TBS * j: TBS * (j + 1)], pb[bcb][:, 0:TBS]),
                waits=[bct, ltok["v"], sq_slot[j % 2]], inc="v")
            g.emit("vector", lambda e, j=j: e.tensor_add(
                ring1[:, :], ring1[:, :], sqr[j % 2][:, :]),
                waits=[g.tok("v")], inc="v")
            bank_free[bcb] = [g.tok("v")]
            sq_slot[j % 2] = ("t", 0)
            g.emit("vector", lambda e, j=j: e.tensor_mul(
                aoutT[:, TBS * j: TBS * (j + 1)], ring1[:, :], gfm[:, TBS * j: TBS * (j + 1)]),
                waits=[g.tok("v"), ltok["g"]], inc="v")
        aout_tok = g.tok("v")
        consumed["o"] = [aout_tok]
        consumed["v"] = [aout_tok]
        consumed["g"] = [aout_tok]
        consumed["s"] = [srtok, aout_tok]

        for m in range(NJ):
            bank = m % 2
            for k in range(NJ):
                g.emit("tensor", lambda e, m=m, k=k, bank=bank: e.matmul(
                    pb[bank][:, 0:TBS], Wosb[:, D * k + 128 * m: D * k + 128 * (m + 1)],
                    aoutT[:, TBS * k: TBS * (k + 1)],
                    start=(k == 0), stop=(k == NJ - 1)),
                    waits=([aout_tok] + bank_free[bank] + wo_ready) if k == 0 else [],
                    inc="t" if k == NJ - 1 else None)
            mmtok = g.tok("t")
            g.emit("vector", lambda e, m=m, bank=bank: e.tensor_copy(
                out=atnT[:, TBS * m: TBS * (m + 1)], in_=pb[bank][:, 0:TBS]),
                waits=[mmtok], inc="v")
            bank_free[bank] = [g.tok("v")]
        atn_tok = g.tok("v")

        prev_sq = ("s", 0)
        for j in range(NJ):
            g.emit("tensor", lambda e, j=j: e.matmul(
                pb[7][0:1, 0:TBS], onesc[:, :], atnT[:, TBS * j: TBS * (j + 1)],
                start=(j == 0), stop=(j == NJ - 1)),
                waits=[atn_tok] + (bank_free[7] if j == 0 else []), inc="t")
        ttok = g.tok("t")
        g.emit("scalar", lambda e: e.activation(
            out=murow[:, :], in_=pb[7][0:1, 0:TBS], func=AF.Copy, scale=1.0 / D),
            waits=[ttok], inc="s")
        st = g.tok("s")
        bank_free[7] = [st]
        g.emit("tensor", lambda e: e.matmul(
            pb[6][:, 0:TBS], onesb[:, :], murow[:, :], start=True, stop=True),
            waits=[st] + bank_free[6], inc="t")
        mubc = g.tok("t")
        sq_slot = [("t", 0), ("t", 0)]
        for j in range(NJ):
            g.emit("vector", lambda e, j=j: e.tensor_sub(
                xc2[:, TBS * j: TBS * (j + 1)], atnT[:, TBS * j: TBS * (j + 1)],
                pb[6][:, 0:TBS]),
                waits=[mubc], inc="v")
            vt = g.tok("v")
            g.emit("scalar", lambda e, j=j: e.activation(
                out=sqr[j % 2][:, :], in_=xc2[:, TBS * j: TBS * (j + 1)], func=AF.Square),
                waits=[vt, sq_slot[j % 2]], inc="s")
            st = g.tok("s")
            g.emit("tensor", lambda e, j=j: e.matmul(
                pb[7][0:1, 0:TBS], onesc[:, :], sqr[j % 2][:, :],
                start=(j == 0), stop=(j == NJ - 1)),
                waits=[st], inc="t")
            sq_slot[j % 2] = g.tok("t")
        bank_free[6] = [g.tok("v")]
        ttok = g.tok("t")
        g.emit("scalar", lambda e: e.activation(
            out=rowf[0:1, :], in_=pb[7][0:1, 0:TBS], func=AF.Sqrt,
            scale=1.0 / D, bias=cst[0:1, 0:1]),
            waits=[ttok, cst_tok], inc="s")
        st = g.tok("s")
        bank_free[7] = [st]
        g.emit("vector", lambda e: e.reciprocal(rowf[0:1, :], rowf[0:1, :]),
               waits=[st], inc="v")
        vt = g.tok("v")
        g.emit("scalar", lambda e: e.copy(out=murow[:, :], in_=rowf[0:1, :]),
               waits=[vt], inc="s")
        st = g.tok("s")
        g.emit("tensor", lambda e: e.matmul(
            pb[6][:, 0:TBS], onesb[:, :], murow[:, :], start=True, stop=True),
            waits=[st] + bank_free[6], inc="t")
        rsbc = g.tok("t")
        for j in range(NJ):
            g.emit("vector", lambda e, j=j: e.tensor_mul(
                ring2[:, :], xc2[:, TBS * j: TBS * (j + 1)], pb[6][:, 0:TBS]),
                waits=[rsbc, g.tok("v")] + (x2_busy if j == 0 else []), inc="v")
            g.emit("vector", lambda e, j=j: e.tensor_add(
                xf32[:, TBS * j: TBS * (j + 1)], ring2[:, :], xf32[:, TBS * j: TBS * (j + 1)]),
                waits=[g.tok("v"), ltok["x"]], inc="v")
        bank_free[6] = [g.tok("v")]
        x2tok = g.tok("v")
        g.emit("gpsimd", lambda e, tb=tb: e.dma_start(
            out=dr["x2_sp"][:, TBS * tb: TBS * (tb + 1)].rearrange("(j p) t -> p j t", p=128),
            in_=xf32[:, :].rearrange("p (j t) -> p j t", t=TBS)),
            waits=[x2tok], inc=dx2, n=16)
        x2_busy = [g.tok(dx2)]
        consumed["x"] = [x2_busy[0]]


def emit_phaseF(g, es, upto=99):
    nc, dr, pb = g.nc, g.dr, g.pb
    sb = lambda name, shape, dt=BF16: es.enter_context(nc.sbuf_tensor(name, list(shape), dt))

    x2h = sb("f_x2", (128, NJ * TB2), F32)
    xln = sb("f_xln", (128, NJ * TB2))
    Aar = sb("f_A", (128, 32 * TB2))
    Bar = sb("f_B", (128, 32 * TB2))
    yar = Bar[:, NJ * TB2: 2 * NJ * TB2]
    wf32 = [sb("f_wf0", (128, 2048), F32)]
    wbf = [sb(f"f_wb{i}", (128, 2048)) for i in range(4)]
    fcol = sb("f_fcol", (128, 136), F32)
    onesc = sb("f_ones", (128, 1))
    onesb = sb("f_onesb", (1, 128))
    cst = sb("f_cst", (128, 8), F32)
    murow = sb("f_mur", (1, TBS))
    rowf = sb("f_rowf", (1, TBS), F32)
    sqrr = [sb(f"f_sqr{i}", (128, TBS)) for i in range(2)]
    ring1 = sb("f_r1", (128, TBS))
    ring2 = sb("f_r2", (128, TBS))

    dc4 = g.dsem("dc4")
    for (dstap, srcap) in [(fcol[:, :], dr["ffncol"][:, :]), (onesc[:, :], dr["ones_col"][:, :]),
                           (onesb[:, :], dr["onesb_row"][:, :])]:
        g.emit("sync", lambda e, dstap=dstap, srcap=srcap: e.dma_start(out=dstap, in_=srcap),
               inc=dc4, n=16)
    tok_c4 = g.tok(dc4)
    g.emit("gpsimd", lambda e: e.memset(cst[:, 0:1], LN_EPS), inc="g")
    cst_tok = g.tok("g")

    dwf = [g.dsem("dwf0")]
    dwb = [g.dsem(f"dwb{i}") for i in range(4)]
    dwsp = g.dsem("dwsp")
    wf_free = [[("v", 0)]]
    wb_free = [[("t", 0)] for _ in range(4)]
    wb_ready = [[("t", 0)] for _ in range(4)]
    dxl = g.dsem("dxl")
    dout = g.dsem("dxo")
    x2_busy = [("v", 0)]
    bank_free = {i: [("v", 0)] for i in range(8)}
    ci = [0]
    fslot = [0]
    spill_stores = [("g", 0)]

    def wload(f32name, bfname, m, k0, ntiles, th):
        slot = fslot[0] % 4
        fslot[0] += 1
        cols = 128 * ntiles
        if th == 0:
            fs = 0
            ci[0] += 1
            g.emit("sync", lambda e, fs=fs: e.dma_start(
                out=wf32[fs][:, 0:cols].rearrange("p (k n) -> p k n", n=128),
                in_=dr[f32name][m, k0:k0 + ntiles, :, :].rearrange("k a b -> a k b")),
                waits=wf_free[fs], inc=dwf[fs], n=16)
            qt = g.tok(dwf[fs])
            eng = "vector"
            if eng == "scalar":
                g.emit(eng, lambda e, fs=fs, slot=slot: e.copy(
                    out=wbf[slot][:, 0:cols], in_=wf32[fs][:, 0:cols]),
                    waits=[qt] + wb_free[slot], inc="s")
            else:
                g.emit(eng, lambda e, fs=fs, slot=slot: e.tensor_copy(
                    out=wbf[slot][:, 0:cols], in_=wf32[fs][:, 0:cols]),
                    waits=[qt] + wb_free[slot], inc=ENG_SEM[eng])
            ctok = g.tok(ENG_SEM[eng])
            wf_free[fs] = [ctok]
            wb_ready[slot] = [ctok]
            g.emit("gpsimd", lambda e, slot=slot: e.dma_start(
                out=dr[bfname][m, k0:k0 + ntiles, :, :].rearrange("k a b -> a k b"),
                in_=wbf[slot][:, 0:cols].rearrange("p (k n) -> p k n", n=128)),
                waits=[ctok], inc=dwsp, n=16)
            spill_stores[0] = ("g", 0)
        else:
            g.emit("sync", lambda e, slot=slot: e.dma_start(
                out=wbf[slot][:, 0:cols].rearrange("p (k n) -> p k n", n=128),
                in_=dr[bfname][m, k0:k0 + ntiles, :, :].rearrange("k a b -> a k b")),
                waits=wb_free[slot], inc=dwb[slot], n=16)
            wb_ready[slot] = [g.tok(dwb[slot])]
        return slot, wb_ready[slot][0]

    def ln_apply(src_ar, width, dst_bf=None, resid_f32=None, dst_f32=None, deps=()):
        cast_first = src_ar.dtype == F32
        tgt0 = dst_bf if dst_bf is not None else xln
        for sub in range(width // TBS):
            for j in range(NJ):
                if cast_first:
                    g.emit("scalar", lambda e, j=j, sub=sub: e.copy(
                        out=tgt0[:, width * j + TBS * sub: width * j + TBS * (sub + 1)],
                        in_=src_ar[:, width * j + TBS * sub: width * j + TBS * (sub + 1)]),
                        waits=list(deps), inc="s")
                    st0 = g.tok("s")
                    g.emit("tensor", lambda e, j=j, sub=sub: e.matmul(
                        pb[7][0:1, 0:TBS], onesc[:, :],
                        tgt0[:, width * j + TBS * sub: width * j + TBS * (sub + 1)],
                        start=(j == 0), stop=(j == NJ - 1)),
                        waits=[st0] + list(deps) + (bank_free[7] if j == 0 else []), inc="t")
                else:
                    g.emit("tensor", lambda e, j=j, sub=sub: e.matmul(
                        pb[7][0:1, 0:TBS], onesc[:, :],
                        src_ar[:, width * j + TBS * sub: width * j + TBS * (sub + 1)],
                        start=(j == 0), stop=(j == NJ - 1)),
                        waits=list(deps) + (bank_free[7] if j == 0 else []), inc="t")
            ttok = g.tok("t")
            g.emit("scalar", lambda e: e.activation(
                out=murow[:, :], in_=pb[7][0:1, 0:TBS], func=AF.Copy, scale=1.0 / D),
                waits=[ttok], inc="s")
            st = g.tok("s")
            bank_free[7] = [st]
            g.emit("tensor", lambda e: e.matmul(
                pb[6][:, 0:TBS], onesb[:, :], murow[:, :], start=True, stop=True),
                waits=[st] + bank_free[6], inc="t")
            mubc = g.tok("t")
            sq_slot = [("t", 0), ("t", 0)]
            for j in range(NJ):
                g.emit("vector", lambda e, j=j, sub=sub: e.tensor_sub(
                    ring1[:, :] if j % 2 == 0 else ring2[:, :],
                    src_ar[:, width * j + TBS * sub: width * j + TBS * (sub + 1)],
                    pb[6][:, 0:TBS]),
                    waits=[mubc, g.tok("s")], inc="v")
                vt = g.tok("v")
                g.emit("vector", lambda e, j=j, sub=sub: e.tensor_copy(
                    out=(dst_bf if dst_bf is not None else xln)[:, width * j + TBS * sub: width * j + TBS * (sub + 1)],
                    in_=ring1[:, :] if j % 2 == 0 else ring2[:, :]),
                    waits=[vt], inc="v")
                vt2 = g.tok("v")
                g.emit("scalar", lambda e, j=j: e.activation(
                    out=sqrr[j % 2][:, :],
                    in_=ring1[:, :] if j % 2 == 0 else ring2[:, :], func=AF.Square),
                    waits=[vt2, sq_slot[j % 2]], inc="s")
                st = g.tok("s")
                g.emit("tensor", lambda e, j=j: e.matmul(
                    pb[7][0:1, 0:TBS], onesc[:, :], sqrr[j % 2][:, :],
                    start=(j == 0), stop=(j == NJ - 1)),
                    waits=[st], inc="t")
                sq_slot[j % 2] = g.tok("t")
            bank_free[6] = [g.tok("v")]
            ttok = g.tok("t")
            g.emit("scalar", lambda e: e.activation(
                out=rowf[:, :], in_=pb[7][0:1, 0:TBS], func=AF.Sqrt,
                scale=1.0 / D, bias=cst[0:1, 0:1]),
                waits=[ttok, cst_tok], inc="s")
            st = g.tok("s")
            bank_free[7] = [st]
            g.emit("vector", lambda e: e.reciprocal(rowf[:, :], rowf[:, :]),
                   waits=[st], inc="v")
            vt = g.tok("v")
            g.emit("scalar", lambda e: e.copy(out=murow[:, :], in_=rowf[:, :]),
                   waits=[vt], inc="s")
            st = g.tok("s")
            g.emit("tensor", lambda e: e.matmul(
                pb[6][:, 0:TBS], onesb[:, :], murow[:, :], start=True, stop=True),
                waits=[st] + bank_free[6], inc="t")
            rsbc = g.tok("t")
            tgt = dst_bf if dst_bf is not None else xln
            for j in range(NJ):
                g.emit("vector", lambda e, j=j, sub=sub, tgt=tgt: e.tensor_mul(
                    tgt[:, width * j + TBS * sub: width * j + TBS * (sub + 1)],
                    tgt[:, width * j + TBS * sub: width * j + TBS * (sub + 1)],
                    pb[6][:, 0:TBS]),
                    waits=[rsbc, g.tok("v")], inc="v")
            bank_free[6] = [g.tok("v")]
        return g.tok("v")

    for th in range(2):
        if th == 1:
            sp_tok = g.tok(dwsp)
            for i in range(4):
                wb_free[i] = wb_free[i] + [sp_tok]
        g.emit("sync", lambda e, th=th: e.dma_start(
            out=x2h[:, :].rearrange("p (j t) -> p j t", t=TB2),
            in_=dr["x2_sp"][:, TB2 * th: TB2 * (th + 1)].rearrange("(j p) t -> p j t", p=128)),
            waits=x2_busy, inc=dxl, n=16)
        xtok = g.tok(dxl)
        xln_tok = ln_apply(x2h, TB2, dst_bf=xln, deps=[xtok, tok_c4])

        for m in range(32):
            s1l, w1t = wload("w1T", "w1b", m, 0, 8, th)
            s2l, w2t = wload("w2T", "w2b", m, 0, 8, th)
            mb = (m % 2) * 4
            for tt in range(2):
                for k in range(NJ):
                    g.emit("tensor", lambda e, tt=tt, k=k, s1l=s1l, mb=mb: e.matmul(
                        pb[mb + tt][:, 0:TBS], wbf[s1l][:, 128 * k: 128 * (k + 1)],
                        xln[:, TB2 * k + TBS * tt: TB2 * k + TBS * (tt + 1)],
                        start=(k == 0), stop=(k == NJ - 1)),
                        waits=([w1t, xln_tok] + bank_free[mb + tt]) if k == 0 else [],
                        inc="t" if k == NJ - 1 else None)
                for k in range(NJ):
                    g.emit("tensor", lambda e, tt=tt, k=k, s2l=s2l, mb=mb: e.matmul(
                        pb[mb + 2 + tt][:, 0:TBS], wbf[s2l][:, 128 * k: 128 * (k + 1)],
                        xln[:, TB2 * k + TBS * tt: TB2 * k + TBS * (tt + 1)],
                        start=(k == 0), stop=(k == NJ - 1)),
                        waits=([w2t] + bank_free[mb + 2 + tt]) if k == 0 else [],
                        inc="t" if k == NJ - 1 else None)
                mmtok = g.tok("t")
                rr = (2 * m + tt) % 2
                g.emit("scalar", lambda e, m=m, tt=tt, mb=mb, rr=rr: e.activation(
                    out=(ring1 if rr == 0 else ring2)[:, :], in_=pb[mb + tt][:, 0:TBS], func=AF.Silu,
                    scale=fcol[:, m:m + 1]),
                    waits=[mmtok, g.tok("s"), g.tok("v")], inc="s")
                g.emit("scalar", lambda e, m=m, tt=tt, mb=mb, rr=rr: e.activation(
                    out=(sqrr[0] if rr == 0 else sqrr[1])[:, :], in_=pb[mb + 2 + tt][:, 0:TBS], func=AF.Copy,
                    scale=fcol[:, 32 + m:32 + m + 1]),
                    inc="s")
                st = g.tok("s")
                g.emit("vector", lambda e, m=m, tt=tt, rr=rr: e.tensor_mul(
                    Aar[:, TB2 * m + TBS * tt: TB2 * m + TBS * (tt + 1)],
                    (ring1 if rr == 0 else ring2)[:, :], (sqrr[0] if rr == 0 else sqrr[1])[:, :]),
                    waits=[st], inc="v")
                bank_free[mb + tt] = [g.tok("v")]
                bank_free[mb + 2 + tt] = bank_free[mb + tt]
            wb_free[s1l] = [g.tok("t")]
            wb_free[s2l] = wb_free[s1l]
        A_tok = g.tok("v")

        def big_layer(src_ar, dst_ar, wname, bfname, scol0, func, nm_out, nk, src_tok):
            for m in range(nm_out):
                nh = max(1, nk // 16)
                for q in range(nh):
                    k0 = 16 * q
                    nt = min(16, nk - k0)
                    slot, qt = wload(wname, bfname, m, k0, nt, th)
                    for tt in range(2):
                        bank = (m % 2) * 2 + tt
                        for k in range(nt):
                            kk = k0 + k
                            g.emit("tensor", lambda e, tt=tt, slot=slot, k=k, kk=kk, bank=bank: e.matmul(
                                pb[bank][:, 0:TBS], wbf[slot][:, 128 * k: 128 * (k + 1)],
                                src_ar[:, TB2 * kk + TBS * tt: TB2 * kk + TBS * (tt + 1)],
                                start=(kk == 0), stop=(kk == nk - 1)),
                                waits=([qt, src_tok] + (bank_free[bank] if kk == 0 else [])) if k == 0 else [],
                                inc="t" if k == nt - 1 else None)
                    wb_free[slot] = [g.tok("t")]
                mmtok = g.tok("t")
                for tt in range(2):
                    bank = (m % 2) * 2 + tt
                    g.emit("scalar", lambda e, m=m, tt=tt, bank=bank, scol0=scol0, func=func: e.activation(
                        out=dst_ar[:, TB2 * m + TBS * tt: TB2 * m + TBS * (tt + 1)],
                        in_=pb[bank][:, 0:TBS], func=func,
                        scale=fcol[:, scol0 + m:scol0 + m + 1]),
                        waits=[mmtok], inc="s")
                    bank_free[bank] = [g.tok("s")]
            return g.tok("s")

        B_tok = big_layer(Aar, Bar, "h1T", "h1b", 64, AF.Gelu, 32, 32, A_tok)
        C_tok = big_layer(Bar, Aar, "h2T", "h2b", 96, AF.Gelu, 32, 32, B_tok)
        y_tok = big_layer(Aar, yar, "w3T", "w3b", 128, AF.Copy, 8, 32, C_tok)

        out_tok = ln_apply(yar, TB2, dst_bf=Bar, resid_f32=x2h, dst_f32=x2h if False else None,
                           deps=[y_tok])
        for j in range(NJ):
            g.emit("vector", lambda e, j=j: e.tensor_add(
                x2h[:, TB2 * j: TB2 * (j + 1)],
                x2h[:, TB2 * j: TB2 * (j + 1)],
                Bar[:, TB2 * j: TB2 * (j + 1)]),
                waits=[out_tok], inc="v")
        x3tok = g.tok("v")
        g.emit("gpsimd", lambda e, th=th: e.dma_start(
            out=dr["out_fm"][:, TB2 * th: TB2 * (th + 1)].rearrange("(j p) t -> p j t", p=128),
            in_=x2h[:, :].rearrange("p (j t) -> p j t", t=TB2)),
            waits=[x3tok], inc=dout, n=16)
        x2_busy = [g.tok(dout), x3tok]



_PHASES = ("P", "S", "O", "F")
_SPILLS_EXT = ()


def kernel(**inputs):
    shared, cores = host_prepare(inputs)
    nc = mkprog(spills_external=_SPILLS_EXT, phases=_PHASES)
    in_maps = []
    for c in range(8):
        m = dict(shared)
        m.update(cores[c])
        in_maps.append(m)
    res = run_bass_kernel_spmd(nc, in_maps, core_ids=list(range(8)))
    outs = [np.ascontiguousarray(np.asarray(r["out_fm"]).T) for r in res.results]
    return np.stack(outs, axis=0)


# version 14
# speedup vs baseline: 1.0841x; 1.0841x over previous
import numpy as np
import ml_dtypes
from contextlib import ExitStack

import concourse.bass as bass
import concourse.mybir as mybir
from concourse.bass_utils import run_bass_kernel_spmd

F32 = mybir.dt.float32
BF16 = mybir.dt.bfloat16
AF = mybir.ActivationFunctionType
ALU = mybir.AluOpType
BF = ml_dtypes.bfloat16

D = 1024
T = 2048
H = 16
HD = 64
NJ = 8
TBS = 512
NTB = T // TBS
CH = 64
NCH = TBS // CH
DECAY = -0.6065306597126334
LN_EPS = 1e-5
GN_EPS = HD * 1e-05
DH = 4096
TB2 = 1024

COLV = {"mix_r": 0, "mix_w": 1, "mix_k": 2, "mix_v": 3, "mix_a": 4, "mix_g": 5,
        "k_k": 6, "k_a": 7, "gn_w": 8, "gn_b": 9, "w_bias": 10, "a_bias": 11}
NCOLV = 12



def host_prepare(inp):
    f = np.float32
    shared = {}
    shared["Wr"] = np.ascontiguousarray(inp["L0_attn_Wr"], f)
    shared["Wk"] = np.ascontiguousarray(inp["L0_attn_Wk"], f)
    shared["Wv"] = np.ascontiguousarray(inp["L0_attn_Wv"], f)
    shared["Wo"] = np.ascontiguousarray(inp["L0_attn_Wo"], f)
    shared["wA"] = np.ascontiguousarray(inp["L0_attn_wA"], f)
    shared["wB"] = np.ascontiguousarray(inp["L0_attn_wB"], f)
    shared["aA"] = np.ascontiguousarray(inp["L0_attn_aA"], f)
    shared["aB"] = np.ascontiguousarray(inp["L0_attn_aB"], f)
    gA = np.asarray(inp["L0_attn_gA"], f)
    shared["gA"] = np.ascontiguousarray(np.hstack([gA, np.zeros((D, 32), f)]))
    gB = np.asarray(inp["L0_attn_gB"], f)
    shared["gB"] = np.ascontiguousarray(np.vstack([gB, np.zeros((32, D), f)]))
    cp = np.zeros((128, 8 * NCOLV), f)

    def putcol(name, vec):
        v = np.asarray(vec, f).reshape(NJ, 128)
        cp[:, 8 * COLV[name]: 8 * COLV[name] + 8] = v.T
    putcol("mix_r", inp["L0_attn_mix_r"]); putcol("mix_w", inp["L0_attn_mix_w"])
    putcol("mix_k", inp["L0_attn_mix_k"]); putcol("mix_v", inp["L0_attn_mix_v"])
    putcol("mix_a", inp["L0_attn_mix_a"]); putcol("mix_g", inp["L0_attn_mix_g"])
    putcol("k_k", inp["L0_attn_k_k"]); putcol("k_a", inp["L0_attn_k_a"])
    putcol("gn_w", inp["L0_attn_gn_weight"]); putcol("gn_b", inp["L0_attn_gn_bias"])
    putcol("w_bias", inp["L0_attn_w_bias"]); putcol("a_bias", inp["L0_attn_a_bias"])
    shared["colpack"] = cp
    for nm, key in [("kk_bc", "L0_attn_k_k"), ("ka_bc", "L0_attn_k_a"),
                    ("wb_bc", "L0_attn_w_bias"), ("ab_bc", "L0_attn_a_bias")]:
        shared[nm] = np.ascontiguousarray(
            np.broadcast_to(np.asarray(inp[key], f).reshape(1, D), (128, D)))
    rk = np.asarray(inp["L0_attn_r_k"], f)
    rkstat = np.zeros((NJ, 128, H), f)
    for j in range(NJ):
        rkstat[j, 0:64, 2 * j] = rk[2 * j]
        rkstat[j, 64:128, 2 * j + 1] = rk[2 * j + 1]
    shared["rkstat"] = rkstat
    b = BF
    shared["ones_col"] = np.ones((128, 1), b)
    shared["onesb_row"] = np.ones((1, 128), b)
    i64 = np.eye(64, dtype=np.float32)
    z64 = np.zeros((64, 64), np.float32)
    bd = lambda a, c: np.block([[a, z64], [z64, c]])
    tril_s = np.tril(np.ones((64, 64), np.float32), -1)
    triu_s = np.triu(np.ones((64, 64), np.float32), 1)
    triu_i = np.triu(np.ones((64, 64), np.float32), 0)
    shared["msl"] = bd(tril_s, tril_s).astype(b)
    shared["msu"] = bd(triu_s, triu_s).astype(b)
    shared["mtu"] = bd(triu_i, triu_i).astype(b)
    shared["I_bd"] = bd(i64, i64).astype(b)
    shared["I64x2"] = np.block([[i64, i64], [i64, i64]]).astype(b)
    shared["cumUT"] = bd(triu_i, triu_i).astype(b)
    shared["mcomb"] = np.hstack([bd(tril_s, tril_s), bd(triu_s, triu_s),
                                 bd(triu_i, triu_i), bd(triu_i, triu_i)]).astype(b)
    hiT = np.zeros((NJ, 128, H), np.float32)
    bci = np.zeros((NJ, H, 128), np.float32)
    for j in range(NJ):
        hiT[j, 0:64, 2 * j] = 1.0
        hiT[j, 64:128, 2 * j + 1] = 1.0
        bci[j, 2 * j, 0:64] = 1.0
        bci[j, 2 * j + 1, 64:128] = 1.0
    shared["hiT"] = hiT.astype(b)
    shared["bcind"] = bci.astype(b)
    indd = np.zeros((H, D), np.float32)
    for h in range(H):
        indd[h, 64 * h: 64 * h + 64] = 1.0
    shared["ind_d"] = indd.astype(b)
    cm = np.ones((128, TBS), np.float32)
    cm[:, 0::CH] = 0.0
    shared["carrym"] = cm.astype(b)

    def tileT(w, nm_, nk_):
        wT = np.asarray(w, f).T
        t4 = wT.reshape(nk_, 128, nm_, 128).transpose(2, 0, 1, 3)
        return np.ascontiguousarray(t4)
    shared["w1T"] = tileT(inp["L0_ffn_w1_w"], 32, 8)
    shared["w2T"] = tileT(inp["L0_ffn_w2_w"], 32, 8)
    shared["h1T"] = tileT(inp["L0_ffn_h1_w"], 32, 32)
    shared["h2T"] = tileT(inp["L0_ffn_h2_w"], 32, 32)
    shared["w3T"] = tileT(inp["L0_ffn_w3_w"], 8, 32)
    fc = np.zeros((128, 136), f)
    fc[:, 0:32] = np.asarray(inp["L0_ffn_w1_s"], f).reshape(32, 128).T
    fc[:, 32:64] = np.asarray(inp["L0_ffn_w2_s"], f).reshape(32, 128).T
    fc[:, 64:96] = np.asarray(inp["L0_ffn_h1_s"], f).reshape(32, 128).T
    fc[:, 96:128] = np.asarray(inp["L0_ffn_h2_s"], f).reshape(32, 128).T
    fc[:, 128:136] = np.asarray(inp["L0_ffn_w3_s"], f).reshape(8, 128).T
    shared["ffncol"] = fc
    x = np.asarray(inp["L0_attn_hidden_states"], f)
    cores = [{"x_fm": np.ascontiguousarray(x[c].T)} for c in range(8)]
    return shared, cores



ENGS = ["tensor", "vector", "scalar", "gpsimd", "sync"]
ENG_SEM = {"tensor": "t", "vector": "v", "scalar": "s", "gpsimd": "g"}


class G:
    def __init__(self, nc, block, sems, pool):
        self.nc = nc
        self.block = block
        self.sems = sems
        self.cnt = {k: 0 for k in sems}
        self._pool = pool
        self._pool_i = 0

    def dsem(self, name):
        if name not in self.sems:
            assert self._pool_i < len(self._pool), "semaphore pool exhausted"
            self.sems[name] = self._pool[self._pool_i]
            self._pool_i += 1
            self.cnt[name] = 0
        return name

    def emit(self, eng, fn, waits=(), inc=None, n=1):
        wl = [(self.sems[s], v) for (s, v) in waits if v > 0]
        g = self

        def body(e):
            seen = set()
            for sh, v in wl:
                key = (id(sh), v)
                if key not in seen:
                    e.wait_ge(sh, v)
                    seen.add(key)
            ins = fn(e)
            if inc is not None:
                assert ins is not None
                ins.then_inc(g.sems[inc], n)
        getattr(self.block, eng)(body)
        if inc is not None:
            self.cnt[inc] += n

    def tok(self, s):
        return (s, self.cnt[s])

    def barrier(self):
        targets = [(s, self.cnt[s]) for s in self.cnt]
        for eng in ENGS:
            self.emit(eng, lambda e: None, waits=targets)


def mkprog(spills_external=(), phases=("P", "S", "O", "F"), upto=99):
    nc = bass.Bass("TRN2", target_bir_lowering=False, debug=False)
    es = ExitStack()
    dr = {}

    def dram(name, shape, dtype, kind):
        dr[name] = nc.dram_tensor(name, list(shape), dtype, kind=kind).ap()

    dram("x_fm", (D, T), F32, "ExternalInput")
    for nm in ["Wr", "Wk", "Wv", "Wo"]:
        dram(nm, (D, D), F32, "ExternalInput")
    dram("wA", (D, 64), F32, "ExternalInput")
    dram("wB", (64, D), F32, "ExternalInput")
    dram("aA", (D, 64), F32, "ExternalInput")
    dram("aB", (64, D), F32, "ExternalInput")
    dram("gA", (D, 192), F32, "ExternalInput")
    dram("gB", (192, D), F32, "ExternalInput")
    dram("colpack", (128, 8 * NCOLV), F32, "ExternalInput")
    for nm in ["kk_bc", "ka_bc", "wb_bc", "ab_bc"]:
        dram(nm, (128, D), F32, "ExternalInput")
    dram("rkstat", (NJ, 128, H), F32, "ExternalInput")
    dram("ones_col", (128, 1), BF16, "ExternalInput")
    dram("onesb_row", (1, 128), BF16, "ExternalInput")
    for nm in ["msl", "msu", "mtu", "I_bd", "I64x2", "cumUT"]:
        dram(nm, (128, 128), BF16, "ExternalInput")
    dram("mcomb", (128, 512), BF16, "ExternalInput")
    dram("hiT", (NJ, 128, H), BF16, "ExternalInput")
    dram("bcind", (NJ, H, 128), BF16, "ExternalInput")
    dram("ind_d", (H, D), BF16, "ExternalInput")
    dram("carrym", (128, TBS), BF16, "ExternalInput")
    dram("w1T", (32, 8, 128, 128), F32, "ExternalInput")
    dram("w2T", (32, 8, 128, 128), F32, "ExternalInput")
    dram("h1T", (32, 32, 128, 128), F32, "ExternalInput")
    dram("h2T", (32, 32, 128, 128), F32, "ExternalInput")
    dram("w3T", (8, 32, 128, 128), F32, "ExternalInput")
    dram("ffncol", (128, 136), F32, "ExternalInput")

    def spk(nm):
        return "ExternalOutput" if nm in spills_external else "Internal"
    for nm in ["r_sp", "k_sp", "v_sp", "w_sp", "a_sp", "g_sp", "o_sp"]:
        dram(nm, (D, T), BF16, spk(nm))
    for nm in ["kT_sp", "vT_sp", "aT_sp", "wT_sp"]:
        dram(nm, (T, D), BF16, spk(nm))
    dram("scorr_sp", (H, T), F32, spk("scorr_sp"))
    dram("w1b", (32, 8, 128, 128), BF16, "Internal")
    dram("w2b", (32, 8, 128, 128), BF16, "Internal")
    dram("h1b", (32, 32, 128, 128), BF16, "Internal")
    dram("h2b", (32, 32, 128, 128), BF16, "Internal")
    dram("w3b", (8, 32, 128, 128), BF16, "Internal")
    dram("x2_sp", (D, T), F32, spk("x2_sp"))
    dram("out_fm", (D, T), F32, "ExternalOutput")

    sems = {}
    for s in ["t", "v", "s", "g"]:
        sems[s] = es.enter_context(nc.semaphore("sem_" + s))
    pool = [es.enter_context(nc.semaphore(f"dpool{i}")) for i in range(48)]
    block = es.enter_context(nc.Block())
    g = G(nc, block, sems, pool)
    g.dr = dr
    g.es = es
    g.pb = [es.enter_context(nc.psum_tensor(f"pb{i}", [128, 512], F32)) for i in range(8)]

    if "P" in phases:
        with ExitStack() as ph_es:
            emit_phaseP(g, ph_es, upto=upto)
        g.barrier()
    if "S" in phases:
        with ExitStack() as ph_es:
            emit_phaseS(g, ph_es, upto=upto)
        g.barrier()
    if "O" in phases:
        with ExitStack() as ph_es:
            emit_phaseO(g, ph_es, upto=upto)
        g.barrier()
    if "F" in phases:
        with ExitStack() as ph_es:
            emit_phaseF(g, ph_es, upto=upto)
        g.barrier()

    final = [(s, g.cnt[s]) for s in g.cnt]
    g.emit("sync", lambda e: None, waits=final)
    es.close()
    return nc



def emit_phaseP(g, es, upto=99):
    nc, dr, pb = g.nc, g.dr, g.pb
    sb = lambda name, shape, dt=BF16: es.enter_context(nc.sbuf_tensor(name, list(shape), dt))

    Wsb = {nm: sb("p_" + nm, (128, NJ * D)) for nm in ["Wr", "Wk", "Wv"]}
    wAsb = sb("p_wA", (128, NJ * 64))
    aAsb = sb("p_aA", (128, NJ * 64))
    gAsb = sb("p_gA", (128, NJ * 192))
    wBsb = sb("p_wB", (64, D))
    aBsb = sb("p_aB", (64, D))
    gBsb = sb("p_gB", (128, 2 * D))
    colpk = sb("p_col", (128, 8 * NCOLV), F32)
    onesc = sb("p_ones", (128, 1))
    onesb = sb("p_onesb", (1, 128))
    wstg = sb("p_wstg", (128, 2 * D), F32)
    xf32 = sb("p_xf32", (128, NJ * TBS), F32)
    xln = sb("p_xln", (128, NJ * T))
    delta = sb("p_delta", (128, NJ * T))
    xMr = [sb(f"p_xM{i}", (128, NJ * TBS)) for i in range(2)]
    sqring = [sb(f"p_sq{i}", (128, TBS)) for i in range(2)]
    rowbf = sb("p_rowbf", (1, TBS))
    rowf = sb("p_rowf", (1, TBS), F32)
    stageT = [sb(f"p_stT{i}", (128, D)) for i in range(2)]
    srng = [sb(f"p_srng{i}", (128, TBS)) for i in range(4)]
    tz = sb("p_tz", (64, T))
    tza = sb("p_tza", (64, T))
    tzg = sb("p_tzg", (128, T))
    tzg2 = sb("p_tzg2", (64, T))
    wbbc = sb("p_wbbc", (128, D), F32)
    abbc = sb("p_abbc", (128, D), F32)
    cst = sb("p_cst", (128, 8), F32)

    colAP = lambda name, j: colpk[:, 8 * COLV[name] + j: 8 * COLV[name] + j + 1]

    g.emit("gpsimd", lambda e: e.memset(cst[:, 0:1], LN_EPS), inc="g")
    g.emit("gpsimd", lambda e: e.memset(cst[:, 1:2], GN_EPS), inc="g")
    cst_tok = g.tok("g")
    g.cst = cst
    g.cst_tok = cst_tok
    g.colpk = colpk
    g.colAP = colAP
    g.onesc = onesc
    g.onesb = onesb

    dc = g.dsem("dconst")
    for (dst, srcn) in [(colpk, "colpack"), (onesc, "ones_col"), (onesb, "onesb_row"),
                        (wbbc, "wb_bc"), (abbc, "ab_bc")]:
        g.emit("sync", lambda e, dst=dst, srcn=srcn: e.dma_start(out=dst[:], in_=dr[srcn][:, :]),
               inc=dc, n=16)
    tok_const = g.tok(dc)

    dwl = g.dsem("dwl")
    cast_done = [("t", 0)]

    def stage_load_cast(src3, dst_ap, nparts, cols, n3=None):
        nonlocal cast_done
        g.emit("sync", lambda e: e.dma_start(
            out=wstg[0:nparts, 0:cols].rearrange("p (c n) -> p c n", n=n3) if n3 else wstg[0:nparts, 0:cols],
            in_=src3),
            waits=cast_done, inc=dwl, n=16)
        qtok = g.tok(dwl)
        toks = []
        npieces = (cols + D - 1) // D
        for p in range(npieces):
            c0, c1 = p * D, min(cols, (p + 1) * D)
            eng = ["gpsimd", "vector", "scalar"][p % 3]
            if eng == "scalar":
                g.emit(eng, lambda e, c0=c0, c1=c1: e.copy(
                    out=dst_ap[:, c0:c1], in_=wstg[0:nparts, c0:c1]),
                    waits=[qtok], inc="s")
            else:
                g.emit(eng, lambda e, c0=c0, c1=c1: e.tensor_copy(
                    out=dst_ap[:, c0:c1], in_=wstg[0:nparts, c0:c1]),
                    waits=[qtok], inc=ENG_SEM[eng])
            toks.append(g.tok(ENG_SEM[eng]))
        cast_done = list(toks)
        return toks

    for nm in ["Wr", "Wk", "Wv"]:
        for kk2 in range(0, NJ, 2):
            rin = dr[nm][128 * kk2: 128 * (kk2 + 2), :].rearrange("(k p) n -> p k n", p=128)
            stage_load_cast(rin, Wsb[nm][:, D * kk2: D * (kk2 + 2)], 128, 2 * D, n3=D)
    for (nm, dst, ccols) in [("wA", wAsb, 64), ("aA", aAsb, 64), ("gA", gAsb, 192)]:
        rin = dr[nm][:, :].rearrange("(k p) n -> p k n", p=128)
        stage_load_cast(rin, dst, 128, NJ * ccols, n3=ccols)
    for (nm, dst, nparts) in [("wB", wBsb, 64), ("aB", aBsb, 64)]:
        stage_load_cast(dr[nm][:, :], dst, nparts, D)
    stage_load_cast(dr["gB"][0:128, :], gBsb[:, 0:D], 128, D)
    stage_load_cast(dr["gB"][128:192, :], gBsb[0:64, D:2 * D], 64, D)
    cast_join = [("g", g.cnt["g"]), ("s", g.cnt["s"]), ("v", g.cnt["v"])]
    if upto <= 1:
        return

    dx = g.dsem("dx")
    pb5_free = ("v", 0)
    pb6_free = ("v", 0)
    prev_sum_t = ("t", 0)
    prev_sq_s = ("s", 0)
    prev_mul_v = ("v", 0)
    for tb in range(NTB):
        g.emit("sync", lambda e, tb=tb: e.dma_start(
            out=xf32[:, :].rearrange("p (j t) -> p j t", t=TBS),
            in_=dr["x_fm"][:, TBS * tb: TBS * (tb + 1)].rearrange("(j p) t -> p j t", p=128)),
            waits=[prev_mul_v, prev_sq_s], inc=dx, n=16)
        xtok = g.tok(dx)
        for j in range(NJ):
            g.emit("scalar", lambda e, j=j: e.copy(
                out=xMr[1][:, TBS * j: TBS * (j + 1)],
                in_=xf32[:, TBS * j: TBS * (j + 1)]),
                waits=[xtok, tok_const, prev_sum_t], inc="s")
            stok = g.tok("s")
            g.emit("tensor", lambda e, j=j: e.matmul(
                pb[7][0:1, 0:TBS], onesc[:, :], xMr[1][:, TBS * j: TBS * (j + 1)],
                start=(j == 0), stop=(j == NJ - 1)),
                waits=[stok], inc="t")
        prev_sum_t = g.tok("t")
        ttok = g.tok("t")
        g.emit("scalar", lambda e: e.activation(
            out=rowbf[:, :], in_=pb[7][0:1, 0:TBS], func=AF.Copy, scale=1.0 / D),
            waits=[ttok], inc="s")
        stok = g.tok("s")
        g.emit("tensor", lambda e: e.matmul(
            pb[6][:, 0:TBS], onesb[:, :], rowbf[:, :], start=True, stop=True),
            waits=[stok, pb6_free], inc="t")
        mubc = g.tok("t")
        for j in range(NJ):
            g.emit("vector", lambda e, j=j: e.tensor_sub(
                xMr[0][:, TBS * j: TBS * (j + 1)], xf32[:, TBS * j: TBS * (j + 1)],
                pb[6][:, 0:TBS]),
                waits=[mubc, prev_sq_s], inc="v")
        pb6_free = g.tok("v")
        xc_done = g.tok("v")
        sq_t = [("t", 0), ("t", 0)]
        for j in range(NJ):
            g.emit("scalar", lambda e, j=j: e.activation(
                out=sqring[j % 2][:, :], in_=xMr[0][:, TBS * j: TBS * (j + 1)],
                func=AF.Square),
                waits=[xc_done, sq_t[j % 2]], inc="s")
            stok = g.tok("s")
            g.emit("tensor", lambda e, j=j: e.matmul(
                pb[7][0:1, 0:TBS], onesc[:, :], sqring[j % 2][:, :],
                start=(j == 0), stop=(j == NJ - 1)),
                waits=[stok], inc="t")
            sq_t[j % 2] = g.tok("t")
        prev_sq_s = g.tok("s")
        ttok = g.tok("t")
        g.emit("scalar", lambda e: e.activation(
            out=rowf[:, :], in_=pb[7][0:1, 0:TBS], func=AF.Sqrt,
            scale=1.0 / D, bias=cst[0:1, 0:1]),
            waits=[ttok, cst_tok], inc="s")
        stok = g.tok("s")
        g.emit("vector", lambda e: e.reciprocal(out=rowf[:, :], in_=rowf[:, :]),
               waits=[stok], inc="v")
        vtok = g.tok("v")
        g.emit("scalar", lambda e: e.copy(out=rowbf[:, :], in_=rowf[:, :]),
               waits=[vtok], inc="s")
        stok = g.tok("s")
        g.emit("tensor", lambda e: e.matmul(
            pb[5][:, 0:TBS], onesb[:, :], rowbf[:, :], start=True, stop=True),
            waits=[stok, pb5_free], inc="t")
        rsbc = g.tok("t")
        for j in range(NJ):
            g.emit("vector", lambda e, j=j, tb=tb: e.tensor_mul(
                xln[:, T * j + TBS * tb: T * j + TBS * (tb + 1)],
                xMr[0][:, TBS * j: TBS * (j + 1)], pb[5][:, 0:TBS]),
                waits=[rsbc], inc="v")
        pb5_free = g.tok("v")
        prev_mul_v = g.tok("v")
    ln_done = g.tok("v")
    if upto <= 2:
        return

    for j in range(NJ):
        g.emit("vector", lambda e, j=j: e.tensor_sub(
            delta[:, T * j + 1: T * (j + 1)],
            xln[:, T * j: T * (j + 1) - 1],
            xln[:, T * j + 1: T * (j + 1)]),
            waits=[ln_done], inc="v")
        g.emit("vector", lambda e, j=j: e.tensor_scalar_mul(
            delta[:, T * j: T * j + 1], xln[:, T * j: T * j + 1], -1.0),
            inc="v")
    delta_done = g.tok("v")

    bank_free = {i: ("v", 0) for i in range(8)}
    bank_free[5] = pb5_free
    bank_free[6] = pb6_free
    bank_free[7] = ("s", g.cnt["s"])
    xm_free = [g.tok("t"), g.tok("t")]
    stT_free = [(g.dsem("dsT0"), 0), (g.dsem("dsT1"), 0)]
    srng_free = [(g.dsem(f"dsr{i}"), 0) for i in range(4)]

    def build_xM(mixname, tt, slot, extra=()):
        for j in range(NJ):
            g.emit("vector", lambda e, j=j, tt=tt, slot=slot: e.scalar_tensor_tensor(
                out=xMr[slot][:, TBS * j: TBS * (j + 1)],
                in0=delta[:, T * j + TBS * tt: T * j + TBS * (tt + 1)],
                scalar=colAP(mixname, j),
                in1=xln[:, T * j + TBS * tt: T * j + TBS * (tt + 1)],
                op0=ALU.mult, op1=ALU.add),
                waits=[delta_done, xm_free[slot]] + list(extra), inc="v")
        return g.tok("v")

    def proj_pass(mixname, Wsb_t, spname, tm_spname=None, first_extra=()):
        for tt in range(NTB):
            slot = tt % 2
            xmtok = build_xM(mixname, tt, slot, extra=first_extra if tt == 0 else ())
            for m in range(NJ):
                bank = m % 2
                for k in range(NJ):
                    g.emit("tensor", lambda e, m=m, k=k, bank=bank, slot=slot: e.matmul(
                        pb[bank][:, 0:TBS],
                        Wsb_t[:, D * k + 128 * m: D * k + 128 * (m + 1)],
                        xMr[slot][:, TBS * k: TBS * (k + 1)],
                        start=(k == 0), stop=(k == NJ - 1)),
                        waits=[bank_free[bank], xmtok] if k == 0 else [],
                        inc="t" if k == NJ - 1 else None)
                mmtok = g.tok("t")
                rs = m % 4
                g.emit("vector", lambda e, bank=bank, rs=rs: e.tensor_copy(
                    out=srng[rs][:, :], in_=pb[bank][:, 0:TBS]),
                    waits=[mmtok, srng_free[rs]], inc="v")
                bank_free[bank] = g.tok("v")
                etok = g.tok("v")
                g.emit("gpsimd", lambda e, m=m, tt=tt, rs=rs: e.dma_start(
                    out=dr[spname][128 * m:128 * (m + 1), TBS * tt: TBS * (tt + 1)],
                    in_=srng[rs][:, :]),
                    waits=[etok], inc=srng_free[rs][0], n=16)
                srng_free[rs] = g.tok(srng_free[rs][0])
            if tm_spname is not None:
                for tl in range(4):
                    tblk = tt * 4 + tl
                    sl2 = tl % 2
                    for nn in range(2):
                        bank = 2 + nn
                        for k in range(NJ):
                            g.emit("tensor", lambda e, tl=tl, nn=nn, k=k, bank=bank, slot=slot: e.matmul(
                                pb[bank][:, 0:TBS],
                                xMr[slot][:, TBS * k + 128 * tl: TBS * k + 128 * (tl + 1)],
                                Wsb_t[:, D * k + TBS * nn: D * k + TBS * (nn + 1)],
                                start=(k == 0), stop=(k == NJ - 1)),
                                waits=[bank_free[bank]] if k == 0 else [],
                                inc="t" if k == NJ - 1 else None)
                        mmtok = g.tok("t")
                        g.emit("vector", lambda e, nn=nn, bank=bank, sl2=sl2: e.tensor_copy(
                            out=stageT[sl2][:, TBS * nn: TBS * (nn + 1)],
                            in_=pb[bank][:, 0:TBS]),
                            waits=[mmtok, stT_free[sl2]], inc="v")
                        bank_free[bank] = g.tok("v")
                    etok = g.tok("v")
                    g.emit("gpsimd", lambda e, tblk=tblk, sl2=sl2: e.dma_start(
                        out=dr[tm_spname][128 * tblk:128 * (tblk + 1), :],
                        in_=stageT[sl2][:, :]),
                        waits=[etok], inc=stT_free[sl2][0], n=16)
                    stT_free[sl2] = g.tok(stT_free[sl2][0])
            xm_free[slot] = g.tok("t")

    if upto <= 3:
        return
    proj_pass("mix_r", Wsb["Wr"], "r_sp", first_extra=cast_join)
    if upto <= 4:
        return
    proj_pass("mix_k", Wsb["Wk"], "k_sp", tm_spname="kT_sp")
    proj_pass("mix_v", Wsb["Wv"], "v_sp", tm_spname="vT_sp")
    if upto <= 5:
        return

    def z1_pass(mixname, Asb, zt, func, extra=()):
        for tt in range(NTB):
            slot = tt % 2
            xmtok = build_xM(mixname, tt, slot, extra=extra if tt == 0 else ())
            for k in range(NJ):
                g.emit("tensor", lambda e, k=k, slot=slot: e.matmul(
                    pb[4][0:64, 0:TBS], Asb[:, 64 * k: 64 * (k + 1)],
                    xMr[slot][:, TBS * k: TBS * (k + 1)],
                    start=(k == 0), stop=(k == NJ - 1)),
                    waits=[bank_free[4], xmtok] if k == 0 else [],
                    inc="t" if k == NJ - 1 else None)
            mmtok = g.tok("t")
            g.emit("scalar", lambda e, tt=tt, zt=zt, func=func: e.activation(
                out=zt[:, TBS * tt: TBS * (tt + 1)], in_=pb[4][0:64, 0:TBS], func=func),
                waits=[mmtok], inc="s")
            bank_free[4] = g.tok("s")
            xm_free[slot] = g.tok("t")
        return g.tok("s")

    def z2_fm(zt, Bsb, spname, bias_name, ztok):
        for m in range(NJ):
            for tt in range(NTB):
                bank = (m * NTB + tt) % 2
                rs = (m * NTB + tt) % 4
                g.emit("tensor", lambda e, m=m, tt=tt, bank=bank: e.matmul(
                    pb[bank][:, 0:TBS], Bsb[:, 128 * m: 128 * (m + 1)],
                    zt[:, TBS * tt: TBS * (tt + 1)], start=True, stop=True),
                    waits=[bank_free[bank], ztok], inc="t")
                mmtok = g.tok("t")
                g.emit("scalar", lambda e, rs=rs, bank=bank, m=m: e.activation(
                    out=srng[rs][:, :], in_=pb[bank][:, 0:TBS],
                    func=AF.Sigmoid, bias=colAP(bias_name, m)),
                    waits=[mmtok, srng_free[rs]], inc="s")
                bank_free[bank] = g.tok("s")
                etok = g.tok("s")
                g.emit("gpsimd", lambda e, m=m, tt=tt, rs=rs: e.dma_start(
                    out=dr[spname][128 * m:128 * (m + 1), TBS * tt: TBS * (tt + 1)],
                    in_=srng[rs][:, :]),
                    waits=[etok], inc=srng_free[rs][0], n=16)
                srng_free[rs] = g.tok(srng_free[rs][0])

    def z2_tm(zt, Bsb, spname, bc_tile, ztok):
        for tblk in range(16):
            slot = tblk % 2
            for nn in range(2):
                bank = 2 + nn
                g.emit("tensor", lambda e, tblk=tblk, nn=nn, bank=bank: e.matmul(
                    pb[bank][:, 0:TBS], zt[0:64, 128 * tblk: 128 * (tblk + 1)],
                    Bsb[:, TBS * nn: TBS * (nn + 1)], start=True, stop=True),
                    waits=[bank_free[bank], ztok], inc="t")
                mmtok = g.tok("t")
                g.emit("vector", lambda e, bank=bank, nn=nn, bc_tile=bc_tile: e.tensor_add(
                    pb[bank][:, 0:TBS], pb[bank][:, 0:TBS],
                    bc_tile[:, TBS * nn: TBS * (nn + 1)]),
                    waits=[mmtok, tok_const], inc="v")
                vtok = g.tok("v")
                g.emit("scalar", lambda e, slot=slot, nn=nn, bank=bank: e.activation(
                    out=stageT[slot][:, TBS * nn: TBS * (nn + 1)], in_=pb[bank][:, 0:TBS],
                    func=AF.Sigmoid),
                    waits=[vtok, stT_free[slot]], inc="s")
                bank_free[bank] = g.tok("s")
            etok = g.tok("s")
            g.emit("gpsimd", lambda e, tblk=tblk, slot=slot: e.dma_start(
                out=dr[spname][128 * tblk:128 * (tblk + 1), :], in_=stageT[slot][:, :]),
                waits=[etok], inc=stT_free[slot][0], n=16)
            stT_free[slot] = g.tok(stT_free[slot][0])

    ztok = z1_pass("mix_w", wAsb, tz, AF.Tanh)
    z2_fm(tz, wBsb, "w_sp", "w_bias", ztok)
    z2_tm(tz, wBsb, "wT_sp", wbbc, ztok)

    ztok = z1_pass("mix_a", aAsb, tza, AF.Copy)
    z2_fm(tza, aBsb, "a_sp", "a_bias", ztok)
    z2_tm(tza, aBsb, "aT_sp", abbc, ztok)

    if upto <= 6:
        return
    for tt in range(NTB):
        slot = tt % 2
        xmtok = build_xM("mix_g", tt, slot)
        for k in range(NJ):
            g.emit("tensor", lambda e, k=k, slot=slot: e.matmul(
                pb[4][:, 0:TBS], gAsb[:, 192 * k: 192 * k + 128],
                xMr[slot][:, TBS * k: TBS * (k + 1)],
                start=(k == 0), stop=(k == NJ - 1)),
                waits=[bank_free[4], xmtok] if k == 0 else [],
                inc="t" if k == NJ - 1 else None)
        for k in range(NJ):
            g.emit("tensor", lambda e, k=k, slot=slot: e.matmul(
                pb[5][0:64, 0:TBS], gAsb[:, 192 * k + 128: 192 * (k + 1)],
                xMr[slot][:, TBS * k: TBS * (k + 1)],
                start=(k == 0), stop=(k == NJ - 1)),
                waits=[bank_free[5]] if k == 0 else [],
                inc="t" if k == NJ - 1 else None)
        mmtok = g.tok("t")
        g.emit("scalar", lambda e, tt=tt: e.activation(
            out=tzg[:, TBS * tt: TBS * (tt + 1)], in_=pb[4][:, 0:TBS], func=AF.Sigmoid),
            waits=[mmtok], inc="s")
        bank_free[4] = g.tok("s")
        g.emit("scalar", lambda e, tt=tt: e.activation(
            out=tzg2[:, TBS * tt: TBS * (tt + 1)], in_=pb[5][0:64, 0:TBS], func=AF.Sigmoid),
            waits=[mmtok], inc="s")
        bank_free[5] = g.tok("s")
        xm_free[slot] = g.tok("t")
    ztok = g.tok("s")
    for m in range(NJ):
        sslot = m % 2
        for tt in range(NTB):
            bank = (m * NTB + tt) % 2
            g.emit("tensor", lambda e, m=m, tt=tt, bank=bank: e.matmul(
                pb[bank][:, 0:TBS], gBsb[:, 128 * m: 128 * (m + 1)],
                tzg[:, TBS * tt: TBS * (tt + 1)], start=True, stop=False),
                waits=[bank_free[bank], ztok], inc="t")
            g.emit("tensor", lambda e, m=m, tt=tt, bank=bank: e.matmul(
                pb[bank][:, 0:TBS], gBsb[0:64, D + 128 * m: D + 128 * (m + 1)],
                tzg2[0:64, TBS * tt: TBS * (tt + 1)], start=False, stop=True),
                inc="t")
            mmtok = g.tok("t")
            rs = (m * NTB + tt) % 4
            g.emit("vector", lambda e, rs=rs, bank=bank: e.tensor_copy(
                out=srng[rs][:, :], in_=pb[bank][:, 0:TBS]),
                waits=[mmtok, srng_free[rs]], inc="v")
            bank_free[bank] = g.tok("v")
            etok = g.tok("v")
            g.emit("gpsimd", lambda e, m=m, tt=tt, rs=rs: e.dma_start(
                out=dr["g_sp"][128 * m:128 * (m + 1), TBS * tt: TBS * (tt + 1)],
                in_=srng[rs][:, :]),
                waits=[etok], inc=srng_free[rs][0], n=16)
            srng_free[rs] = g.tok(srng_free[rs][0])


def emit_phaseS(g, es, upto=99):
    nc, dr, pb = g.nc, g.dr, g.pb
    sb = lambda name, shape, dt=BF16: es.enter_context(nc.sbuf_tensor(name, list(shape), dt))

    rfm = sb("s_rfm", (128, NJ * TBS))
    kfm = sb("s_kfm", (128, NJ * TBS))
    wfm = sb("s_wfm", (128, NJ * TBS))
    afm = sb("s_afm", (128, NJ * TBS))
    kTt = sb("s_kT", (128, D))
    vTt = sb("s_vT", (128, 4 * D))
    aTt = sb("s_aT", (128, D))
    wTt = sb("s_wT", (128, D))
    fma = sb("s_fma", (128, NJ * 2048))
    aTar = sb("s_aTar", (128, 4 * D))
    bTar = sb("s_bTar", (128, 4 * D))
    kTar = sb("s_kTar", (128, 4 * D))
    oar = sb("s_oar", (128, NJ * T))
    g.oar = oar
    scs = sb("s_scorr", (16, T), F32)
    g.scs = scs
    cCf = sb("s_cC", (128, 64), F32)
    stj = sb("s_st", (128, NJ * 64))
    colpk = sb("s_col", (128, 8 * NCOLV), F32)
    omka = sb("s_omka", (128, 8), F32)
    masks = {nm: sb("s_" + nm, (128, 128)) for nm in ["msl", "msu", "mtu", "I_bd", "I64x2", "cumUT"]}
    hiTs = sb("s_hiT", (128, NJ * H))
    bcis = sb("s_bci", (16, NJ * 128))
    rkbf = sb("s_rkb", (128, NJ * H))
    rkf32 = sb("s_rkf", (128, NJ * H), F32)
    indd = sb("s_indd", (16, D))
    carrym = sb("s_cm", (128, TBS))
    kkbc = sb("s_kkbc", (128, D))
    kabc = sb("s_kabc", (128, D))
    omkabc = sb("s_omkabc", (128, D))
    kkr = sb("s_kkr", (128, TBS))
    sqr = [sb(f"s_sqr{i}", (128, TBS)) for i in range(2)]
    nrow = sb("s_nrow", (16, TBS), F32)
    nrbf = sb("s_nrbf", (16, TBS))
    gring = sb("s_gring", (128, TBS), F32)
    gmr = sb("s_gmr", (128, TBS), F32)
    cr = sb("s_cr", (128, TBS))
    cir = sb("s_cir", (128, TBS))
    cpr = sb("s_cpr", (128, TBS))
    t1r = sb("s_t1r", (128, TBS))
    t2r = sb("s_t2r", (128, TBS))
    kmr = sb("s_kmr", (128, TBS))
    gmT = sb("s_gmT", (128, D), F32)
    ctT = sb("s_ctT", (128, D))
    ciT = sb("s_ciT", (128, D))
    cpT = sb("s_cpT", (128, D))
    kkTr = sb("s_kkTr", (128, D))
    knTr = sb("s_knTr", (128, D))
    t2T = sb("s_t2T", (128, D))
    kmT = sb("s_kmT", (128, D))
    t3T = kmT
    scu2 = [[sb(f"s_scu{p}_{hh}", (128, 1344)) for hh in range(2)] for p in range(2)]
    mcomb = sb("s_mcomb", (128, 512))
    mcl2 = [sb(f"s_mcl{p}", (128, 128)) for p in range(2)]
    reff2 = [sb(f"s_reff{p}", (128, 128)) for p in range(2)]
    srw = nrow

    colAP = lambda name, j: colpk[:, 8 * COLV[name] + j: 8 * COLV[name] + j + 1]

    dc2 = g.dsem("dc2")
    loads = [(colpk, dr["colpack"][:, :]), (indd, dr["ind_d"][:, :]),
             (carrym, dr["carrym"][:, :]),
             (hiTs[:, :].rearrange("p (j h) -> p j h", h=H), dr["hiT"][:, :, :].rearrange("j p h -> p j h")),
             (bcis[:, :].rearrange("p (j c) -> p j c", c=128), dr["bcind"][:, :, :].rearrange("j p c -> p j c")),
             (rkf32[:, :].rearrange("p (j h) -> p j h", h=H), dr["rkstat"][:, :, :].rearrange("j p h -> p j h"))]
    for nm in masks:
        loads.append((masks[nm], dr[nm][:, :]))
    loads.append((mcomb, dr["mcomb"][:, :]))
    for (dst, srcap) in loads:
        dstap = dst if hasattr(dst, "offset") else dst[:, :]
        g.emit("sync", lambda e, dstap=dstap, srcap=srcap: e.dma_start(out=dstap, in_=srcap),
               inc=dc2, n=16)
    tok_c2 = g.tok(dc2)
    dbc = g.dsem("dbc")
    cast_t = [("v", 0)]
    for (dstb, srcn) in [(kkbc, "kk_bc"), (kabc, "ka_bc")]:
        g.emit("sync", lambda e, srcn=srcn: e.dma_start(out=gmT[:, :], in_=dr[srcn][:, :]),
               waits=cast_t, inc=dbc, n=16)
        qt_ = g.tok(dbc)
        g.emit("vector", lambda e, dstb=dstb: e.tensor_copy(out=dstb[:, :], in_=gmT[:, :]),
               waits=[qt_], inc="v")
        cast_t = [g.tok("v")]
    tok_c2 = [tok_c2, cast_t[0]]
    tok_c2 = tok_c2[0]
    bc_ready = cast_t[0]
    g.emit("vector", lambda e: e.tensor_scalar(
        out=omka[:, :], in0=colpk[:, 8 * COLV["k_a"]: 8 * COLV["k_a"] + 8],
        scalar1=-1.0, scalar2=1.0, op0=ALU.mult, op1=ALU.add),
        waits=[tok_c2], inc="v")
    g.emit("vector", lambda e: e.tensor_scalar(
        out=omkabc[:, :], in0=kabc[:, :], scalar1=-1.0, scalar2=1.0,
        op0=ALU.mult, op1=ALU.add),
        waits=[bc_ready], inc="v")
    g.emit("scalar", lambda e: e.copy(out=rkbf[:, :], in_=rkf32[:, :]),
           waits=[tok_c2], inc="s")
    g.emit("gpsimd", lambda e: e.memset(stj[:, :], 0.0), waits=[tok_c2], inc="g")
    tok_init = [("v", g.cnt["v"]), ("s", g.cnt["s"]), ("g", g.cnt["g"])]

    dls = {nm: g.dsem("dl_" + nm) for nm in ["r", "k", "w", "a", "kT", "vT", "aT", "wT"]}
    arena_consumed = {nm: [("t", 0), ("v", 0), ("s", 0)] for nm in dls}
    bank_free = {i: [("v", 0)] for i in range(8)}
    dso = g.dsem("dso")
    doo = g.dsem("doo")
    scr_busy = [[[("t", 0)], [("t", 0)]], [[("t", 0)], [("t", 0)]]]
    mclreff_busy = [[("t", 0)], [("t", 0)]]
    stj_tok = [[("v", 0)] for _ in range(NJ)]
    oar_slices_tok = []

    for tb in range(NTB):
        fmsrc = {"r": "r_sp", "k": "k_sp", "w": "w_sp", "a": "a_sp"}
        fmdst = {"r": rfm, "k": kfm, "w": wfm, "a": afm}
        ltok = {}
        for nm in fmsrc:
            g.emit("sync", lambda e, nm=nm, tb=tb: e.dma_start(
                out=fmdst[nm][:, :].rearrange("p (j t) -> p j t", t=TBS),
                in_=dr[fmsrc[nm]][:, TBS * tb: TBS * (tb + 1)].rearrange("(j p) t -> p j t", p=128)),
                waits=arena_consumed[nm], inc=dls[nm], n=16)
            ltok[nm] = g.tok(dls[nm])
        g.emit("sync", lambda e, tb=tb: e.dma_start(
            out=vTt[:, :].rearrange("p (b n) -> p b n", n=D),
            in_=dr["vT_sp"][TBS * tb: TBS * (tb + 1), :].rearrange("(b p) n -> p b n", p=128)),
            waits=arena_consumed["vT"], inc=dls["vT"], n=16)
        ltok["vT"] = g.tok(dls["vT"])

        prev_sq = ("s", 0)
        sq_slot = [("t", 0), ("t", 0)]
        for j in range(NJ):
            g.emit("vector", lambda e, j=j: e.tensor_scalar_mul(
                kkr[:, :], kfm[:, TBS * j: TBS * (j + 1)], colAP("k_k", j)),
                waits=[ltok["k"], tok_c2, prev_sq] + (bank_free[4] if j == 0 else []), inc="v")
            vt = g.tok("v")
            g.emit("scalar", lambda e, j=j: e.activation(
                out=sqr[j % 2][:, :], in_=kkr[:, :], func=AF.Square),
                waits=[vt, sq_slot[j % 2]], inc="s")
            st = g.tok("s")
            prev_sq = st
            g.emit("tensor", lambda e, j=j: e.matmul(
                pb[4][0:16, 0:TBS], hiTs[:, H * j: H * (j + 1)], sqr[j % 2][:, :],
                start=(j == 0), stop=(j == NJ - 1)),
                waits=[st], inc="t")
            sq_slot[j % 2] = g.tok("t")
        ttok = g.tok("t")
        g.emit("scalar", lambda e: e.activation(
            out=nrow[:, :], in_=pb[4][0:16, 0:TBS], func=AF.Sqrt),
            waits=[ttok, g.tok(dso)], inc="s")
        st = g.tok("s")
        bank_free[4] = [st]
        g.emit("vector", lambda e: e.tensor_scalar_max(
            nrow[:, :], nrow[:, :], 1e-12), waits=[st], inc="v")
        g.emit("vector", lambda e: e.reciprocal(nrow[:, :], nrow[:, :]),
               waits=[g.tok("v")], inc="v")
        vt = g.tok("v")
        g.emit("scalar", lambda e: e.copy(out=nrbf[:, :], in_=nrow[:, :]),
               waits=[vt], inc="s")
        ninv_tok = g.tok("s")

        karena_toks = []
        for j in range(NJ):
            bcb = 5 + (j % 2)
            g.emit("tensor", lambda e, j=j, bcb=bcb: e.matmul(
                pb[bcb][:, 0:TBS], bcis[:, 128 * j: 128 * (j + 1)], nrbf[:, :],
                start=True, stop=True),
                waits=[ninv_tok] + bank_free[bcb], inc="t")
            bct = g.tok("t")
            g.emit("vector", lambda e, j=j: e.tensor_scalar_mul(
                kkr[:, :], kfm[:, TBS * j: TBS * (j + 1)], colAP("k_k", j)),
                waits=[bct, g.tok("v")], inc="v")
            g.emit("vector", lambda e, j=j, bcb=bcb: e.tensor_mul(
                kkr[:, :], kkr[:, :], pb[bcb][:, 0:TBS]),
                waits=[g.tok("v")], inc="v")
            bank_free[bcb] = [g.tok("v")]
            g.emit("vector", lambda e, j=j: e.tensor_tensor_scan(
                out=gring[:, :], data0=carrym[:, :], data1=wfm[:, TBS * j: TBS * (j + 1)],
                initial=0.0, op0=ALU.mult, op1=ALU.add),
                waits=[ltok["w"]], inc="v")
            vt = g.tok("v")
            g.emit("scalar", lambda e, j=j: e.activation(
                out=cCf[:, 8 * j: 8 * (j + 1)], in_=gring[:, CH - 1:TBS:CH],
                func=AF.Exp, scale=DECAY),
                waits=[vt], inc="s")
            g.emit("scalar", lambda e, j=j: e.activation(
                out=cr[:, :], in_=gring[:, :], func=AF.Exp, scale=DECAY),
                waits=[vt], inc="s")
            g.emit("scalar", lambda e, j=j: e.activation(
                out=cir[:, :], in_=gring[:, :], func=AF.Exp, scale=-DECAY),
                waits=[vt], inc="s")
            g.emit("vector", lambda e, j=j: e.tensor_sub(
                gmr[:, :], gring[:, :], wfm[:, TBS * j: TBS * (j + 1)]),
                waits=[vt], inc="v")
            vt2 = g.tok("v")
            g.emit("scalar", lambda e, j=j: e.activation(
                out=cpr[:, :], in_=gmr[:, :], func=AF.Exp, scale=DECAY),
                waits=[vt2], inc="s")
            cfam = g.tok("s")
            def arr(off):
                return fma[:, 2048 * j: 2048 * (j + 1)].rearrange(
                    "p (c x) -> p c x", x=256)[:, :, off:off + 64]
            def rng(t_):
                return t_[:, :].rearrange("p (c x) -> p c x", x=64)
            g.emit("vector", lambda e, j=j: e.scalar_tensor_tensor(
                out=arr(0), in0=kkr[:, :].rearrange("p (c x) -> p c x", x=64),
                scalar=-1.0, in1=rng(cpr), op0=ALU.mult, op1=ALU.mult),
                waits=[cfam], inc="v")
            g.emit("vector", lambda e, j=j: e.tensor_mul(
                arr(64), rfm[:, TBS * j: TBS * (j + 1)].rearrange("p (c x) -> p c x", x=64),
                rng(cr)),
                waits=[ltok["r"], cfam], inc="v")
            g.emit("vector", lambda e, j=j: e.tensor_scalar(
                out=t1r[:, :], in0=afm[:, TBS * j: TBS * (j + 1)],
                scalar1=colAP("k_a", j), scalar2=omka[:, j:j + 1],
                op0=ALU.mult, op1=ALU.add),
                waits=[ltok["a"]] + tok_init, inc="v")
            vt1 = g.tok("v")
            g.emit("vector", lambda e, j=j: e.tensor_mul(
                kmr[:, :], kfm[:, TBS * j: TBS * (j + 1)], t1r[:, :]),
                waits=[vt1], inc="v")
            g.emit("vector", lambda e, j=j: e.tensor_mul(
                arr(192), rng(kmr), rng(cir)), waits=[g.tok("v"), cfam], inc="v")
            g.emit("vector", lambda e, j=j: e.tensor_mul(
                t2r[:, :], kkr[:, :], afm[:, TBS * j: TBS * (j + 1)]),
                waits=[g.tok("v")], inc="v")
            g.emit("vector", lambda e, j=j: e.tensor_mul(
                arr(128), rng(t2r), rng(cir)), waits=[g.tok("v")], inc="v")
            g.emit("vector", lambda e, j=j: e.tensor_mul(
                t2r[:, :], rfm[:, TBS * j: TBS * (j + 1)], kmr[:, :]),
                waits=[g.tok("v")], inc="v")
            vt3 = g.tok("v")
            g.emit("tensor", lambda e, j=j: e.matmul(
                pb[4][0:16, 0:TBS], rkbf[:, H * j: H * (j + 1)], t2r[:, :],
                start=(j == 0), stop=(j == NJ - 1)),
                waits=[vt3] + (bank_free[4] if j == 0 else []), inc="t")
            karena_toks.append(g.tok("v"))
        ttok = g.tok("t")
        g.emit("vector", lambda e, tb=tb: e.tensor_copy(
            out=srw[:, :], in_=pb[4][0:16, 0:TBS]), waits=[ttok, g.tok(dso)], inc="v")
        vt = g.tok("v")
        bank_free[4] = [vt]
        g.emit("vector", lambda e, tb=tb: e.tensor_copy(
            out=scs[:, TBS * tb: TBS * (tb + 1)], in_=srw[:, :]),
            waits=[g.tok("v")], inc="v")
        vt2 = g.tok("v")
        g.emit("gpsimd", lambda e, tb=tb: e.dma_start(
            out=dr["scorr_sp"][:, TBS * tb: TBS * (tb + 1)], in_=srw[:, :]),
            waits=[vt2], inc=dso, n=16)

        tm_toks = []
        for b4 in range(4):
            for (nm, dst) in [("kT", kTt), ("aT", aTt), ("wT", wTt)]:
                g.emit("sync", lambda e, nm=nm, dst=dst, tb=tb, b4=b4: e.dma_start(
                    out=dst[:, :],
                    in_=dr[nm + "_sp"][TBS * tb + 128 * b4: TBS * tb + 128 * (b4 + 1), :]),
                    waits=arena_consumed[nm], inc=dls[nm], n=16)
                ltok[nm] = g.tok(dls[nm])
            for nn in range(2):
                g.emit("tensor", lambda e, b4=b4, nn=nn: e.matmul(
                    pb[6 + nn][:, 0:TBS], masks["cumUT"][:, :],
                    wTt[:, TBS * nn: TBS * (nn + 1)],
                    start=True, stop=True),
                    waits=[ltok["wT"], tok_c2] + bank_free[6 + nn], inc="t")
            gt = g.tok("t")
            for nn in range(2):
                g.emit("scalar", lambda e, nn=nn: e.activation(
                    out=ctT[:, TBS * nn: TBS * (nn + 1)], in_=pb[6 + nn][:, 0:TBS],
                    func=AF.Exp, scale=DECAY),
                    waits=[gt], inc="s")
                g.emit("scalar", lambda e, nn=nn: e.activation(
                    out=ciT[:, TBS * nn: TBS * (nn + 1)], in_=pb[6 + nn][:, 0:TBS],
                    func=AF.Exp, scale=-DECAY),
                    inc="s")
                g.emit("vector", lambda e, b4=b4, nn=nn: e.tensor_sub(
                    gmT[:, TBS * nn: TBS * (nn + 1)], pb[6 + nn][:, 0:TBS],
                    wTt[:, TBS * nn: TBS * (nn + 1)]),
                    waits=[gt], inc="v")
                vt = g.tok("v")
                g.emit("scalar", lambda e, nn=nn: e.activation(
                    out=cpT[:, TBS * nn: TBS * (nn + 1)], in_=gmT[:, TBS * nn: TBS * (nn + 1)],
                    func=AF.Exp, scale=DECAY),
                    waits=[vt], inc="s")
            bank_free[6] = [g.tok("s"), g.tok("v")]
            bank_free[7] = bank_free[6]
            cfamT = g.tok("s")
            g.emit("vector", lambda e, b4=b4: e.tensor_mul(
                kkTr[:, :], kTt[:, :], kkbc[:, :]),
                waits=[ltok["kT"], bc_ready], inc="v")
            kkt = g.tok("v")
            for nn in range(2):
                g.emit("tensor", lambda e, b4=b4, nn=nn: e.matmul(
                    pb[4 + nn][:, 0:TBS], nrbf[:, 128 * b4: 128 * (b4 + 1)],
                    indd[:, TBS * nn: TBS * (nn + 1)], start=True, stop=True),
                    waits=[ninv_tok] + bank_free[4 + nn], inc="t")
                nbt = g.tok("t")
                g.emit("vector", lambda e, nn=nn: e.tensor_mul(
                    knTr[:, TBS * nn: TBS * (nn + 1)], kkTr[:, TBS * nn: TBS * (nn + 1)],
                    pb[4 + nn][:, 0:TBS]),
                    waits=[nbt, kkt], inc="v")
                bank_free[4 + nn] = [g.tok("v")]
            knt = g.tok("v")
            g.emit("vector", lambda e, b4=b4: e.scalar_tensor_tensor(
                out=aTar[:, D * b4: D * (b4 + 1)], in0=knTr[:, :], scalar=-1.0,
                in1=cpT[:, :], op0=ALU.mult, op1=ALU.mult),
                waits=[knt, cfamT], inc="v")
            g.emit("vector", lambda e, b4=b4: e.tensor_mul(
                t2T[:, :], aTt[:, :], kabc[:, :]),
                waits=[ltok["aT"], bc_ready], inc="v")
            g.emit("vector", lambda e: e.tensor_add(t2T[:, :], t2T[:, :], omkabc[:, :]),
                   waits=[g.tok("v")], inc="v")
            g.emit("vector", lambda e, b4=b4: e.tensor_mul(
                kmT[:, :], kTt[:, :], t2T[:, :]), waits=[g.tok("v")], inc="v")
            g.emit("vector", lambda e, b4=b4: e.tensor_mul(
                kTar[:, D * b4: D * (b4 + 1)], kmT[:, :], ciT[:, :]),
                waits=[g.tok("v"), cfamT], inc="v")
            g.emit("vector", lambda e, b4=b4: e.tensor_mul(
                t3T[:, :], knTr[:, :], aTt[:, :]), waits=[g.tok("v")], inc="v")
            g.emit("vector", lambda e, b4=b4: e.tensor_mul(
                bTar[:, D * b4: D * (b4 + 1)], t3T[:, :], ciT[:, :]),
                waits=[g.tok("v"), cfamT], inc="v")
            tm_toks.append(g.tok("v"))
            for nm in ["kT", "aT", "wT"]:
                arena_consumed[nm] = [g.tok("v"), g.tok("s")]
        prep_done = g.tok("v")
        if upto <= 11:
            arena_consumed = {nm: [prep_done, g.tok("t"), g.tok("s")] for nm in dls}
            continue

        PING = [(0, 128, 256), (768, 896, 1024)]

        def make_group(cp, j, gp):
            c0g = 2 * cp
            B = 4 * gp
            scu = scu2[gp]
            mcl = mcl2[gp]
            reff = reff2[gp]
            U = []
            for hh in range(2):
                U.append({"h": 2 * j + hh, "hp": hh * 64, "tmc": D * cp + 64 * (2 * j + hh),
                          "b0": B + 2 * hh, "b1": B + 2 * hh + 1,
                          "scu": scu[hh], "cur": 0, "ev": None, "tt": None, "hh": hh})

            def A_(hp, c):
                return fma[hp:hp + 64, 2048 * j + 256 * c: 2048 * j + 256 * c + 64]
            def R_(hp, c):
                return fma[hp:hp + 64, 2048 * j + 256 * c + 64: 2048 * j + 256 * c + 128]
            def B_(hp, c):
                return fma[hp:hp + 64, 2048 * j + 256 * c + 128: 2048 * j + 256 * c + 192]
            def K_(hp, c):
                return fma[hp:hp + 64, 2048 * j + 256 * c + 192: 2048 * j + 256 * c + 256]

            def stA(u):
                hp, b0, b1 = u["hp"], u["b0"], u["b1"]
                pre_w = [karena_toks[-1], prep_done] + bank_free[b0] + bank_free[b1] + scr_busy[gp][u["hh"]]
                for ci in range(2):
                    c = c0g + ci
                    po = 64 * ci
                    for (rgn, lhs_f, rhs_f, bnk, ofs) in [
                            (0, A_, B_, b0, 0), (1, K_, A_, b0, 128),
                            (2, B_, R_, b0, 256), (3, K_, R_, b0, 384),
                            (4, B_, A_, b1, 384)]:
                        g.emit("tensor", lambda e, c=c, po=po, hp=hp, bnk=bnk, ofs=ofs, lhs_f=lhs_f, rhs_f=rhs_f: e.matmul(
                            pb[bnk][po:po + 64, ofs + po:ofs + po + 64], lhs_f(hp, c), rhs_f(hp, c),
                            start=True, stop=True, tile_position=(hp, po)),
                            waits=pre_w if (ci == 0 and rgn == 0) else [], inc="t")
                u["tt"] = g.tok("t")

            def stB(u):
                s, b0, b1 = u["scu"], u["b0"], u["b1"]
                g.emit("vector", lambda e, s=s, b0=b0: e.tensor_mul(
                    s[:, 256:768], pb[b0][:, 0:512], mcomb[:, :]),
                    waits=[u["tt"]], inc="v")
                g.emit("vector", lambda e, s=s, b1=b1: e.tensor_mul(
                    s[:, 128:256], pb[b1][:, 384:512], masks["msu"][:, :]),
                    inc="v")
                g.emit("vector", lambda e, s=s: e.tensor_add(
                    s[:, 768:896], s[:, 128:256], masks["I_bd"][:, :]),
                    waits=[g.tok("v")], inc="v")
                u["ev"] = g.tok("v")
                bank_free[b0] = [u["ev"]]

            def mkC(it):
                def stC(u):
                    s, b1 = u["scu"], u["b1"]
                    cb = PING[u["cur"]]
                    if it == 1:
                        g.emit("tensor", lambda e, s=s, b1=b1, cb=cb: e.matmul(
                            pb[b1][:, 128:256], s[:, cb[2]:cb[2] + 128], s[:, cb[1]:cb[1] + 128],
                            start=True, stop=True),
                            waits=[u["ev"]], inc="t")
                    else:
                        g.emit("tensor", lambda e, s=s, b1=b1, cb=cb: e.matmul(
                            pb[b1][:, 0:256], s[:, cb[2]:cb[2] + 128], s[:, cb[0]:cb[0] + 256],
                            start=True, stop=True),
                            waits=[u["ev"]], inc="t")
                    g.emit("tensor", lambda e, s=s, b1=b1, cb=cb: e.matmul(
                        pb[b1][:, 256:384], s[:, cb[1]:cb[1] + 128], s[:, cb[2]:cb[2] + 128],
                        start=True, stop=True), inc="t")
                    u["tt"] = g.tok("t")
                return stC

            def mkD(it):
                def stD(u):
                    s, b1 = u["scu"], u["b1"]
                    cb = PING[u["cur"]]
                    ob = PING[1 - u["cur"]]
                    g.emit("vector", lambda e, s=s, b1=b1, ob=ob: e.tensor_copy(
                        out=s[:, ob[1]:ob[1] + 256], in_=pb[b1][:, 128:384]),
                        waits=[u["tt"]], inc="v")
                    if it > 1:
                        g.emit("vector", lambda e, s=s, b1=b1, cb=cb, ob=ob: e.tensor_add(
                            s[:, ob[0]:ob[0] + 128], pb[b1][:, 0:128], s[:, cb[0]:cb[0] + 128]),
                            inc="v")
                    u["ev"] = g.tok("v")
                    u["cur"] = 1 - u["cur"]
                return stD

            def stCf(u):
                s, b1 = u["scu"], u["b1"]
                cb = PING[u["cur"]]
                g.emit("tensor", lambda e, s=s, b1=b1, cb=cb: e.matmul(
                    pb[b1][:, 0:128], s[:, cb[2]:cb[2] + 128], s[:, cb[0]:cb[0] + 128],
                    start=True, stop=True),
                    waits=[u["ev"]], inc="t")
                u["tt"] = g.tok("t")

            def stDf(u):
                s, b1 = u["scu"], u["b1"]
                cb = PING[u["cur"]]
                ob = PING[1 - u["cur"]]
                g.emit("vector", lambda e, s=s, b1=b1, cb=cb, ob=ob: e.tensor_add(
                    s[:, ob[0]:ob[0] + 128], pb[b1][:, 0:128], s[:, cb[0]:cb[0] + 128]),
                    waits=[u["tt"]], inc="v")
                u["ev"] = g.tok("v")
                u["cur"] = 1 - u["cur"]

            def stE1(u):
                s, b1, tmc = u["scu"], u["b1"], u["tmc"]
                g.emit("tensor", lambda e, s=s, b1=b1, tmc=tmc: e.matmul(
                    pb[b1][:, 448:512], s[:, 384:512], vTt[:, tmc: tmc + 64],
                    start=True, stop=True),
                    waits=[u["ev"], ltok["vT"]], inc="t")
                u["tt"] = g.tok("t")

            def stF1(u):
                s, b1 = u["scu"], u["b1"]
                g.emit("vector", lambda e, s=s, b1=b1: e.tensor_copy(
                    out=s[:, 1152:1216], in_=pb[b1][:, 448:512]),
                    waits=[u["tt"]], inc="v")
                u["ev"] = g.tok("v")

            def stE2(u):
                s, b1, tmc = u["scu"], u["b1"], u["tmc"]
                tcol = PING[u["cur"]][0]
                g.emit("tensor", lambda e, s=s, b1=b1, tcol=tcol: e.matmul(
                    pb[b1][:, 128:192], s[:, tcol:tcol + 128], s[:, 1152:1216],
                    start=True, stop=True),
                    waits=[u["ev"]], inc="t")
                g.emit("tensor", lambda e, s=s, b1=b1, tcol=tcol, tmc=tmc: e.matmul(
                    pb[b1][:, 192:256], s[:, tcol:tcol + 128], aTar[:, tmc: tmc + 64],
                    start=True, stop=True), inc="t")
                u["tt"] = g.tok("t")

            def stF2(u):
                s, b1 = u["scu"], u["b1"]
                g.emit("vector", lambda e, s=s, b1=b1: e.tensor_copy(
                    out=s[:, 1216:1344], in_=pb[b1][:, 128:256]),
                    waits=[u["tt"]], inc="v")
                u["ev"] = g.tok("v")

            def stE3(u):
                s, b1, hp, tmc = u["scu"], u["b1"], u["hp"], u["tmc"]
                xw = [u["ev"]] + mclreff_busy[gp]
                for ci in range(2):
                    po = 64 * ci
                    g.emit("tensor", lambda e, s=s, b1=b1, hp=hp, po=po, tmc=tmc: e.matmul(
                        pb[b1][hp:hp + 64, 256 + po:256 + po + 64],
                        s[po:po + 64, 1280:1344], bTar[po:po + 64, tmc: tmc + 64],
                        start=True, stop=True, tile_position=(po, hp)),
                        waits=xw if ci == 0 else [], inc="t")
                    g.emit("tensor", lambda e, s=s, b1=b1, hp=hp, po=po: e.matmul(
                        pb[b1][hp:hp + 64, 384 + po:384 + po + 64],
                        s[po:po + 64, 1280:1344], s[po:po + 64, 512 + po:512 + po + 64],
                        start=True, stop=True, tile_position=(po, hp)), inc="t")
                u["tt"] = g.tok("t")

            grp = {"mr": None}

            def stF3(u):
                s, b1, hp = u["scu"], u["b1"], u["hp"]
                g.emit("vector", lambda e, s=s, b1=b1, hp=hp: e.tensor_add(
                    mcl[hp:hp + 64, 0:128], pb[b1][hp:hp + 64, 256:384],
                    masks["I64x2"][hp:hp + 64, 0:128]),
                    waits=[u["tt"]], inc="v")
                rhat2 = fma[hp:hp + 64, 2048 * j: 2048 * (j + 1)].rearrange(
                    "p (c x) -> p c x", x=256)[:, c0g:c0g + 2, 64:128]
                g.emit("vector", lambda e, s=s, b1=b1, hp=hp, rhat2=rhat2: e.tensor_add(
                    reff[hp:hp + 64, 0:128].rearrange("p (c x) -> p c x", x=64),
                    pb[b1][hp:hp + 64, 384:512].rearrange("p (c x) -> p c x", x=64),
                    rhat2),
                    inc="v")
                u["ev"] = g.tok("v")
                bank_free[b1] = [u["ev"]]
                grp["mr"] = u["ev"]

            ops = []
            stages = [stA, stB, mkC(1), mkD(1), mkC(2), mkD(2), mkC(3), mkD(3),
                      mkC(4), mkD(4), mkC(5), mkD(5), stCf, stDf,
                      stE1, stF1, stE2, stF2, stE3, stF3]
            for stg in stages:
                ops.append(lambda stg=stg: stg(U[0]))
                ops.append(lambda stg=stg: stg(U[1]))

            def oseq(ci):
                def run():
                    mr_t = grp["mr"]
                    c = c0g + ci
                    po = 64 * ci
                    ow = [mr_t] + bank_free[B] + bank_free[B + 2] + stj_tok[j]
                    for hh in range(2):
                        hp = hh * 64
                        ob = B + 2 * hh
                        s = scu[hh]
                        tmc = D * cp + 64 * (2 * j + hh)
                        g.emit("tensor", lambda e, hp=hp, ci=ci, ob=ob: e.matmul(
                            pb[ob][hp:hp + 64, 0:64], stj[hp:hp + 64, 64 * j:64 * (j + 1)],
                            reff[hp:hp + 64, 64 * ci:64 * (ci + 1)],
                            start=True, stop=False, tile_position=(hp, hp)),
                            waits=ow if hh == 0 else [], inc="t")
                        g.emit("tensor", lambda e, s=s, po=po, hp=hp, ob=ob: e.matmul(
                            pb[ob][hp:hp + 64, 0:64], s[po:po + 64, 1216:1280],
                            s[po:po + 64, 512 + po:512 + po + 64],
                            start=False, stop=False, tile_position=(po, hp)), inc="t")
                        g.emit("tensor", lambda e, s=s, po=po, hp=hp, tmc=tmc, ob=ob: e.matmul(
                            pb[ob][hp:hp + 64, 0:64], vTt[po:po + 64, tmc: tmc + 64],
                            s[po:po + 64, 640 + po:640 + po + 64],
                            start=False, stop=True, tile_position=(po, hp)), inc="t")
                    ot = g.tok("t")
                    for hh in range(2):
                        hp = hh * 64
                        ob = B + 2 * hh
                        g.emit("vector", lambda e, tb=tb, c=c, hp=hp, ob=ob: e.tensor_copy(
                            out=oar[hp:hp + 64, T * j + TBS * tb + CH * c: T * j + TBS * tb + CH * (c + 1)],
                            in_=pb[ob][hp:hp + 64, 0:64]),
                            waits=[ot], inc="v")
                    vt = g.tok("v")
                    bank_free[B] = [vt]
                    bank_free[B + 2] = [vt]
                    sw = bank_free[B + 1] + bank_free[B + 3] + [ot]
                    for hh in range(2):
                        hp = hh * 64
                        sbk = B + 1 + 2 * hh
                        s = scu[hh]
                        tmc = D * cp + 64 * (2 * j + hh)
                        g.emit("tensor", lambda e, hp=hp, ci=ci, sbk=sbk: e.matmul(
                            pb[sbk][hp:hp + 64, 0:64], mcl[hp:hp + 64, 64 * ci:64 * (ci + 1)],
                            stj[hp:hp + 64, 64 * j:64 * (j + 1)],
                            start=True, stop=False, tile_position=(hp, hp)),
                            waits=sw if hh == 0 else [], inc="t")
                        g.emit("tensor", lambda e, s=s, po=po, hp=hp, tmc=tmc, sbk=sbk: e.matmul(
                            pb[sbk][hp:hp + 64, 0:64], bTar[po:po + 64, tmc: tmc + 64],
                            s[po:po + 64, 1216:1280],
                            start=False, stop=False, tile_position=(po, hp)), inc="t")
                        g.emit("tensor", lambda e, po=po, hp=hp, tmc=tmc, sbk=sbk: e.matmul(
                            pb[sbk][hp:hp + 64, 0:64], kTar[po:po + 64, tmc: tmc + 64],
                            vTt[po:po + 64, tmc: tmc + 64],
                            start=False, stop=True, tile_position=(po, hp)), inc="t")
                    st_t = g.tok("t")
                    for hh in range(2):
                        hp = hh * 64
                        sbk = B + 1 + 2 * hh
                        g.emit("vector", lambda e, c=c, hp=hp, sbk=sbk: e.tensor_scalar_mul(
                            stj[hp:hp + 64, 64 * j:64 * (j + 1)], pb[sbk][hp:hp + 64, 0:64],
                            cCf[hp:hp + 64, 8 * j + c: 8 * j + c + 1]),
                            waits=[st_t], inc="v")
                    vt = g.tok("v")
                    bank_free[B + 1] = [vt]
                    bank_free[B + 3] = [vt]
                    stj_tok[j] = [vt]
                    if ci == 1:
                        scr_busy[gp][0] = scr_busy[gp][1] = [g.tok("t")]
                        mclreff_busy[gp] = [g.tok("t")]
                return run

            ops.append(oseq(0))
            ops.append(oseq(1))
            return ops

        gidx = 0
        prevB = []
        for cp in range(4):
            for j in range(NJ):
                ops = make_group(cp, j, gidx % 2)
                gidx += 1
                half = 22
                A_, B_ = ops[:half], ops[half:]
                na, nb = len(A_), len(prevB)
                k = 0
                for i in range(max(na, nb)):
                    if i < na:
                        A_[i]()
                    if i < nb:
                        prevB[i]()
                prevB = B_
        for op in prevB:
            op()
        arena_consumed = {nm: [g.tok("t"), g.tok("v"), g.tok("s")] for nm in dls}

    otok = [("v", g.cnt["v"]), ("s", g.cnt["s"])]
    for j in range(NJ):
        g.emit("gpsimd", lambda e, j=j: e.dma_start(
            out=dr["o_sp"][128 * j:128 * (j + 1), :], in_=oar[:, T * j: T * (j + 1)]),
            waits=otok, inc=doo, n=16)


def emit_phaseO(g, es, upto=99):
    nc, dr, pb = g.nc, g.dr, g.pb
    sb = lambda name, shape, dt=BF16: es.enter_context(nc.sbuf_tensor(name, list(shape), dt))

    Wosb = sb("o_Wo", (128, NJ * D))
    wstg = sb("o_wstg", (128, 2 * D), F32)
    colpk = sb("o_col", (128, 8 * NCOLV), F32)
    hiTs = sb("o_hiT", (128, NJ * H))
    bcis = sb("o_bci", (16, NJ * 128))
    onesc = sb("o_ones", (128, 1))
    onesb = sb("o_onesb", (1, 128))
    cst = sb("o_cst", (128, 8), F32)
    oL = sb("o_oL", (128, NJ * TBS))
    odar = sb("o_od", (128, NJ * TBS))
    vfm = sb("o_v", (128, NJ * TBS))
    gfm = sb("o_g", (128, NJ * TBS))
    xf32 = sb("o_x", (128, NJ * TBS), F32)
    aoutT = sb("o_aout", (128, NJ * TBS))
    atnT = sb("o_atn", (128, NJ * TBS))
    xc2 = sb("o_xc2", (128, NJ * TBS))
    srow = sb("o_srow", (16, TBS), F32)
    srbf = sb("o_srbf", (16, TBS))
    murow = sb("o_mur", (1, TBS))
    mrow16 = sb("o_m16", (16, TBS))
    rowf = sb("o_rowf", (16, TBS), F32)
    sqr = [sb(f"o_sqr{i}", (128, TBS)) for i in range(2)]
    ring1 = sb("o_r1", (128, TBS))
    ring2 = sb("o_r2", (128, TBS), F32)

    colAP = lambda name, j: colpk[:, 8 * COLV[name] + j: 8 * COLV[name] + j + 1]

    dc3 = g.dsem("dc3")
    loads = [(colpk[:, :], dr["colpack"][:, :]), (onesc[:, :], dr["ones_col"][:, :]),
             (onesb[:, :], dr["onesb_row"][:, :]),
             (hiTs[:, :].rearrange("p (j h) -> p j h", h=H), dr["hiT"][:, :, :].rearrange("j p h -> p j h")),
             (bcis[:, :].rearrange("p (j c) -> p j c", c=128), dr["bcind"][:, :, :].rearrange("j p c -> p j c"))]
    for (dstap, srcap) in loads:
        g.emit("sync", lambda e, dstap=dstap, srcap=srcap: e.dma_start(out=dstap, in_=srcap),
               inc=dc3, n=16)
    tok_c3 = g.tok(dc3)
    g.emit("gpsimd", lambda e: e.memset(cst[:, 0:1], LN_EPS), inc="g")
    g.emit("gpsimd", lambda e: e.memset(cst[:, 1:2], GN_EPS), inc="g")
    cst_tok = g.tok("g")

    dwo = g.dsem("dwo")
    cast_done = [("v", 0)]
    for kk2 in range(0, NJ, 2):
        g.emit("sync", lambda e, kk2=kk2: e.dma_start(
            out=wstg[:, :].rearrange("p (c n) -> p c n", n=D),
            in_=dr["Wo"][128 * kk2:128 * (kk2 + 2), :].rearrange("(k p) n -> p k n", p=128)),
            waits=cast_done, inc=dwo, n=16)
        qt = g.tok(dwo)
        for c2 in range(2):
            eng = ["gpsimd", "vector"][c2 % 2]
            g.emit(eng, lambda e, kk2=kk2, c2=c2: e.tensor_copy(
                out=Wosb[:, D * (kk2 + c2): D * (kk2 + c2 + 1)], in_=wstg[:, D * c2: D * (c2 + 1)]),
                waits=[qt], inc=ENG_SEM[eng])
        cast_done = [("g", g.cnt["g"]), ("v", g.cnt["v"])]
    wo_ready = cast_done

    dlo = {nm: g.dsem("do_" + nm) for nm in ["o", "v", "g", "x", "s"]}
    consumed = {nm: [("t", 0), ("v", 0), ("s", 0)] for nm in dlo}
    bank_free = {i: [("v", 0)] for i in range(8)}
    dx2 = g.dsem("dx2")
    x2_busy = [("v", 0)]

    for tb in range(NTB):
        ltok = {}
        srcs = {"o": ("o_sp", oL), "v": ("v_sp", vfm), "g": ("g_sp", gfm)}
        for nm, (spn, dst) in srcs.items():
            g.emit("sync", lambda e, spn=spn, dst=dst, tb=tb: e.dma_start(
                out=dst[:, :].rearrange("p (j t) -> p j t", t=TBS),
                in_=dr[spn][:, TBS * tb: TBS * (tb + 1)].rearrange("(j p) t -> p j t", p=128)),
                waits=consumed[nm], inc=dlo[nm], n=16)
            ltok[nm] = g.tok(dlo[nm])
        g.emit("sync", lambda e, tb=tb: e.dma_start(
            out=xf32[:, :].rearrange("p (j t) -> p j t", t=TBS),
            in_=dr["x_fm"][:, TBS * tb: TBS * (tb + 1)].rearrange("(j p) t -> p j t", p=128)),
            waits=consumed["x"], inc=dlo["x"], n=16)
        ltok["x"] = g.tok(dlo["x"])
        g.emit("sync", lambda e, tb=tb: e.dma_start(
            out=srow[:, :], in_=dr["scorr_sp"][:, TBS * tb: TBS * (tb + 1)]),
            waits=consumed["s"], inc=dlo["s"], n=16)
        ltok["s"] = g.tok(dlo["s"])

        for j in range(NJ):
            g.emit("tensor", lambda e, j=j: e.matmul(
                pb[4][0:16, 0:TBS], hiTs[:, H * j: H * (j + 1)], oL[:, TBS * j: TBS * (j + 1)],
                start=(j == 0), stop=(j == NJ - 1)),
                waits=[ltok["o"], tok_c3] + (bank_free[4] if j == 0 else []), inc="t")
        ttok = g.tok("t")
        g.emit("scalar", lambda e: e.activation(
            out=mrow16[:, :], in_=pb[4][0:16, 0:TBS], func=AF.Copy, scale=1.0 / HD),
            waits=[ttok], inc="s")
        mutok = g.tok("s")
        bank_free[4] = [mutok]
        prev_sq = ("s", 0)
        sq_slot = [("t", 0), ("t", 0)]
        for j in range(NJ):
            bcb = 5 + (j % 2)
            g.emit("tensor", lambda e, j=j, bcb=bcb: e.matmul(
                pb[bcb][:, 0:TBS], bcis[:, 128 * j: 128 * (j + 1)], mrow16[:, :],
                start=True, stop=True),
                waits=[mutok] + bank_free[bcb], inc="t")
            bct = g.tok("t")
            g.emit("vector", lambda e, j=j, bcb=bcb: e.tensor_sub(
                odar[:, TBS * j: TBS * (j + 1)], oL[:, TBS * j: TBS * (j + 1)],
                pb[bcb][:, 0:TBS]),
                waits=[bct], inc="v")
            vt = g.tok("v")
            bank_free[bcb] = [vt]
            g.emit("scalar", lambda e, j=j: e.activation(
                out=sqr[j % 2][:, :], in_=odar[:, TBS * j: TBS * (j + 1)], func=AF.Square),
                waits=[vt, sq_slot[j % 2]], inc="s")
            st = g.tok("s")
            g.emit("tensor", lambda e, j=j: e.matmul(
                pb[4][0:16, 0:TBS], hiTs[:, H * j: H * (j + 1)], sqr[j % 2][:, :],
                start=(j == 0), stop=(j == NJ - 1)),
                waits=[st] + (bank_free[4] if j == 0 else []), inc="t")
            sq_slot[j % 2] = g.tok("t")
        ttok = g.tok("t")
        g.emit("scalar", lambda e: e.activation(
            out=rowf[:, :], in_=pb[4][0:16, 0:TBS], func=AF.Sqrt,
            scale=1.0 / HD, bias=cst[0:16, 1:2]),
            waits=[ttok, cst_tok], inc="s")
        st = g.tok("s")
        bank_free[4] = [st]
        g.emit("vector", lambda e: e.reciprocal(rowf[:, :], rowf[:, :]),
               waits=[st], inc="v")
        vt = g.tok("v")
        g.emit("scalar", lambda e: e.copy(out=mrow16[:, :], in_=rowf[:, :]),
               waits=[vt], inc="s")
        rstok = g.tok("s")
        g.emit("scalar", lambda e: e.copy(out=srbf[:, :], in_=srow[:, :]),
               waits=[ltok["s"]], inc="s")
        srtok = g.tok("s")
        for j in range(NJ):
            bcb = 5 + (j % 2)
            g.emit("tensor", lambda e, j=j, bcb=bcb: e.matmul(
                pb[bcb][:, 0:TBS], bcis[:, 128 * j: 128 * (j + 1)], mrow16[:, :],
                start=True, stop=True),
                waits=[rstok] + bank_free[bcb], inc="t")
            bct = g.tok("t")
            g.emit("vector", lambda e, j=j, bcb=bcb: e.tensor_mul(
                ring1[:, :], odar[:, TBS * j: TBS * (j + 1)], pb[bcb][:, 0:TBS]),
                waits=[bct, g.tok("v")], inc="v")
            g.emit("vector", lambda e, j=j: e.tensor_scalar(
                out=ring1[:, :], in0=ring1[:, :], scalar1=colAP("gn_w", j),
                scalar2=colAP("gn_b", j), op0=ALU.mult, op1=ALU.add),
                waits=[g.tok("v")], inc="v")
            bank_free[bcb] = [g.tok("v")]
            g.emit("tensor", lambda e, j=j, bcb=bcb: e.matmul(
                pb[bcb][:, 0:TBS], bcis[:, 128 * j: 128 * (j + 1)], srbf[:, :],
                start=True, stop=True),
                waits=[srtok] + bank_free[bcb], inc="t")
            bct = g.tok("t")
            g.emit("vector", lambda e, j=j, bcb=bcb: e.tensor_mul(
                sqr[j % 2][:, :], vfm[:, TBS * j: TBS * (j + 1)], pb[bcb][:, 0:TBS]),
                waits=[bct, ltok["v"], sq_slot[j % 2]], inc="v")
            g.emit("vector", lambda e, j=j: e.tensor_add(
                ring1[:, :], ring1[:, :], sqr[j % 2][:, :]),
                waits=[g.tok("v")], inc="v")
            bank_free[bcb] = [g.tok("v")]
            sq_slot[j % 2] = ("t", 0)
            g.emit("vector", lambda e, j=j: e.tensor_mul(
                aoutT[:, TBS * j: TBS * (j + 1)], ring1[:, :], gfm[:, TBS * j: TBS * (j + 1)]),
                waits=[g.tok("v"), ltok["g"]], inc="v")
        aout_tok = g.tok("v")
        consumed["o"] = [aout_tok]
        consumed["v"] = [aout_tok]
        consumed["g"] = [aout_tok]
        consumed["s"] = [srtok, aout_tok]

        for m in range(NJ):
            bank = m % 2
            for k in range(NJ):
                g.emit("tensor", lambda e, m=m, k=k, bank=bank: e.matmul(
                    pb[bank][:, 0:TBS], Wosb[:, D * k + 128 * m: D * k + 128 * (m + 1)],
                    aoutT[:, TBS * k: TBS * (k + 1)],
                    start=(k == 0), stop=(k == NJ - 1)),
                    waits=([aout_tok] + bank_free[bank] + wo_ready) if k == 0 else [],
                    inc="t" if k == NJ - 1 else None)
            mmtok = g.tok("t")
            g.emit("vector", lambda e, m=m, bank=bank: e.tensor_copy(
                out=atnT[:, TBS * m: TBS * (m + 1)], in_=pb[bank][:, 0:TBS]),
                waits=[mmtok], inc="v")
            bank_free[bank] = [g.tok("v")]
        atn_tok = g.tok("v")

        prev_sq = ("s", 0)
        for j in range(NJ):
            g.emit("tensor", lambda e, j=j: e.matmul(
                pb[7][0:1, 0:TBS], onesc[:, :], atnT[:, TBS * j: TBS * (j + 1)],
                start=(j == 0), stop=(j == NJ - 1)),
                waits=[atn_tok] + (bank_free[7] if j == 0 else []), inc="t")
        ttok = g.tok("t")
        g.emit("scalar", lambda e: e.activation(
            out=murow[:, :], in_=pb[7][0:1, 0:TBS], func=AF.Copy, scale=1.0 / D),
            waits=[ttok], inc="s")
        st = g.tok("s")
        bank_free[7] = [st]
        g.emit("tensor", lambda e: e.matmul(
            pb[6][:, 0:TBS], onesb[:, :], murow[:, :], start=True, stop=True),
            waits=[st] + bank_free[6], inc="t")
        mubc = g.tok("t")
        sq_slot = [("t", 0), ("t", 0)]
        for j in range(NJ):
            g.emit("vector", lambda e, j=j: e.tensor_sub(
                xc2[:, TBS * j: TBS * (j + 1)], atnT[:, TBS * j: TBS * (j + 1)],
                pb[6][:, 0:TBS]),
                waits=[mubc], inc="v")
            vt = g.tok("v")
            g.emit("scalar", lambda e, j=j: e.activation(
                out=sqr[j % 2][:, :], in_=xc2[:, TBS * j: TBS * (j + 1)], func=AF.Square),
                waits=[vt, sq_slot[j % 2]], inc="s")
            st = g.tok("s")
            g.emit("tensor", lambda e, j=j: e.matmul(
                pb[7][0:1, 0:TBS], onesc[:, :], sqr[j % 2][:, :],
                start=(j == 0), stop=(j == NJ - 1)),
                waits=[st], inc="t")
            sq_slot[j % 2] = g.tok("t")
        bank_free[6] = [g.tok("v")]
        ttok = g.tok("t")
        g.emit("scalar", lambda e: e.activation(
            out=rowf[0:1, :], in_=pb[7][0:1, 0:TBS], func=AF.Sqrt,
            scale=1.0 / D, bias=cst[0:1, 0:1]),
            waits=[ttok, cst_tok], inc="s")
        st = g.tok("s")
        bank_free[7] = [st]
        g.emit("vector", lambda e: e.reciprocal(rowf[0:1, :], rowf[0:1, :]),
               waits=[st], inc="v")
        vt = g.tok("v")
        g.emit("scalar", lambda e: e.copy(out=murow[:, :], in_=rowf[0:1, :]),
               waits=[vt], inc="s")
        st = g.tok("s")
        g.emit("tensor", lambda e: e.matmul(
            pb[6][:, 0:TBS], onesb[:, :], murow[:, :], start=True, stop=True),
            waits=[st] + bank_free[6], inc="t")
        rsbc = g.tok("t")
        for j in range(NJ):
            g.emit("vector", lambda e, j=j: e.tensor_mul(
                ring2[:, :], xc2[:, TBS * j: TBS * (j + 1)], pb[6][:, 0:TBS]),
                waits=[rsbc, g.tok("v")] + (x2_busy if j == 0 else []), inc="v")
            g.emit("vector", lambda e, j=j: e.tensor_add(
                xf32[:, TBS * j: TBS * (j + 1)], ring2[:, :], xf32[:, TBS * j: TBS * (j + 1)]),
                waits=[g.tok("v"), ltok["x"]], inc="v")
        bank_free[6] = [g.tok("v")]
        x2tok = g.tok("v")
        g.emit("gpsimd", lambda e, tb=tb: e.dma_start(
            out=dr["x2_sp"][:, TBS * tb: TBS * (tb + 1)].rearrange("(j p) t -> p j t", p=128),
            in_=xf32[:, :].rearrange("p (j t) -> p j t", t=TBS)),
            waits=[x2tok], inc=dx2, n=16)
        x2_busy = [g.tok(dx2)]
        consumed["x"] = [x2_busy[0]]


def emit_phaseF(g, es, upto=99):
    nc, dr, pb = g.nc, g.dr, g.pb
    sb = lambda name, shape, dt=BF16: es.enter_context(nc.sbuf_tensor(name, list(shape), dt))

    x2h = sb("f_x2", (128, NJ * TB2), F32)
    xln = sb("f_xln", (128, NJ * TB2))
    Aar = sb("f_A", (128, 32 * TB2))
    Bar = sb("f_B", (128, 32 * TB2))
    yar = Bar[:, NJ * TB2: 2 * NJ * TB2]
    wf32 = [sb("f_wf0", (128, 2048), F32)]
    wbf = [sb(f"f_wb{i}", (128, 2048)) for i in range(4)]
    fcol = sb("f_fcol", (128, 136), F32)
    onesc = sb("f_ones", (128, 1))
    onesb = sb("f_onesb", (1, 128))
    cst = sb("f_cst", (128, 8), F32)
    murow = sb("f_mur", (1, TBS))
    rowf = sb("f_rowf", (1, TBS), F32)
    sqrr = [sb(f"f_sqr{i}", (128, TBS)) for i in range(2)]
    ring1 = sb("f_r1", (128, TBS))
    ring2 = sb("f_r2", (128, TBS))

    dc4 = g.dsem("dc4")
    for (dstap, srcap) in [(fcol[:, :], dr["ffncol"][:, :]), (onesc[:, :], dr["ones_col"][:, :]),
                           (onesb[:, :], dr["onesb_row"][:, :])]:
        g.emit("sync", lambda e, dstap=dstap, srcap=srcap: e.dma_start(out=dstap, in_=srcap),
               inc=dc4, n=16)
    tok_c4 = g.tok(dc4)
    g.emit("gpsimd", lambda e: e.memset(cst[:, 0:1], LN_EPS), inc="g")
    cst_tok = g.tok("g")

    dwf = [g.dsem("dwf0")]
    dwb = [g.dsem(f"dwb{i}") for i in range(4)]
    dwsp = g.dsem("dwsp")
    wf_free = [[("v", 0)]]
    wb_free = [[("t", 0)] for _ in range(4)]
    wb_ready = [[("t", 0)] for _ in range(4)]
    dxl = g.dsem("dxl")
    dout = g.dsem("dxo")
    x2_busy = [("v", 0)]
    bank_free = {i: [("v", 0)] for i in range(8)}
    ci = [0]
    fslot = [0]
    spill_stores = [("g", 0)]

    def wload(f32name, bfname, m, k0, ntiles, th):
        slot = fslot[0] % 4
        fslot[0] += 1
        cols = 128 * ntiles
        if th == 0:
            fs = 0
            ci[0] += 1
            g.emit("sync", lambda e, fs=fs: e.dma_start(
                out=wf32[fs][:, 0:cols].rearrange("p (k n) -> p k n", n=128),
                in_=dr[f32name][m, k0:k0 + ntiles, :, :].rearrange("k a b -> a k b")),
                waits=wf_free[fs], inc=dwf[fs], n=16)
            qt = g.tok(dwf[fs])
            eng = "vector"
            if eng == "scalar":
                g.emit(eng, lambda e, fs=fs, slot=slot: e.copy(
                    out=wbf[slot][:, 0:cols], in_=wf32[fs][:, 0:cols]),
                    waits=[qt] + wb_free[slot], inc="s")
            else:
                g.emit(eng, lambda e, fs=fs, slot=slot: e.tensor_copy(
                    out=wbf[slot][:, 0:cols], in_=wf32[fs][:, 0:cols]),
                    waits=[qt] + wb_free[slot], inc=ENG_SEM[eng])
            ctok = g.tok(ENG_SEM[eng])
            wf_free[fs] = [ctok]
            wb_ready[slot] = [ctok]
            g.emit("gpsimd", lambda e, slot=slot: e.dma_start(
                out=dr[bfname][m, k0:k0 + ntiles, :, :].rearrange("k a b -> a k b"),
                in_=wbf[slot][:, 0:cols].rearrange("p (k n) -> p k n", n=128)),
                waits=[ctok], inc=dwsp, n=16)
            wb_free[slot] = wb_free[slot] + [g.tok(dwsp)]
        else:
            g.emit("sync", lambda e, slot=slot: e.dma_start(
                out=wbf[slot][:, 0:cols].rearrange("p (k n) -> p k n", n=128),
                in_=dr[bfname][m, k0:k0 + ntiles, :, :].rearrange("k a b -> a k b")),
                waits=wb_free[slot], inc=dwb[slot], n=16)
            wb_ready[slot] = [g.tok(dwb[slot])]
        return slot, wb_ready[slot][0]

    def ln_apply(src_ar, width, dst_bf=None, resid_f32=None, dst_f32=None, deps=()):
        cast_first = src_ar.dtype == F32
        tgt0 = dst_bf if dst_bf is not None else xln
        for sub in range(width // TBS):
            for j in range(NJ):
                if cast_first:
                    g.emit("scalar", lambda e, j=j, sub=sub: e.copy(
                        out=tgt0[:, width * j + TBS * sub: width * j + TBS * (sub + 1)],
                        in_=src_ar[:, width * j + TBS * sub: width * j + TBS * (sub + 1)]),
                        waits=list(deps), inc="s")
                    st0 = g.tok("s")
                    g.emit("tensor", lambda e, j=j, sub=sub: e.matmul(
                        pb[7][0:1, 0:TBS], onesc[:, :],
                        tgt0[:, width * j + TBS * sub: width * j + TBS * (sub + 1)],
                        start=(j == 0), stop=(j == NJ - 1)),
                        waits=[st0] + list(deps) + (bank_free[7] if j == 0 else []), inc="t")
                else:
                    g.emit("tensor", lambda e, j=j, sub=sub: e.matmul(
                        pb[7][0:1, 0:TBS], onesc[:, :],
                        src_ar[:, width * j + TBS * sub: width * j + TBS * (sub + 1)],
                        start=(j == 0), stop=(j == NJ - 1)),
                        waits=list(deps) + (bank_free[7] if j == 0 else []), inc="t")
            ttok = g.tok("t")
            g.emit("scalar", lambda e: e.activation(
                out=murow[:, :], in_=pb[7][0:1, 0:TBS], func=AF.Copy, scale=1.0 / D),
                waits=[ttok], inc="s")
            st = g.tok("s")
            bank_free[7] = [st]
            g.emit("tensor", lambda e: e.matmul(
                pb[6][:, 0:TBS], onesb[:, :], murow[:, :], start=True, stop=True),
                waits=[st] + bank_free[6], inc="t")
            mubc = g.tok("t")
            sq_slot = [("t", 0), ("t", 0)]
            for j in range(NJ):
                g.emit("vector", lambda e, j=j, sub=sub: e.tensor_sub(
                    ring1[:, :] if j % 2 == 0 else ring2[:, :],
                    src_ar[:, width * j + TBS * sub: width * j + TBS * (sub + 1)],
                    pb[6][:, 0:TBS]),
                    waits=[mubc, g.tok("s")], inc="v")
                vt = g.tok("v")
                g.emit("vector", lambda e, j=j, sub=sub: e.tensor_copy(
                    out=(dst_bf if dst_bf is not None else xln)[:, width * j + TBS * sub: width * j + TBS * (sub + 1)],
                    in_=ring1[:, :] if j % 2 == 0 else ring2[:, :]),
                    waits=[vt], inc="v")
                vt2 = g.tok("v")
                g.emit("scalar", lambda e, j=j: e.activation(
                    out=sqrr[j % 2][:, :],
                    in_=ring1[:, :] if j % 2 == 0 else ring2[:, :], func=AF.Square),
                    waits=[vt2, sq_slot[j % 2]], inc="s")
                st = g.tok("s")
                g.emit("tensor", lambda e, j=j: e.matmul(
                    pb[7][0:1, 0:TBS], onesc[:, :], sqrr[j % 2][:, :],
                    start=(j == 0), stop=(j == NJ - 1)),
                    waits=[st], inc="t")
                sq_slot[j % 2] = g.tok("t")
            bank_free[6] = [g.tok("v")]
            ttok = g.tok("t")
            g.emit("scalar", lambda e: e.activation(
                out=rowf[:, :], in_=pb[7][0:1, 0:TBS], func=AF.Sqrt,
                scale=1.0 / D, bias=cst[0:1, 0:1]),
                waits=[ttok, cst_tok], inc="s")
            st = g.tok("s")
            bank_free[7] = [st]
            g.emit("vector", lambda e: e.reciprocal(rowf[:, :], rowf[:, :]),
                   waits=[st], inc="v")
            vt = g.tok("v")
            g.emit("scalar", lambda e: e.copy(out=murow[:, :], in_=rowf[:, :]),
                   waits=[vt], inc="s")
            st = g.tok("s")
            g.emit("tensor", lambda e: e.matmul(
                pb[6][:, 0:TBS], onesb[:, :], murow[:, :], start=True, stop=True),
                waits=[st] + bank_free[6], inc="t")
            rsbc = g.tok("t")
            tgt = dst_bf if dst_bf is not None else xln
            for j in range(NJ):
                g.emit("vector", lambda e, j=j, sub=sub, tgt=tgt: e.tensor_mul(
                    tgt[:, width * j + TBS * sub: width * j + TBS * (sub + 1)],
                    tgt[:, width * j + TBS * sub: width * j + TBS * (sub + 1)],
                    pb[6][:, 0:TBS]),
                    waits=[rsbc, g.tok("v")], inc="v")
            bank_free[6] = [g.tok("v")]
        return g.tok("v")

    for th in range(2):
        if th == 1:
            sp_tok = g.tok(dwsp)
            for i in range(4):
                wb_free[i] = wb_free[i] + [sp_tok]
        g.emit("sync", lambda e, th=th: e.dma_start(
            out=x2h[:, :].rearrange("p (j t) -> p j t", t=TB2),
            in_=dr["x2_sp"][:, TB2 * th: TB2 * (th + 1)].rearrange("(j p) t -> p j t", p=128)),
            waits=x2_busy, inc=dxl, n=16)
        xtok = g.tok(dxl)
        xln_tok = ln_apply(x2h, TB2, dst_bf=xln, deps=[xtok, tok_c4])

        for m in range(32):
            s1l, w1t = wload("w1T", "w1b", m, 0, 8, th)
            s2l, w2t = wload("w2T", "w2b", m, 0, 8, th)
            mb = (m % 2) * 4
            for tt in range(2):
                for k in range(NJ):
                    g.emit("tensor", lambda e, tt=tt, k=k, s1l=s1l, mb=mb: e.matmul(
                        pb[mb + tt][:, 0:TBS], wbf[s1l][:, 128 * k: 128 * (k + 1)],
                        xln[:, TB2 * k + TBS * tt: TB2 * k + TBS * (tt + 1)],
                        start=(k == 0), stop=(k == NJ - 1)),
                        waits=([w1t, xln_tok] + bank_free[mb + tt]) if k == 0 else [],
                        inc="t" if k == NJ - 1 else None)
                for k in range(NJ):
                    g.emit("tensor", lambda e, tt=tt, k=k, s2l=s2l, mb=mb: e.matmul(
                        pb[mb + 2 + tt][:, 0:TBS], wbf[s2l][:, 128 * k: 128 * (k + 1)],
                        xln[:, TB2 * k + TBS * tt: TB2 * k + TBS * (tt + 1)],
                        start=(k == 0), stop=(k == NJ - 1)),
                        waits=([w2t] + bank_free[mb + 2 + tt]) if k == 0 else [],
                        inc="t" if k == NJ - 1 else None)
                mmtok = g.tok("t")
                rr = (2 * m + tt) % 2
                g.emit("scalar", lambda e, m=m, tt=tt, mb=mb, rr=rr: e.activation(
                    out=(ring1 if rr == 0 else ring2)[:, :], in_=pb[mb + tt][:, 0:TBS], func=AF.Silu,
                    scale=fcol[:, m:m + 1]),
                    waits=[mmtok, g.tok("s"), g.tok("v")], inc="s")
                g.emit("scalar", lambda e, m=m, tt=tt, mb=mb, rr=rr: e.activation(
                    out=(sqrr[0] if rr == 0 else sqrr[1])[:, :], in_=pb[mb + 2 + tt][:, 0:TBS], func=AF.Copy,
                    scale=fcol[:, 32 + m:32 + m + 1]),
                    inc="s")
                st = g.tok("s")
                g.emit("vector", lambda e, m=m, tt=tt, rr=rr: e.tensor_mul(
                    Aar[:, TB2 * m + TBS * tt: TB2 * m + TBS * (tt + 1)],
                    (ring1 if rr == 0 else ring2)[:, :], (sqrr[0] if rr == 0 else sqrr[1])[:, :]),
                    waits=[st], inc="v")
                bank_free[mb + tt] = [g.tok("v")]
                bank_free[mb + 2 + tt] = bank_free[mb + tt]
            wb_free[s1l] = [g.tok("t")]
            wb_free[s2l] = wb_free[s1l]
        A_tok = g.tok("v")

        def big_layer(src_ar, dst_ar, wname, bfname, scol0, func, nm_out, nk, src_tok):
            for m in range(nm_out):
                nh = max(1, nk // 16)
                for q in range(nh):
                    k0 = 16 * q
                    nt = min(16, nk - k0)
                    slot, qt = wload(wname, bfname, m, k0, nt, th)
                    for tt in range(2):
                        bank = (m % 2) * 2 + tt
                        for k in range(nt):
                            kk = k0 + k
                            g.emit("tensor", lambda e, tt=tt, slot=slot, k=k, kk=kk, bank=bank: e.matmul(
                                pb[bank][:, 0:TBS], wbf[slot][:, 128 * k: 128 * (k + 1)],
                                src_ar[:, TB2 * kk + TBS * tt: TB2 * kk + TBS * (tt + 1)],
                                start=(kk == 0), stop=(kk == nk - 1)),
                                waits=([qt, src_tok] + (bank_free[bank] if kk == 0 else [])) if k == 0 else [],
                                inc="t" if k == nt - 1 else None)
                    wb_free[slot] = [g.tok("t")]
                mmtok = g.tok("t")
                for tt in range(2):
                    bank = (m % 2) * 2 + tt
                    g.emit("scalar", lambda e, m=m, tt=tt, bank=bank, scol0=scol0, func=func: e.activation(
                        out=dst_ar[:, TB2 * m + TBS * tt: TB2 * m + TBS * (tt + 1)],
                        in_=pb[bank][:, 0:TBS], func=func,
                        scale=fcol[:, scol0 + m:scol0 + m + 1]),
                        waits=[mmtok], inc="s")
                    bank_free[bank] = [g.tok("s")]
            return g.tok("s")

        B_tok = big_layer(Aar, Bar, "h1T", "h1b", 64, AF.Gelu, 32, 32, A_tok)
        C_tok = big_layer(Bar, Aar, "h2T", "h2b", 96, AF.Gelu, 32, 32, B_tok)
        y_tok = big_layer(Aar, yar, "w3T", "w3b", 128, AF.Copy, 8, 32, C_tok)

        out_tok = ln_apply(yar, TB2, dst_bf=Bar, resid_f32=x2h, dst_f32=x2h if False else None,
                           deps=[y_tok])
        for j in range(NJ):
            g.emit("vector", lambda e, j=j: e.tensor_add(
                x2h[:, TB2 * j: TB2 * (j + 1)],
                x2h[:, TB2 * j: TB2 * (j + 1)],
                Bar[:, TB2 * j: TB2 * (j + 1)]),
                waits=[out_tok], inc="v")
        x3tok = g.tok("v")
        g.emit("gpsimd", lambda e, th=th: e.dma_start(
            out=dr["out_fm"][:, TB2 * th: TB2 * (th + 1)].rearrange("(j p) t -> p j t", p=128),
            in_=x2h[:, :].rearrange("p (j t) -> p j t", t=TB2)),
            waits=[x3tok], inc=dout, n=16)
        x2_busy = [g.tok(dout), x3tok]



_PHASES = ("P", "S", "O", "F")
_SPILLS_EXT = ()


def kernel(**inputs):
    shared, cores = host_prepare(inputs)
    nc = mkprog(spills_external=_SPILLS_EXT, phases=_PHASES)
    in_maps = []
    for c in range(8):
        m = dict(shared)
        m.update(cores[c])
        in_maps.append(m)
    res = run_bass_kernel_spmd(nc, in_maps, core_ids=list(range(8)))
    outs = [np.ascontiguousarray(np.asarray(r["out_fm"]).T) for r in res.results]
    return np.stack(outs, axis=0)
